# version 10
# baseline (speedup 1.0000x reference)
import numpy as np
import ml_dtypes
import concourse.bass as bass
import concourse.mybir as mybir
from concourse.bass_utils import run_bass_kernel_spmd

F32 = mybir.dt.float32
BF16 = mybir.dt.bfloat16
AF = mybir.ActivationFunctionType
ALU = mybir.AluOpType
AX = mybir.AxisListType

NCORES = 8
D = 2048
NFT = 16
T_SEQ = 8192
OWN = 1024
HALO = 8
TE = OWN + 2 * HALO
W0 = 256
TW = W0 + TE
CHUNKS = [(i * 128, 128) for i in range(10)] + [(1280, 16)]
WARM_CHUNKS = 2
CH_W = [(0, 512), (512, 512), (1024, 272)]
CH_E = [(256, 512), (768, 512), (1280, 16)]
CH_OWN = [(264, 512), (776, 512)]

INTER = 5632
NQ = 4
QI = INTER // NQ
EPS = 1e-5

DBG = 0


def _pf32(a):
    return np.ascontiguousarray(a, dtype=np.float32)


def _sw(wt):
    K, O = wt.shape
    a = wt.reshape(K // 128, 128, O // 128, 128)
    return _pf32(a.transpose(2, 1, 0, 3).reshape(O // 128, 128, K * 128 // 128))


def host_prep(inp):
    out = {}
    W_in = np.asarray(inp["L0_attn_W_in"])
    WT = W_in.T
    out["wt_in_a"] = _pf32(WT[:, 8192:8224])
    cols = []
    for g in range(16):
        for base in (6144, 4096, 0):
            for h in (2 * g, 2 * g + 1):
                cols.append(np.arange(base + 64 * h, base + 64 * h + 64))
    out["wt_in_cbx"] = _sw(_pf32(WT[:, np.concatenate(cols)]))
    out["wt_in_z"] = _pf32(WT[:, 2048:4096])

    out["wt_out"] = _sw(np.asarray(inp["L0_attn_W_out"]).T)
    out["wt_gate"] = _sw(np.asarray(inp["L0_ffn_gate_w"]).T)
    out["wt_up"] = _sw(np.asarray(inp["L0_ffn_up_w"]).T)
    dwn = np.asarray(inp["L0_ffn_down_w"]).T
    out["wt_down"] = _pf32(np.stack([_sw(dwn[1408 * q:1408 * (q + 1), :])
                                     for q in range(4)]))
    out["s_gate"] = _pf32(np.asarray(inp["L0_ffn_gate_s"]).reshape(44, 128).T)
    out["s_up"] = _pf32(np.asarray(inp["L0_ffn_up_s"]).reshape(44, 128).T)
    out["s_down"] = _pf32(np.asarray(inp["L0_ffn_down_s"]).reshape(16, 128).T)
    out["gamma"] = _pf32(np.asarray(inp["L0_attn_gamma"]).reshape(1, 2048))
    out["beta"] = _pf32(np.asarray(inp["L0_attn_beta"]).reshape(1, 2048))
    out["dmat"] = _pf32(np.asarray(inp["L0_attn_D"]))

    qkv = np.asarray(inp["L1_attn_qkv_w"]).T
    cols = []
    for g in range(16):
        for base in (0, 2048, 4096):
            for h in (2 * g, 2 * g + 1):
                cols.append(np.arange(base + 64 * h, base + 64 * h + 64))
    out["wt_qkv"] = _sw(_pf32(qkv[:, np.concatenate(cols)]))
    out["wt_proj"] = _sw(np.asarray(inp["L1_attn_proj_w"]).T)
    out["proj_b"] = _pf32(np.asarray(inp["L1_attn_proj_b"]).reshape(16, 128).T)
    out["wt_router"] = _pf32(np.asarray(inp["L1_ffn_gate_weight"]).T)
    out["wt_eg"] = _pf32(np.stack([_sw(np.asarray(inp["L1_ffn_w_gate"])[e].T)
                                   for e in range(8)]))
    out["wt_eu"] = _pf32(np.stack([_sw(np.asarray(inp["L1_ffn_w_up"])[e].T)
                                   for e in range(8)]))
    out["wt_ed"] = _pf32(np.stack([_sw(np.asarray(inp["L1_ffn_w_down"])[e].T)
                                   for e in range(8)]))
    out["wt_sg"] = _sw(np.asarray(inp["L1_ffn_sw_gate"]).T)
    out["wt_su"] = _sw(np.asarray(inp["L1_ffn_sw_up"]).T)
    sdn = np.asarray(inp["L1_ffn_sw_down"]).T
    out["wt_sd"] = _pf32(np.stack([_sw(sdn[1408 * h:1408 * (h + 1), :])
                                   for h in range(2)]))

    out["ident_f"] = _pf32(np.eye(128))
    out["ident_b"] = np.eye(128).astype(ml_dtypes.bfloat16)
    jj, ii = np.meshgrid(np.arange(128), np.arange(128), indexing="ij")
    out["neglt"] = _pf32(-(jj <= ii).astype(np.float32))
    out["lt_b"] = (jj <= ii).astype(ml_dtypes.bfloat16)
    sel = np.zeros((128, 256), np.float32)
    sel[127, 0:128] = 1.0
    sel[15, 128:256] = 1.0
    out["sel_last"] = sel
    oh32 = np.zeros((32, 4096), np.float32)
    for h in range(32):
        oh32[h, 128 * h:128 * h + 128] = 1.0
    out["oh32"] = oh32
    oh8 = np.zeros((8, 1024), np.float32)
    for e in range(8):
        oh8[e, 128 * e:128 * e + 128] = 1.0
    out["oh8"] = oh8
    oh2 = np.zeros((2, 256), np.float32)
    oh2[0, 0:128] = 1.0
    oh2[1, 128:256] = 1.0
    out["oh2"] = oh2
    return out


def host_percore(inp, core):
    x0 = np.asarray(inp["L0_attn_x"]).reshape(T_SEQ, D)
    start = 1024 * core - HALO - W0
    pos = start + np.arange(TW)
    xl = _pf32(x0[np.clip(pos, 0, T_SEQ - 1)])
    xl[(pos < 0) | (pos >= T_SEQ)] = 0.0

    scale = 64 ** -0.5
    r = np.arange(128)[:, None]
    c = np.arange(144)[None, :]
    dd = np.abs(r + 8 - c).astype(np.float32)
    band = (c >= r) & (c < r + 16)
    mmul = np.where(band, scale * (8.0 - dd / 2.0) / 8.0, 0.0).astype(ml_dtypes.bfloat16)
    madd = np.where(band, 0.0, -1e30).astype(ml_dtypes.bfloat16)
    mm_first, ma_first = mmul.copy(), madd.copy()
    mm_last, ma_last = mmul.copy(), madd.copy()
    if core == 0:
        mm_first[:, :8] = 0.0
        ma_first[:, :8] = -1e30
    if core == NCORES - 1:
        mm_last[:, 136:] = 0.0
        ma_last[:, 136:] = -1e30
    return {
        "x_loc": xl,
        "mm_first": mm_first, "ma_first": ma_first,
        "mm_mid": mmul, "ma_mid": madd,
        "mm_last": mm_last, "ma_last": ma_last,
    }


class KB:
    def __init__(self):
        self.nc = bass.Bass("TRN2", target_bir_lowering=False)
        self.cnt = {}
        self.sems = {}
        self._ctx = []

    def sem(self, name):
        h = self.nc.semaphore(name)
        sem = h.__enter__()
        self._ctx.append(h)
        self.sems[name] = sem
        self.cnt[name] = 0
        return sem

    def sb(self, name, shape, dtype):
        h = self.nc.sbuf_tensor(name, shape, dtype)
        t = h.__enter__()
        self._ctx.append(h)
        return t

    def ps(self, name, shape, dtype):
        h = self.nc.psum_tensor(name, shape, dtype)
        t = h.__enter__()
        self._ctx.append(h)
        return t

    def inc(self, instr, name, k=1):
        instr.then_inc(self.sems[name], k)
        self.cnt[name] += k
        return self.cnt[name]

    def w(self, eng, name, val=None):
        v = self.cnt[name] if val is None else val
        if v > 0:
            eng.wait_ge(self.sems[name], v)

    def close(self):
        for h in reversed(self._ctx):
            h.__exit__(None, None, None)


def build_program():
    kb = KB()
    nc = kb.nc

    def ein(name, shape, dt=F32):
        return nc.dram_tensor(name, shape, dt, kind="ExternalInput").ap()

    x_loc = ein("x_loc", [TW, D])
    wt_in_a = ein("wt_in_a", [D, 32])
    wt_in_cbx = ein("wt_in_cbx", [48, 128, 2048])
    wt_in_z = ein("wt_in_z", [D, D])
    wt_out = ein("wt_out", [16, 128, 2048])
    wt_gate = ein("wt_gate", [44, 128, 2048])
    wt_up = ein("wt_up", [44, 128, 2048])
    wt_down = ein("wt_down", [4, 16, 128, 1408])
    s_gate = ein("s_gate", [128, 44])
    s_up = ein("s_up", [128, 44])
    s_down = ein("s_down", [128, 16])
    gamma = ein("gamma", [1, D])
    beta = ein("beta", [1, D])
    dmat = ein("dmat", [32, 64])
    wt_qkv = ein("wt_qkv", [48, 128, 2048])
    wt_proj = ein("wt_proj", [16, 128, 2048])
    proj_b = ein("proj_b", [128, 16])
    wt_router = ein("wt_router", [D, 8])
    wt_eg = ein("wt_eg", [8, 11, 128, 2048])
    wt_eu = ein("wt_eu", [8, 11, 128, 2048])
    wt_ed = ein("wt_ed", [8, 16, 128, 1408])
    wt_sg = ein("wt_sg", [22, 128, 2048])
    wt_su = ein("wt_su", [22, 128, 2048])
    wt_sd = ein("wt_sd", [2, 16, 128, 1408])
    ident_f = ein("ident_f", [128, 128])
    ident_b = ein("ident_b", [128, 128], BF16)
    neglt = ein("neglt", [128, 128])
    lt_b = ein("lt_b", [128, 128], BF16)
    sel_last = ein("sel_last", [128, 256])
    oh32 = ein("oh32", [32, 4096])
    oh8 = ein("oh8", [8, 1024])
    oh2 = ein("oh2", [2, 256])
    am = {nm: ein(nm, [128, 144], BF16) for nm in
          ("mm_first", "ma_first", "mm_mid", "ma_mid", "mm_last", "ma_last")}

    out = nc.dram_tensor("out", [OWN, D], F32, kind="ExternalOutput").ap()
    dbg = None
    if DBG:
        dbg = nc.dram_tensor("dbg", [128, 31000], F32, kind="ExternalOutput").ap()

    A = kb.sb("A_x", [128, NFT * TE], F32)
    B = kb.sb("B_xn", [128, NFT * TW], BF16)
    C = kb.sb("C_scr", [128, 11712], BF16)
    Dn = kb.sb("D_y", [128, 18432], BF16)
    E = kb.sb("E_w", [128, 4096], BF16)
    E2 = kb.sb("E2_w", [128, 4096], BF16)
    Fa = kb.sb("F_bc", [128, 1024], F32)
    G = kb.sb("G_ms", [128, 1024], F32)
    Hc = kb.sb("H_cn", [128, 896], BF16)
    CNST = kb.sb("CNST", [128, 512], F32)
    OH32s = kb.sb("OH32s", [32, 256], F32)
    OH2 = kb.sb("OH2", [2, 256], F32)
    OH8s = kb.sb("OH8s", [8, 256], F32)
    AMASK = kb.sb("AMASK", [128, 864], BF16)
    DMT = kb.sb("DMT", [32, 64], F32)
    SCL = kb.sb("SCL", [128, 104], F32)
    PRB = kb.sb("PRB", [128, 16], F32)
    R1 = kb.sb("R1", [128, 2 * TW], F32)

    pa = kb.ps("pa", [128, 512], F32)
    pb = kb.ps("pb", [128, 512], F32)
    pc = kb.ps("pc", [128, 512], F32)
    pd = kb.ps("pd", [128, 512], F32)
    pg = kb.ps("pg", [128, 512], F32)
    ph = kb.ps("ph", [128, 512], F32)
    pt_a = kb.ps("pt_a", [128, 1024], BF16)
    pt_b = kb.ps("pt_b", [128, 1024], BF16)

    for s in ("sx0", "sx1", "sxc", "so0", "so1", "s80", "s81", "sxg",
              "wE0", "wE1", "wF0", "wF1", "wS0", "wS1", "wS2", "wS3",
              "wZ", "wH", "pe", "ve", "ac", "gp", "st0", "st1"):
        kb.sem(s)

    SUMS = R1[0:1, :]
    MEAN = R1[32:33, :]

    ev = {}

    def c_identf(): return CNST[:, 0:128]
    def c_neglt(): return CNST[:, 128:256]
    def c_sel(): return CNST[:, 256:384]
    def c_ones1f(p=0): return CNST[p:p + 1, 384:512]
    def c_onesbf(): return Hc[:, 0:1]
    def c_identb(): return Hc[:, 128:256]
    def c_ltb(): return Hc[:, 256:384]

    SRF = Fa[:, 512:576]
    SRB = C[:, 8032:8096]
    Dn32 = Dn[:, :].bitcast(F32)
    XW = C[:, :].bitcast(F32)
    GSB = Fa[:, :].bitcast(BF16)

    def x_ext(ft, c0, ln):
        return A[:, TE * ft + c0: TE * ft + c0 + ln]

    def x_warm(ft, c0, ln):
        return XW[:, W0 * ft + c0: W0 * ft + c0 + ln]

    def xb(ft, c0, ln):
        return B[:, TW * ft + c0: TW * ft + c0 + ln]

    def xslot(par):
        return Dn32[:, 2048 * par:2048 * (par + 1)]

    with nc.Block() as block:
        FSEMS = ("pe", "ve", "ac", "gp", "st0", "st1", "sx0", "sx1", "sxc",
                 "so0", "so1", "s80", "s81", "sxg", "wE0", "wE1", "wF0",
                 "wF1", "wS0", "wS1", "wS2", "wS3", "wZ", "wH")

        def fence(engines=("tensor", "vector", "scalar", "gpsimd", "sync")):
            def vset(vector):
                vector.tensor_copy(G[0:1, 1020:1021], G[0:1, 1021:1022])
                kb.inc(vector.tensor_copy(G[0:1, 1020:1021],
                                          G[0:1, 1021:1022]), "ve", 1)
            block.vector(vset)

            def gset(gp):
                gp.tensor_copy(G[0:1, 1021:1022], G[0:1, 1022:1023])
                kb.inc(gp.tensor_copy(G[0:1, 1021:1022],
                                      G[0:1, 1022:1023]), "gp", 1)
            block.gpsimd(gset)

            def mk(eng):
                for s in FSEMS:
                    kb.w(eng, s)
            for en in engines:
                getattr(block, en)(mk)

        def p0(sync):
            for dst, src in ((CNST[:, 0:128], ident_f),
                             (CNST[:, 128:256], neglt),
                             (CNST[:, 256:384], sel_last[:, 0:128]),
                             (G[:, 748:876], sel_last[:, 128:256]),
                             (DMT[:, :], dmat), (OH2[:, :], oh2),
                             (SCL[:, 0:44], s_gate), (SCL[:, 44:88], s_up),
                             (SCL[:, 88:104], s_down), (PRB[:, :], proj_b),
                             (Hc[:, 128:256], ident_b), (Hc[:, 256:384], lt_b)):
                kb.inc(sync.dma_start(out=dst, in_=src), "sxc", 16)
            for i, nm in enumerate(("mm_first", "ma_first", "mm_mid", "ma_mid",
                                    "mm_last", "ma_last")):
                kb.inc(sync.dma_start(out=AMASK[:, 144 * i:144 * (i + 1)],
                                      in_=am[nm]), "sxc", 16)
        block.sync(p0)

        def p0v(vector):
            vector.memset(CNST[:, 384:512], 1.0)
            vector.memset(G[:, 1023:1024], EPS)
            _li = vector.memset(Hc[:, 0:1], 1.0)
            kb.inc(_li, "ve", 1)
        block.vector(p0v)

        for tt in range(11):
            rows = 128 if tt < 10 else 16
            par = tt % 2

            def p0x_s(sync, tt=tt, rows=rows, par=par):
                if tt >= 2:
                    kb.w(sync, "pe", ev[("p0t", tt - 2)])
                sem = "sx%d" % par
                kb.inc(sync.dma_start(out=xslot(par)[0:rows, :],
                                      in_=x_loc[128 * tt:128 * tt + rows, :]),
                       sem, 16)
                ev[("p0s", tt)] = (sem, kb.cnt[sem])
            block.sync(p0x_s)

            for fg in range(4):
                def p0x_t(tensor, tt=tt, rows=rows, par=par, fg=fg):
                    if tt == 0 and fg == 0:
                        kb.w(tensor, "sxc")
                    if fg == 0:
                        kb.w(tensor, ev[("p0s", tt)][0], ev[("p0s", tt)][1])
                    bank = (pa, pb)[(tt * 4 + fg) % 2]
                    if tt * 4 + fg >= 2:
                        kb.w(tensor, "ve", ev[("p0v", tt * 4 + fg - 2)])
                    last = None
                    for f2 in range(4):
                        ft = 4 * fg + f2
                        last = tensor.transpose(
                            bank[0:128, 128 * f2:128 * f2 + rows],
                            xslot(par)[0:rows, 128 * ft:128 * (ft + 1)],
                            c_identf()[0:rows, 0:rows])
                    kb.inc(last, "pe", 1)
                    ev[("p0t_g", tt * 4 + fg)] = kb.cnt["pe"]
                    if fg == 3:
                        ev[("p0t", tt)] = kb.cnt["pe"]
                block.tensor(p0x_t)

                def p0x_v(vector, tt=tt, rows=rows, fg=fg):
                    kb.w(vector, "pe", ev[("p0t_g", tt * 4 + fg)])
                    bank = (pa, pb)[(tt * 4 + fg) % 2]
                    for f2 in range(4):
                        ft = 4 * fg + f2
                        if tt < 2:
                            dstv = x_warm(ft, 128 * tt, rows)
                        else:
                            dstv = x_ext(ft, 128 * tt - W0, rows)
                        _li = vector.tensor_copy(dstv, bank[0:128, 128 * f2:128 * f2 + rows])
                    kb.inc(_li, "ve", 1)
                    ev[("p0v", tt * 4 + fg)] = kb.cnt["ve"]
                block.vector(p0x_v)

        fence()

        def p1_cast(vector):
            for ft in range(NFT):
                vector.tensor_copy(xb(ft, 0, W0), x_warm(ft, 0, W0))
                _li = vector.tensor_copy(xb(ft, W0, TE), x_ext(ft, 0, TE))
            kb.inc(_li, "ve", 1)
        block.vector(p1_cast)
        fence()

        for ci, (c0, ln) in enumerate(CH_W):
            for ft in range(NFT):
                def p1_sq(scalar, ci=ci, c0=c0, ln=ln, ft=ft):
                    if (ci, ft) > (0, 1):
                        pi, pf = (ci, ft - 2) if ft >= 2 else (ci - 1, NFT + ft - 2)
                        kb.w(scalar, "pe", ev[("p1sqmm", pi, pf)])
                    kb.inc(scalar.activation(
                        GSB[:, 512 * (ft % 2):512 * (ft % 2) + ln],
                        xb(ft, c0, ln), AF.Square), "ac", 1)
                    ev[("p1sq", ci, ft)] = kb.cnt["ac"]
                block.scalar(p1_sq)

                def p1_mm(tensor, ci=ci, c0=c0, ln=ln, ft=ft):
                    if ft == 0 and ci >= 1:
                        kb.w(tensor, "ve", ev[("p1ev", ci - 1)])
                    tensor.matmul(pg[0:1, 0:ln], c_onesbf(), xb(ft, c0, ln),
                                  start=(ft == 0), stop=(ft == NFT - 1))
                    kb.w(tensor, "ac", ev[("p1sq", ci, ft)])
                    kb.inc(tensor.matmul(ph[0:1, 0:ln], c_onesbf(),
                                         GSB[:, 512 * (ft % 2):512 * (ft % 2) + ln],
                                         start=(ft == 0), stop=(ft == NFT - 1)),
                           "pe", 1)
                    ev[("p1sqmm", ci, ft)] = kb.cnt["pe"]
                    if ft == NFT - 1:
                        ev[("p1mm", ci)] = kb.cnt["pe"]
                block.tensor(p1_mm)

            def p1_ev(vector, ci=ci, c0=c0, ln=ln):
                kb.w(vector, "pe", ev[("p1mm", ci)])
                vector.tensor_copy(SUMS[0:1, c0:c0 + ln], pg[0:1, 0:ln])
                _li = vector.tensor_copy(SUMS[0:1, TW + c0:TW + c0 + ln], ph[0:1, 0:ln])
                kb.inc(_li, "ve", 1)
                ev[("p1ev", ci)] = kb.cnt["ve"]
            block.vector(p1_ev)
        fence()

        def p1_rows(vector):
            vector.tensor_scalar_mul(MEAN[0:1, 0:TW], SUMS[0:1, 0:TW], 1.0 / D)
            vector.tensor_scalar_mul(SUMS[0:1, TW:], SUMS[0:1, TW:], 1.0 / D)
            vector.tensor_mul(SUMS[0:1, 0:TW], MEAN[0:1, 0:TW], MEAN[0:1, 0:TW])
            _li = vector.tensor_sub(SUMS[0:1, TW:], SUMS[0:1, TW:], SUMS[0:1, 0:TW])
            kb.inc(_li, "ve", 1)
        block.vector(p1_rows)
        fence(("scalar",))

        def p1_sqrt(scalar):
            kb.inc(scalar.activation(SUMS[0:1, TW:], SUMS[0:1, TW:], AF.Sqrt,
                                     bias=G[0:1, 1023:1024]), "ac", 1)
        block.scalar(p1_sqrt)
        fence(("vector",))

        def p1_rstd(vector):
            _li = vector.reciprocal(MEAN[0:1, TW:], SUMS[0:1, TW:])
            kb.inc(_li, "ve", 1)
        block.vector(p1_rstd)
        fence()

        def Mb(ln): return Fa[:, 0:ln]
        def Rb(ln): return Fa[:, 512:512 + ln]

        CH_A = [(0, 256), (256, 512), (768, 512), (1024, 272)]
        for ci, (c0, ln) in enumerate(CH_A):
            def p1_bc(tensor, ci=ci, c0=c0, ln=ln):
                if ci >= 1:
                    kb.w(tensor, "ve", ev[("p1ap", ci - 1)])
                kb.inc(tensor.matmul(pa[0:128, 0:ln], c_ones1f(32),
                                     MEAN[0:1, c0:c0 + ln],
                                     start=True, stop=True), "pe", 1)
                kb.inc(tensor.matmul(pb[0:128, 0:ln], c_ones1f(32),
                                     MEAN[0:1, TW + c0:TW + c0 + ln],
                                     start=True, stop=True), "pe", 1)
                ev[("p1bc", ci)] = kb.cnt["pe"]
            block.tensor(p1_bc)

            def p1_ap(vector, ci=ci, c0=c0, ln=ln):
                kb.w(vector, "pe", ev[("p1bc", ci)])
                vector.tensor_copy(Mb(ln), pa[0:128, 0:ln])
                vector.tensor_copy(Rb(ln), pb[0:128, 0:ln])
                for ft in range(NFT):
                    xsrc = x_warm(ft, c0, ln) if c0 == 0 else x_ext(ft, c0 - W0, ln)
                    vector.tensor_sub(G[:, 0:ln], xsrc, Mb(ln))
                    _li = vector.tensor_mul(xb(ft, c0, ln), G[:, 0:ln], Rb(ln))
                kb.inc(_li, "ve", 1)
                ev[("p1ap", ci)] = kb.cnt["ve"]
            block.vector(p1_ap)
        fence()

        def load_strip(gp, swz_ap, ot, slot_ap, sem, kmax=16):
            kb.inc(gp.dma_start(out=slot_ap[:, 0:kmax * 128],
                                in_=swz_ap[ot]), sem, 16)
            return (sem, kb.cnt[sem])

        def eslot(i):
            return E[:, 2048 * (i % 2):2048 * (i % 2) + 2048]

        def eslot2(i):
            return E2[:, 2048 * (i % 2):2048 * (i % 2) + 2048]

        def wtile(slot, kt, owid=128):
            return slot[:, owid * kt:owid * (kt + 1)]

        def hq(ot, c0, ln):
            return C[:, TE * ot + (c0 - W0): TE * ot + (c0 - W0) + ln]

        if DBG != 3:
            for q in range(NQ):
                for ot in range(11):
                    gi = q * 11 + ot

                    def p2_w(gp, q=q, ot=ot, gi=gi):
                        if gi >= 2:
                            kb.w(gp, "pe", ev[("p2u", 3 * (gi - 2) + 2)])
                        if ot < 2 and q >= 1:
                            kb.w(gp, "pe", ev[("p2d", 3 * (q * 16 - 1) + 2)])
                        e1 = load_strip(gp, wt_gate, gi, eslot(gi),
                                        "wE%d" % (gi % 2))
                        e2 = load_strip(gp, wt_up, gi, eslot2(gi),
                                        "wF%d" % (gi % 2))
                        ev[("p2w", gi)] = (e1, e2)
                    block.gpsimd(p2_w)

                    for ci, (c0, ln) in enumerate(CH_E):
                        idx = gi * 3 + ci

                        def p2_mm(tensor, q=q, ot=ot, gi=gi, ci=ci, c0=c0, ln=ln,
                                  idx=idx):
                            if ci == 0:
                                for sm, v in ev[("p2w", gi)]:
                                    kb.w(tensor, sm, v)
                            if idx >= 2:
                                kb.w(tensor, "ac", ev[("p2sl", idx - 2)])
                            for kt in range(16):
                                last = tensor.matmul(
                                    (pa, pb)[idx % 2][:, 0:ln],
                                    wtile(eslot(gi), kt), xb(kt, c0, ln),
                                    start=(kt == 0), stop=(kt == 15))
                            kb.inc(last, "pe", 1)
                            ev[("p2g", idx)] = kb.cnt["pe"]
                            if idx >= 2:
                                kb.w(tensor, "ve", ev[("p2h", idx - 2)])
                            for kt in range(16):
                                last = tensor.matmul(
                                    (pc, pd)[idx % 2][:, 0:ln],
                                    wtile(eslot2(gi), kt), xb(kt, c0, ln),
                                    start=(kt == 0), stop=(kt == 15))
                            kb.inc(last, "pe", 1)
                            ev[("p2u", idx)] = kb.cnt["pe"]
                        block.tensor(p2_mm)

                        def p2_silu(scalar, q=q, ot=ot, ci=ci, c0=c0, ln=ln, idx=idx):
                            kb.w(scalar, "pe", ev[("p2g", idx)])
                            if idx >= 2:
                                kb.w(scalar, "ve", ev[("p2h", idx - 2)])
                            kb.inc(scalar.activation(
                                GSB[:, 512 * (idx % 2):512 * (idx % 2) + ln],
                                (pa, pb)[idx % 2][:, 0:ln], AF.Silu,
                                scale=SCL[:, q * 11 + ot:q * 11 + ot + 1]), "ac", 1)
                            ev[("p2sl", idx)] = kb.cnt["ac"]
                        block.scalar(p2_silu)

                        def p2_h(vector, q=q, ot=ot, ci=ci, c0=c0, ln=ln, idx=idx):
                            kb.w(vector, "pe", ev[("p2u", idx)])
                            kb.w(vector, "ac", ev[("p2sl", idx)])
                            _li = vector.scalar_tensor_tensor(
                                hq(ot, c0, ln), (pc, pd)[idx % 2][:, 0:ln],
                                SCL[:, 44 + q * 11 + ot:44 + q * 11 + ot + 1],
                                GSB[:, 512 * (idx % 2):512 * (idx % 2) + ln],
                                ALU.mult, ALU.mult)
                            kb.inc(_li, "ve", 1)
                            ev[("p2h", idx)] = kb.cnt["ve"]
                        block.vector(p2_h)
                fence()

                for ot in range(16):
                    di = q * 16 + ot

                    def p2_dw(gp, q=q, ot=ot, di=di):
                        if di % 16 >= 2:
                            kb.w(gp, "pe", ev[("p2d", 3 * (di - 2) + 2)])
                        else:
                            kb.w(gp, "pe", ev[("p2u", 3 * (q * 11 + 10) + 2)])
                        ev[("p2dw", di)] = [load_strip(gp, wt_down[q], ot,
                                                       eslot(di),
                                                       "wE%d" % (di % 2),
                                                       kmax=11)]
                    block.gpsimd(p2_dw)

                    for ci, (c0, ln) in enumerate(CH_E):
                        idx = di * 3 + ci

                        def p2_dmm(tensor, q=q, ot=ot, di=di, ci=ci, c0=c0, ln=ln,
                                   idx=idx):
                            if ci == 0:
                                for sm, v in ev[("p2dw", di)]:
                                    kb.w(tensor, sm, v)
                            if di % 16 * 3 + ci >= 2:
                                kb.w(tensor, "ve", ev[("p2da", idx - 2)])
                            for kt in range(11):
                                last = tensor.matmul(
                                    (pa, pb)[idx % 2][:, 0:ln],
                                    wtile(eslot(di), kt), hq(kt, c0, ln),
                                    start=(kt == 0), stop=(kt == 10))
                            kb.inc(last, "pe", 1)
                            ev[("p2d", idx)] = kb.cnt["pe"]
                        block.tensor(p2_dmm)

                        def p2_dadd(vector, ot=ot, ci=ci, c0=c0, ln=ln, idx=idx):
                            kb.w(vector, "pe", ev[("p2d", idx)])
                            xd = x_ext(ot, c0 - W0, ln)
                            _li = vector.scalar_tensor_tensor(
                                xd, (pa, pb)[idx % 2][:, 0:ln],
                                SCL[:, 88 + ot:88 + ot + 1], xd,
                                ALU.mult, ALU.add)
                            kb.inc(_li, "ve", 1)
                            ev[("p2da", idx)] = kb.cnt["ve"]
                        block.vector(p2_dadd)
                fence()


        def awt(kt):
            return Hc[:, 384 + 32 * kt:384 + 32 * (kt + 1)]

        def p3_w(gp):
            src = wt_in_a.rearrange("(kt p) o -> p kt o", p=128)
            kb.inc(gp.dma_start(
                out=Hc[:, 384:896].rearrange("p (kt o) -> p kt o", o=32),
                in_=src[:, :, 0:32]), "wH", 16)
            ev["p3w"] = ("wH", kb.cnt["wH"])
        block.gpsimd(p3_w)

        for c, (tc0, qc) in enumerate(CHUNKS):
            def p3_mm(tensor, c=c, tc0=tc0, qc=qc):
                if c == 0:
                    kb.w(tensor, ev["p3w"][0], ev["p3w"][1])
                if c >= 2:
                    kb.w(tensor, "ac", ev[("p3ea", c - 2)])
                bank = (pg, ph)[c % 2]
                for kt in range(16):
                    last = tensor.matmul(bank[0:qc, 0:32], xb(kt, tc0, qc),
                                         awt(kt), start=(kt == 0), stop=(kt == 15))
                kb.inc(last, "pe", 1)
                ev[("p3mm", c)] = kb.cnt["pe"]
            block.tensor(p3_mm)

            def p3_ea(scalar, c=c, qc=qc):
                kb.w(scalar, "pe", ev[("p3mm", c)])
                bank = (pg, ph)[c % 2]
                kb.inc(scalar.activation(G[0:qc, 32 * c:32 * c + 32],
                                         bank[0:qc, 0:32], AF.Exp), "ac", 1)
                ev[("p3ea", c)] = kb.cnt["ac"]
            block.scalar(p3_ea)
        fence()

        GC_CAG = (352, 480)
        GC_NCG = (608, 610)
        GC_EAT = (612, 614)
        GC_DTB = (616, 618)
        GC_DBC = (620, 684)
        GC_SEL15 = 748
        FA_TCL = (0, 128)
        FA_XD = (256, 320)
        FA_Y2 = (384, 448)
        CBX0 = 0
        CBX1 = 3888
        C_XT = (7776, 7840)
        C_BH = (7904, 7968)

        def cbx(gp2, j):
            base = (CBX0, CBX1)[gp2] + 1296 * j
            return C[:, base:base + 1296]

        def slot4(si):
            sl = si % 4
            return (E, E, E2, E2)[sl][:, 2048 * (sl % 2):2048 * (sl % 2) + 2048]

        def etile(sp):
            return Hc[:, 384 + 128 * sp:384 + 128 * (sp + 1)]

        def wtile_m(sp):
            return Hc[:, 640 + 128 * sp:640 + 128 * (sp + 1)]

        def psW(sp):
            return (pa, pb)[sp]

        def psT(sp):
            return (pt_a, pt_b)[sp]

        def pcd(cp):
            return (pc, pd)[cp]

        NWARM = WARM_CHUNKS
        for g in range(16):
            gp2 = g % 2
            for j in range(3):
                si = 3 * g + j

                def p4_w(gp, g=g, j=j, si=si):
                    if si >= 4:
                        kb.w(gp, "pe", ev[("p4smm", si - 4)])
                    ev[("p4w", si)] = [load_strip(gp, wt_in_cbx, si, slot4(si),
                                                  "wS%d" % (si % 4))]
                block.gpsimd(p4_w)

                for ci, (c0, ln) in enumerate(CH_W):
                    def p4_smm(tensor, g=g, j=j, si=si, ci=ci, c0=c0, ln=ln):
                        if ci == 0:
                            for sm, v in ev[("p4w", si)]:
                                kb.w(tensor, sm, v)
                        idx = si * 3 + ci
                        if idx >= 2:
                            kb.w(tensor, "ve", ev[("p4sev", idx - 2)])
                        bank = (pg, ph)[idx % 2]
                        for kt in range(16):
                            last = tensor.matmul(bank[:, 0:ln],
                                                 wtile(slot4(si), kt), xb(kt, c0, ln),
                                                 start=(kt == 0), stop=(kt == 15))
                        kb.inc(last, "pe", 1)
                        ev[("p4smmi", idx)] = kb.cnt["pe"]
                        if ci == 2:
                            ev[("p4smm", si)] = kb.cnt["pe"]
                    block.tensor(p4_smm)

                    def p4_sev(vector, g=g, j=j, si=si, ci=ci, c0=c0, ln=ln):
                        idx = si * 3 + ci
                        kb.w(vector, "pe", ev[("p4smmi", idx)])
                        if ci == 0 and g >= 2:
                            kb.w(vector, "pe", ev[("p4gdone", g - 2)])
                        bank = (pg, ph)[idx % 2]
                        _li = vector.tensor_copy(cbx(g % 2, j)[:, c0:c0 + ln],
                                           bank[:, 0:ln])
                        kb.inc(_li, "ve", 1)
                        ev[("p4sev", idx)] = kb.cnt["ve"]
                    block.vector(p4_sev)

            def p4_sinit(vector, g=g):
                if g >= 1:
                    kb.w(vector, "pe", ev[("p4gdone", g - 1)])
                    kb.w(vector, "gp", ev[("stepG", g - 1, 2 * len(CHUNKS) - 1)])
                vector.memset(SRF[:, :], 0.0)
                _li = vector.memset(SRB[:, :], 0.0)
                kb.inc(_li, "ve", 1)
                ev[("p4si", g)] = kb.cnt["ve"]
            block.vector(p4_sinit)

            for hh in range(2):
                hgl = 2 * g + hh
                hp = hgl % 2

                def p4_ohs(sync, hgl=hgl, hp=hp):
                    if hgl >= 2:
                        kb.w(sync, "pe", ev[("dbcmm", hgl - 2)])
                    sem = "so%d" % hp
                    kb.inc(sync.dma_start(out=OH32s[:, 128 * hp:128 * (hp + 1)],
                                          in_=oh32[:, 128 * hgl:128 * (hgl + 1)]),
                           sem, 16)
                    ev[("ohs", hgl)] = (sem, kb.cnt[sem])
                block.sync(p4_ohs)

                def p4_dbc(tensor, g=g, hh=hh, hgl=hgl, hp=hp):
                    kb.w(tensor, ev[("ohs", hgl)][0], ev[("ohs", hgl)][1])
                    if hgl >= 2:
                        kb.w(tensor, "ve", ev[("dbcev", hgl - 2)])
                    bank = psW(hh)
                    kb.inc(tensor.matmul(bank[:, 448:512],
                                         OH32s[:, 128 * hp:128 * (hp + 1)],
                                         DMT[:, :], start=True, stop=True), "pe", 1)
                    ev[("dbcmm", hgl)] = kb.cnt["pe"]
                block.tensor(p4_dbc)

                def p4_dbcv(vector, hh=hh, hgl=hgl, hp=hp):
                    kb.w(vector, "pe", ev[("dbcmm", hgl)])
                    bank = psW(hh)
                    _li = vector.tensor_copy(G[:, GC_DBC[0] + 64 * hp:GC_DBC[0] + 64 * (hp + 1)],
                                       bank[:, 448:512])
                    kb.inc(_li, "ve", 1)
                    ev[("dbcev", hgl)] = kb.cnt["ve"]
                block.vector(p4_dbcv)

            def emit_prep(c):
                tc0, qc = CHUNKS[c]
                cp = c % 2
                sel_ap = (c_sel()[0:qc, 0:128] if qc == 128
                          else G[0:qc, GC_SEL15:GC_SEL15 + 128])
                def prep_t1(tensor, g=g, c=c, tc0=tc0, qc=qc, cp=cp):
                    if c >= 2:
                        kb.w(tensor, "ve", ev[("prepv2", g, c - 2)])
                        kb.w(tensor, "ve", ev[("stepV", g, 2 * (c - 2) + 1)])
                        kb.w(tensor, "gp", ev[("stepG", g, 2 * (c - 2) + 1)])
                    kb.inc(tensor.matmul(pcd(cp)[0:2, 0:qc],
                                         G[0:qc, 32 * c + 2 * g:32 * c + 2 * g + 2],
                                         c_neglt()[0:qc, 0:qc],
                                         start=True, stop=True), "pe", 1)
                    ev[("prept1", g, c)] = kb.cnt["pe"]
                block.tensor(prep_t1)

                def prep_v1(vector, g=g, c=c, qc=qc, cp=cp):
                    kb.w(vector, "pe", ev[("prept1", g, c)])
                    _li = vector.tensor_copy(G[0:2, GC_CAG[0] + 128 * cp:
                                         GC_CAG[0] + 128 * cp + qc],
                                       pcd(cp)[0:2, 0:qc])
                    kb.inc(_li, "ve", 1)
                    ev[("prepv1", g, c)] = kb.cnt["ve"]
                block.vector(prep_v1)

                def prep_t2(tensor, g=g, c=c, qc=qc, cp=cp):
                    kb.w(tensor, "ve", ev[("prepv1", g, c)])
                    kb.inc(tensor.transpose(
                        pcd(cp)[0:qc, 128:130],
                        G[0:2, GC_CAG[0] + 128 * cp:GC_CAG[0] + 128 * cp + qc],
                        c_identf()[0:2, 0:2]), "pe", 1)
                    ev[("prept2", g, c)] = kb.cnt["pe"]
                block.tensor(prep_t2)

                def prep_a1(scalar, g=g, c=c, qc=qc, cp=cp):
                    kb.w(scalar, "pe", ev[("prept2", g, c)])
                    if c >= 2:
                        kb.w(scalar, "ve", ev[("stepV", g, 2 * (c - 2) + 1)])
                    kb.inc(scalar.activation(
                        G[0:qc, GC_EAT[0] + 2 * cp:GC_EAT[0] + 2 * cp + 2],
                        pcd(cp)[0:qc, 128:130], AF.Exp), "ac", 1)
                    ev[("prepa1", g, c)] = kb.cnt["ac"]
                block.scalar(prep_a1)

                def prep_vn(vector, g=g, c=c, qc=qc, cp=cp):
                    kb.w(vector, "pe", ev[("prept2", g, c)])
                    if c >= 2:
                        kb.w(vector, "ac",
                             ev[("stA1", g, 2 * (c - 2) + 1)])
                    _li = vector.tensor_scalar_mul(
                        G[0:qc, GC_NCG[0] + 2 * cp:GC_NCG[0] + 2 * cp + 2],
                        pcd(cp)[0:qc, 128:130], -1.0)
                    kb.inc(_li, "ve", 1)
                    ev[("prepvn", g, c)] = kb.cnt["ve"]
                block.vector(prep_vn)

                def prep_t3(tensor, g=g, c=c, qc=qc, cp=cp, sel_ap=sel_ap):
                    kb.w(tensor, "ac", ev[("prepa1", g, c)])
                    kb.inc(tensor.matmul(
                        pcd(cp)[0:128, 132:134], sel_ap,
                        G[0:qc, GC_EAT[0] + 2 * cp:GC_EAT[0] + 2 * cp + 2],
                        start=True, stop=True), "pe", 1)
                    ev[("prept3", g, c)] = kb.cnt["pe"]
                block.tensor(prep_t3)

                def prep_v2(vector, g=g, c=c, cp=cp):
                    kb.w(vector, "pe", ev[("prept3", g, c)])
                    _li = vector.tensor_copy(G[:, GC_DTB[0] + 2 * cp:GC_DTB[0] + 2 * cp + 2],
                                       pcd(cp)[0:128, 132:134])
                    kb.inc(_li, "ve", 1)
                    ev[("prepv2", g, c)] = kb.cnt["ve"]
                block.vector(prep_v2)

            def emit_steps(c):
                tc0, qc = CHUNKS[c]
                cp = c % 2
                for hh in range(2):
                    s = 2 * c + hh
                    sp = s % 2
                    hgl = 2 * g + hh
                    hp = hgl % 2
                    warm = c < NWARM
                    ncg = G[0:qc, GC_NCG[0] + 2 * cp + hh:GC_NCG[0] + 2 * cp + hh + 1]
                    eat = G[0:qc, GC_EAT[0] + 2 * cp + hh:GC_EAT[0] + 2 * cp + hh + 1]
                    dtb = G[64 * hh:64 * hh + 64,
                            GC_DTB[0] + 2 * cp + hh:GC_DTB[0] + 2 * cp + hh + 1]
                    dbc = G[0:qc, GC_DBC[0] + 64 * hp:GC_DBC[0] + 64 * hp + 64]
                    xtok = C[0:qc, C_XT[0] + 64 * sp:C_XT[0] + 64 * sp + 64]
                    bhat = C[0:qc, C_BH[0] + 64 * sp:C_BH[0] + 64 * sp + 64]
                    tcl = Fa[0:qc, FA_TCL[0] + 128 * sp:FA_TCL[0] + 128 * sp + qc]
                    xd = Fa[0:qc, FA_XD[0] + 64 * sp:FA_XD[0] + 64 * sp + 64]
                    y2 = Fa[0:qc, FA_Y2[0] + 64 * sp:FA_Y2[0] + 64 * sp + 64]
                    srf = SRF[64 * hh:64 * hh + 64, 0:64]
                    srb = SRB[64 * hh:64 * hh + 64, 0:64]
                    cgs = cbx(gp2, 0)[64 * hh:64 * hh + 64, tc0:tc0 + qc]
                    bgs = cbx(gp2, 1)[64 * hh:64 * hh + 64, tc0:tc0 + qc]
                    xgs = cbx(gp2, 2)[64 * hh:64 * hh + 64, tc0:tc0 + qc]
                    idb64 = c_identb()[64 * hh:64 * hh + 64, 64 * hh:64 * hh + 64]

                    def st_tA(tensor, g=g, c=c, s=s, sp=sp, hh=hh, qc=qc, cp=cp):
                        if s >= 2:
                            kb.w(tensor, "ve", ev[("stepV", g, s - 2)])
                            kb.w(tensor, "gp", ev[("stepG", g, s - 2)])
                        elif g >= 1:
                            kb.w(tensor, "ve", ev[("stepV", g - 1, 2 * len(CHUNKS) - 2 + sp)])
                            kb.w(tensor, "gp", ev[("stepG", g - 1, 2 * len(CHUNKS) - 2 + sp)])
                        kb.w(tensor, "ve", ev[("prepv1", g, c)])
                        kb.inc(tensor.matmul(
                            psW(sp)[0:qc, 0:qc],
                            OH2[0:2, 128 * hh:128 * hh + qc],
                            G[0:2, GC_CAG[0] + 128 * cp:GC_CAG[0] + 128 * cp + qc],
                            start=True, stop=True), "pe", 1)
                        ev[("stT1", g, s)] = kb.cnt["pe"]
                    block.tensor(st_tA)

                    def st_vA(vector, g=g, c=c, s=s, sp=sp, qc=qc, ncg=ncg, tcl=tcl):
                        kb.w(vector, "pe", ev[("stT1", g, s)])
                        if s >= 2:
                            kb.w(vector, "ac", ev[("stA1", g, s - 2)])
                        _li = vector.tensor_scalar(tcl, psW(sp)[0:qc, 0:qc],
                                             ncg, 0.0, ALU.add, ALU.min)
                        kb.inc(_li, "ve", 1)
                        ev[("stV1", g, s)] = kb.cnt["ve"]
                    block.vector(st_vA)

                    def st_aA(scalar, g=g, s=s, sp=sp, qc=qc, tcl=tcl):
                        kb.w(scalar, "ve", ev[("stV1", g, s)])
                        if s >= 2:
                            kb.w(scalar, "ve", ev[("stepV", g, s - 2)])
                            if ("stGP", g, s - 2) in ev:
                                kb.w(scalar, "gp", ev[("stGP", g, s - 2)])
                        scalar.activation(etile(sp)[0:qc, 0:qc], tcl, AF.Exp)
                        kb.inc(scalar.activation(G[0:qc, 908 + sp:909 + sp],
                                                 tcl[0:qc, qc - 1:qc], AF.Exp),
                               "ac", 1)
                        ev[("stA1", g, s)] = kb.cnt["ac"]
                    block.scalar(st_aA)

                    if not warm:
                        def st_gA(gp, g=g, s=s, sp=sp, qc=qc):
                            kb.w(gp, "ac", ev[("stA1", g, s)])
                            if s >= 2:
                                kb.w(gp, "ve", ev[("stepV", g, s - 2)])
                            _li = gp.tensor_mul(wtile_m(sp)[0:qc, 0:qc],
                                          etile(sp)[0:qc, 0:qc],
                                          c_ltb()[0:qc, 0:qc])
                            kb.inc(_li, "gp", 1)
                            ev[("stGP", g, s)] = kb.cnt["gp"]
                        block.gpsimd(st_gA)

                    def st_tB(tensor, g=g, s=s, sp=sp, qc=qc, hh=hh, warm=warm,
                              cgs=cgs, bgs=bgs, xgs=xgs, idb64=idb64):
                        if not warm:
                            tensor.matmul(psW(sp)[0:qc, 128:256][:, 0:qc],
                                          bgs, cgs, start=True, stop=True)
                        tensor.transpose(psT(sp)[0:qc, 0:64], xgs, idb64)
                        last = tensor.transpose(psT(sp)[0:qc, 64:128], bgs, idb64)
                        kb.inc(last, "pe", 1)
                        ev[("stT234", g, s)] = kb.cnt["pe"]
                    block.tensor(st_tB)

                    def st_vB(vector, g=g, s=s, sp=sp, qc=qc, xtok=xtok,
                              bhat=bhat, warm=warm):
                        kb.w(vector, "pe", ev[("stT234", g, s)])
                        kb.w(vector, "ac", ev[("stA1", g, s)])
                        if s >= 2:
                            kb.w(vector, "pe", ev[("stT567", g, s - 2)])
                            kb.w(vector, "ve", ev[("stepV", g, s - 2)])
                        vector.tensor_copy(xtok, psT(sp)[0:qc, 0:64])
                        _li = vector.tensor_scalar_mul(
                            bhat, psT(sp)[0:qc, 64:128],
                            G[0:qc, 908 + sp:909 + sp])
                        if not warm:
                            kb.w(vector, "gp", ev[("stGP", g, s)])
                            _li = vector.tensor_mul(wtile_m(sp)[0:qc, 0:qc],
                                              wtile_m(sp)[0:qc, 0:qc],
                                              psW(sp)[0:qc, 128:256][:, 0:qc])
                        kb.inc(_li, "ve", 1)
                        ev[("stV2", g, s)] = kb.cnt["ve"]
                    block.vector(st_vB)

                    def st_tC(tensor, g=g, s=s, sp=sp, qc=qc, hh=hh, warm=warm,
                              cgs=cgs, xtok=xtok, bhat=bhat):
                        kb.w(tensor, "ve", ev[("stV2", g, s)])
                        if s >= 2:
                            kb.w(tensor, "gp", ev[("stA3g", g, s - 2)])
                        kb.w(tensor, "ve", ev[("p4si", g)])
                        if not warm:
                            tensor.matmul(psW(sp)[0:qc, 256:320],
                                          wtile_m(sp)[0:qc, 0:qc], xtok,
                                          start=True, stop=True)
                            tensor.matmul(psW(sp)[0:qc, 320:384], cgs,
                                          SRB[64 * hh:64 * hh + 64, 0:64],
                                          start=True, stop=True)
                        last = tensor.matmul(
                            psW(sp)[64 * hh:64 * hh + 64, 384:448], bhat, xtok,
                            start=True, stop=True, tile_position=(0, 64 * hh))
                        kb.inc(last, "pe", 1)
                        ev[("stT567", g, s)] = kb.cnt["pe"]
                    block.tensor(st_tC)

                    def st_vC(vector, g=g, c=c, s=s, sp=sp, qc=qc, hh=hh,
                              hgl=hgl, warm=warm, xd=xd, y2=y2, xtok=xtok,
                              eat=eat, dtb=dtb, dbc=dbc, srf=srf):
                        kb.w(vector, "pe", ev[("stT567", g, s)])
                        kb.w(vector, "ac", ev[("prepa1", g, c)])
                        kb.w(vector, "ve", ev[("prepv2", g, c)])
                        if not warm:
                            kb.w(vector, "ve", ev[("dbcev", hgl)])
                            vector.tensor_mul(xd, xtok, dbc)
                            vector.scalar_tensor_tensor(
                                y2, psW(sp)[0:qc, 320:384][:, 0:64], eat, xd,
                                ALU.mult, ALU.add)
                            yslice = Dn[0:qc, 2048 * (c - NWARM) + 64 * hgl:
                                        2048 * (c - NWARM) + 64 * hgl + 64]
                            vector.tensor_add(yslice, y2,
                                              psW(sp)[0:qc, 256:320][:, 0:64])
                        _li = vector.scalar_tensor_tensor(
                            srf, srf, dtb,
                            psW(sp)[64 * hh:64 * hh + 64, 384:448], ALU.mult,
                            ALU.add)
                        kb.inc(_li, "ve", 1)
                        ev[("stepV", g, s)] = kb.cnt["ve"]
                    block.vector(st_vC)

                    def st_gC(gp, g=g, s=s, hh=hh, srf=srf, srb=srb):
                        kb.w(gp, "ve", ev[("stepV", g, s)])
                        _li = gp.tensor_copy(srb, srf)
                        kb.inc(_li, "gp", 1)
                        ev[("stA3g", g, s)] = kb.cnt["gp"]
                        ev[("stepA", g, s)] = kb.cnt["ac"]
                        ev[("stepG", g, s)] = kb.cnt["gp"]
                    block.gpsimd(st_gC)

            emit_prep(0)
            for c in range(len(CHUNKS)):
                if c + 1 < len(CHUNKS):
                    emit_prep(c + 1)
                emit_steps(c)
            ev[("p4gdone", g)] = kb.cnt["pe"]
        fence()

        if DBG == 4:
            def pdbg4(sync):
                for sname in FSEMS:
                    kb.w(sync, sname)
                kb.inc(sync.dma_start(out=dbg[:, 0:9216],
                                      in_=Dn[:, :].bitcast(F32)), "st0", 16)
                kb.w(sync, "st0")
            block.sync(pdbg4)

        NTT = 9

        for zg in range(4):
            def p5_zw(gp, zg=zg):
                if zg >= 1:
                    kb.w(gp, "pe", ev[("p5zmm", zg - 1)])
                src = wt_in_z.rearrange("(kt p) o -> p kt o", p=128)
                for kt in range(16):
                    buf = (E, E2)[kt // 8]
                    k2 = kt % 8
                    dst = buf[:, 512 * k2:512 * (k2 + 1)]
                    kb.inc(gp.dma_start(
                        out=dst.rearrange("p (a o) -> p a o", a=1),
                        in_=src[:, kt:kt + 1, 512 * zg:512 * (zg + 1)]),
                        "wZ", 16)
                ev[("p5zw", zg)] = kb.cnt["wZ"]
            block.gpsimd(p5_zw)

            for tt in range(NTT):
                qt = 128 if tt < 8 else 16
                idx = zg * NTT + tt

                def p5_zmm(tensor, zg=zg, tt=tt, qt=qt, idx=idx):
                    if tt == 0:
                        kb.w(tensor, "wZ", ev[("p5zw", zg)])
                    if idx >= 2:
                        kb.w(tensor, "ve", ev[("p5gat", idx - 2)])
                    bank = (pa, pb)[idx % 2]
                    for kt in range(16):
                        buf = (E, E2)[kt // 8]
                        k2 = kt % 8
                        last = tensor.matmul(
                            bank[0:qt, 0:512],
                            xb(kt, W0 + 128 * tt, qt),
                            buf[:, 512 * k2:512 * (k2 + 1)],
                            start=(kt == 0), stop=(kt == 15))
                    kb.inc(last, "pe", 1)
                    ev[("p5zmmi", idx)] = kb.cnt["pe"]
                    if tt == NTT - 1:
                        ev[("p5zmm", zg)] = kb.cnt["pe"]
                block.tensor(p5_zmm)

                def p5_sil(scalar, zg=zg, tt=tt, qt=qt, idx=idx):
                    kb.w(scalar, "pe", ev[("p5zmmi", idx)])
                    if idx >= 2:
                        kb.w(scalar, "ve", ev[("p5gat", idx - 2)])
                    bank = (pa, pb)[idx % 2]
                    kb.inc(scalar.activation(
                        GSB[0:qt, 512 * (idx % 2):512 * (idx % 2) + 512],
                        bank[0:qt, 0:512], AF.Silu), "ac", 1)
                    ev[("p5sil", idx)] = kb.cnt["ac"]
                block.scalar(p5_sil)

                def p5_gat(vector, zg=zg, tt=tt, qt=qt, idx=idx):
                    kb.w(vector, "ac", ev[("p5sil", idx)])
                    ysl = Dn[0:qt, 2048 * tt + 512 * zg:2048 * tt + 512 * zg + 512]
                    _li = vector.tensor_mul(ysl, ysl,
                                      GSB[0:qt, 512 * (idx % 2):512 * (idx % 2) + 512])
                    kb.inc(_li, "ve", 1)
                    ev[("p5gat", idx)] = kb.cnt["ve"]
                block.vector(p5_gat)
        fence()

        for cg in range(4):
            def p5_gbl(sync, cg=cg):
                if cg >= 1:
                    kb.w(sync, "ve", ev[("p5gbe", cg - 1)])
                kb.inc(sync.dma_start(out=Fa[0:1, 0:512],
                                      in_=gamma[0:1, 512 * cg:512 * (cg + 1)]),
                       "sxg", 16)
                kb.inc(sync.dma_start(out=Fa[0:1, 512:1024],
                                      in_=beta[0:1, 512 * cg:512 * (cg + 1)]),
                       "sxg", 16)
                ev[("p5gbl", cg)] = kb.cnt["sxg"]
            block.sync(p5_gbl)

            def p5_gbm(tensor, cg=cg):
                kb.w(tensor, "sxg", ev[("p5gbl", cg)])
                if cg >= 1:
                    kb.w(tensor, "ve", ev[("p5gbe", cg - 1)])
                tensor.matmul(pa[:, 0:512], c_ones1f(0), Fa[0:1, 0:512],
                              start=True, stop=True)
                kb.inc(tensor.matmul(pb[:, 0:512], c_ones1f(0), Fa[0:1, 512:1024],
                                     start=True, stop=True), "pe", 1)
                ev[("p5gbm", cg)] = kb.cnt["pe"]
            block.tensor(p5_gbm)

            def p5_gbe(vector, cg=cg):
                kb.w(vector, "pe", ev[("p5gbm", cg)])
                vector.tensor_copy(E[:, 512 * cg:512 * (cg + 1)], pa[:, 0:512])
                _li = vector.tensor_copy(E[:, 2048 + 512 * cg:2048 + 512 * (cg + 1)],
                                   pb[:, 0:512])
                kb.inc(_li, "ve", 1)
                ev[("p5gbe", cg)] = kb.cnt["ve"]
            block.vector(p5_gbe)
        fence()

        GLN = 876
        for tt in range(NTT):
            qt = 128 if tt < 8 else 16
            tp = tt % 2
            gc0 = GLN + 16 * tp
            ysl = Dn[0:qt, 2048 * tt:2048 * (tt + 1)]
            nsc = G[0:qt, gc0 + 0:gc0 + 1]
            sqc = G[0:qt, gc0 + 1:gc0 + 2]
            m2c = G[0:qt, gc0 + 2:gc0 + 3]
            qcc = G[0:qt, gc0 + 3:gc0 + 4]
            sdc = G[0:qt, gc0 + 4:gc0 + 5]
            rsc = G[0:qt, gc0 + 5:gc0 + 6]
            nmc = G[0:qt, gc0 + 6:gc0 + 7]

            def ln2_v1(vector, tt=tt, qt=qt, ysl=ysl, nsc=nsc):
                if tt >= 2:
                    kb.w(vector, "ve", ev[("ln2v6", tt - 2)])
                    kb.w(vector, "gp", ev[("ln2gp", tt - 2)])
                    kb.w(vector, "ac", ev[("ln2a2", tt - 2)])
                _li = vector.reduce_sum(nsc, ysl, AX.X, negate=True)
                kb.inc(_li, "ve", 1)
                ev[("ln2v1", tt)] = kb.cnt["ve"]
            block.vector(ln2_v1)

            def ln2_a1(scalar, tt=tt, qt=qt, ysl=ysl, sqc=sqc):
                if tt >= 2:
                    kb.w(scalar, "gp", ev[("ln2gp", tt - 2)])
                kb.inc(scalar.activation(C[0:qt, 0:2048], ysl, AF.Square,
                                          accum_out=sqc), "ac", 1)
                ev[("ln2a1", tt)] = kb.cnt["ac"]
            block.scalar(ln2_a1)

            def ln2_gp1(gp, tt=tt, nsc=nsc, m2c=m2c):
                kb.w(gp, "ve", ev[("ln2v1", tt)])
                _li = gp.tensor_mul(m2c, nsc, nsc)
                kb.inc(_li, "gp", 1)
                ev[("ln2gp1", tt)] = kb.cnt["gp"]
            block.gpsimd(ln2_gp1)

            def ln2_v2(vector, tt=tt, m2c=m2c, sqc=sqc, qcc=qcc):
                kb.w(vector, "gp", ev[("ln2gp1", tt)])
                kb.w(vector, "ac", ev[("ln2a1", tt)])
                _li = vector.scalar_tensor_tensor(qcc, m2c, -1.0 / D, sqc,
                                            ALU.mult, ALU.add)
                kb.inc(_li, "ve", 1)
                ev[("ln2v2", tt)] = kb.cnt["ve"]
            block.vector(ln2_v2)

            def ln2_a2(scalar, tt=tt, qcc=qcc, sdc=sdc):
                kb.w(scalar, "ve", ev[("ln2v2", tt)])
                kb.inc(scalar.activation(sdc, qcc, AF.Sqrt, scale=1.0 / D,
                                         bias=G[0:qt, 1023:1024]), "ac", 1)
                ev[("ln2a2", tt)] = kb.cnt["ac"]
            block.scalar(ln2_a2)

            def ln2_v3(vector, tt=tt, sdc=sdc, rsc=rsc, nsc=nsc, nmc=nmc):
                kb.w(vector, "ac", ev[("ln2a2", tt)])
                vector.tensor_scalar_mul(nmc, nsc, 1.0 / D)
                _li = vector.reciprocal(rsc, sdc)
                kb.inc(_li, "ve", 1)
                ev[("ln2v3", tt)] = kb.cnt["ve"]
            block.vector(ln2_v3)

            def ln2_gp2(gp, tt=tt, qt=qt, ysl=ysl, nmc=nmc, rsc=rsc):
                kb.w(gp, "ve", ev[("ln2v3", tt)])
                _li = gp.tensor_scalar(ysl, ysl, nmc, rsc, ALU.add, ALU.mult)
                kb.inc(_li, "gp", 1)
                ev[("ln2gp", tt)] = kb.cnt["gp"]
            block.gpsimd(ln2_gp2)

            def ln2_v6(vector, tt=tt, qt=qt, ysl=ysl):
                kb.w(vector, "gp", ev[("ln2gp", tt)])
                vector.tensor_mul(ysl, ysl, E[0:qt, 0:2048])
                _li = vector.tensor_add(ysl, ysl, E[0:qt, 2048:4096])
                kb.inc(_li, "ve", 1)
                ev[("ln2v6", tt)] = kb.cnt["ve"]
            block.vector(ln2_v6)
        fence()

        for tt in range(NTT):
            qt = 128 if tt < 8 else 16
            for fg in range(2):
                idx = tt * 2 + fg

                def yt_t(tensor, tt=tt, qt=qt, fg=fg, idx=idx):
                    if idx >= 2:
                        kb.w(tensor, "ve", ev[("ytv", idx - 2)])
                    bank = (pt_a, pt_b)[idx % 2]
                    last = None
                    for f2 in range(8):
                        ft = 8 * fg + f2
                        last = tensor.transpose(
                            bank[0:128, 128 * f2:128 * f2 + qt],
                            Dn[0:qt, 2048 * tt + 128 * ft:2048 * tt + 128 * (ft + 1)],
                            c_identb()[0:qt, 0:qt])
                    kb.inc(last, "pe", 1)
                    ev[("ytt", idx)] = kb.cnt["pe"]
                block.tensor(yt_t)

                def yt_v(vector, tt=tt, qt=qt, fg=fg, idx=idx):
                    kb.w(vector, "pe", ev[("ytt", idx)])
                    bank = (pt_a, pt_b)[idx % 2]
                    for f2 in range(8):
                        ft = 8 * fg + f2
                        _li = vector.tensor_copy(
                            B[:, TE * ft + 128 * tt:TE * ft + 128 * tt + qt],
                            bank[0:128, 128 * f2:128 * f2 + qt])
                    kb.inc(_li, "ve", 1)
                    ev[("ytv", idx)] = kb.cnt["ve"]
                block.vector(yt_v)
        fence()

        if DBG == 5:
            def pdbg5(sync):
                for sname in FSEMS:
                    kb.w(sync, sname)
                kb.inc(sync.dma_start(out=dbg[:, 0:8320],
                                      in_=B[:, 0:NFT * TE].bitcast(F32)), "st0", 16)
                kb.w(sync, "st0")
            block.sync(pdbg5)

        def ynT(ft, c0, ln):
            return B[:, TE * ft + c0:TE * ft + c0 + ln]

        CH_P5 = [(0, 512), (512, 512), (1024, 16)]
        for ot in range(16):
            def wo_w(gp, ot=ot):
                if ot >= 2:
                    kb.w(gp, "pe", ev[("womm", ot - 2)])
                ev[("wow", ot)] = [load_strip(gp, wt_out, ot, eslot2(ot),
                                              "wF%d" % (ot % 2))]
            block.gpsimd(wo_w)

            for ci, (c0, ln) in enumerate(CH_P5):
                idx = ot * 3 + ci

                def wo_mm(tensor, ot=ot, ci=ci, c0=c0, ln=ln, idx=idx):
                    if ci == 0:
                        for sm, v in ev[("wow", ot)]:
                            kb.w(tensor, sm, v)
                    if idx >= 2:
                        kb.w(tensor, "ve", ev[("woa", idx - 2)])
                    bank = (pa, pb)[idx % 2]
                    for kt in range(16):
                        last = tensor.matmul(bank[:, 0:ln], wtile(eslot2(ot), kt),
                                             ynT(kt, c0, ln),
                                             start=(kt == 0), stop=(kt == 15))
                    kb.inc(last, "pe", 1)
                    ev[("wommi", idx)] = kb.cnt["pe"]
                    if ci == 2:
                        ev[("womm", ot)] = kb.cnt["pe"]
                block.tensor(wo_mm)

                def wo_a(vector, ot=ot, ci=ci, c0=c0, ln=ln, idx=idx):
                    kb.w(vector, "pe", ev[("wommi", idx)])
                    bank = (pa, pb)[idx % 2]
                    xd_ = x_ext(ot, c0, ln)
                    _li = vector.tensor_add(xd_, xd_, bank[:, 0:ln])
                    kb.inc(_li, "ve", 1)
                    ev[("woa", idx)] = kb.cnt["ve"]
                block.vector(wo_a)
        fence()


        def xb1(ft, c0, ln):
            return B[:, TE * ft + c0:TE * ft + c0 + ln]

        def p6_cast(vector):
            for ft in range(NFT):
                _li = vector.tensor_copy(xb1(ft, 0, TE), x_ext(ft, 0, TE))
            kb.inc(_li, "ve", 1)
        block.vector(p6_cast)
        fence()

        CH_L1 = [(0, 512), (512, 512), (1024, 16)]
        for ci, (c0, ln) in enumerate(CH_L1):
            for ft in range(NFT):
                def p6_sq(scalar, ci=ci, c0=c0, ln=ln, ft=ft):
                    if (ci, ft) > (0, 1):
                        pi, pf = (ci, ft - 2) if ft >= 2 else (ci - 1, NFT + ft - 2)
                        kb.w(scalar, "pe", ev[("p6sqmm", pi, pf)])
                    kb.inc(scalar.activation(
                        GSB[:, 512 * (ft % 2):512 * (ft % 2) + ln],
                        xb1(ft, c0, ln), AF.Square), "ac", 1)
                    ev[("p6sq", ci, ft)] = kb.cnt["ac"]
                block.scalar(p6_sq)

                def p6_mm(tensor, ci=ci, c0=c0, ln=ln, ft=ft):
                    if ft == 0 and ci >= 1:
                        kb.w(tensor, "ve", ev[("p6ev", ci - 1)])
                    tensor.matmul(pg[0:1, 0:ln], c_onesbf(), xb1(ft, c0, ln),
                                  start=(ft == 0), stop=(ft == NFT - 1))
                    kb.w(tensor, "ac", ev[("p6sq", ci, ft)])
                    kb.inc(tensor.matmul(ph[0:1, 0:ln], c_onesbf(),
                                         GSB[:, 512 * (ft % 2):512 * (ft % 2) + ln],
                                         start=(ft == 0), stop=(ft == NFT - 1)),
                           "pe", 1)
                    ev[("p6sqmm", ci, ft)] = kb.cnt["pe"]
                    if ft == NFT - 1:
                        ev[("p6mm", ci)] = kb.cnt["pe"]
                block.tensor(p6_mm)

            def p6_ev(vector, ci=ci, c0=c0, ln=ln):
                kb.w(vector, "pe", ev[("p6mm", ci)])
                vector.tensor_copy(SUMS[0:1, c0:c0 + ln], pg[0:1, 0:ln])
                _li = vector.tensor_copy(SUMS[0:1, TW + c0:TW + c0 + ln], ph[0:1, 0:ln])
                kb.inc(_li, "ve", 1)
                ev[("p6ev", ci)] = kb.cnt["ve"]
            block.vector(p6_ev)
        fence()

        def p6_rows(vector):
            vector.tensor_scalar_mul(MEAN[0:1, 0:TE], SUMS[0:1, 0:TE], 1.0 / D)
            vector.tensor_scalar_mul(SUMS[0:1, TW:TW + TE], SUMS[0:1, TW:TW + TE],
                                     1.0 / D)
            vector.tensor_mul(SUMS[0:1, 0:TE], MEAN[0:1, 0:TE], MEAN[0:1, 0:TE])
            _li = vector.tensor_sub(SUMS[0:1, TW:TW + TE], SUMS[0:1, TW:TW + TE],
                              SUMS[0:1, 0:TE])
            kb.inc(_li, "ve", 1)
        block.vector(p6_rows)
        fence(("scalar",))

        def p6_sqrt(scalar):
            kb.inc(scalar.activation(SUMS[0:1, TW:TW + TE], SUMS[0:1, TW:TW + TE],
                                     AF.Sqrt, bias=G[0:1, 1023:1024]), "ac", 1)
        block.scalar(p6_sqrt)
        fence(("vector",))

        def p6_rstd(vector):
            _li = vector.reciprocal(MEAN[0:1, TW:TW + TE], SUMS[0:1, TW:TW + TE])
            kb.inc(_li, "ve", 1)
        block.vector(p6_rstd)
        fence()

        CH_A1 = [(0, 512), (512, 512), (768, 272)]
        for ci, (c0, ln) in enumerate(CH_A1):
            def p6_bc(tensor, ci=ci, c0=c0, ln=ln):
                if ci >= 1:
                    kb.w(tensor, "ve", ev[("p6ap", ci - 1)])
                tensor.matmul(pa[0:128, 0:ln], c_ones1f(32),
                              MEAN[0:1, c0:c0 + ln], start=True, stop=True)
                kb.inc(tensor.matmul(pb[0:128, 0:ln], c_ones1f(32),
                                     MEAN[0:1, TW + c0:TW + c0 + ln],
                                     start=True, stop=True), "pe", 1)
                ev[("p6bc", ci)] = kb.cnt["pe"]
            block.tensor(p6_bc)

            def p6_ap(vector, ci=ci, c0=c0, ln=ln):
                kb.w(vector, "pe", ev[("p6bc", ci)])
                vector.tensor_copy(Mb(ln), pa[0:128, 0:ln])
                vector.tensor_copy(Rb(ln), pb[0:128, 0:ln])
                for ft in range(NFT):
                    vector.tensor_sub(G[:, 0:ln], x_ext(ft, c0, ln), Mb(ln))
                    _li = vector.tensor_mul(xb1(ft, c0, ln), G[:, 0:ln], Rb(ln))
                kb.inc(_li, "ve", 1)
                ev[("p6ap", ci)] = kb.cnt["ve"]
            block.vector(p6_ap)
        fence()

        def qkv_buf(gp2, j):
            return C[:, 3120 * gp2 + 1040 * j:3120 * gp2 + 1040 * (j + 1)]

        def vtok(hp, tt):
            return C[:, 6240 + 576 * hp + 64 * tt:6240 + 576 * hp + 64 * (tt + 1)]

        def ptile(bp, part):
            return C[:, 7392 + 256 * bp + 128 * part:7392 + 256 * bp + 128 * (part + 1)]

        NBLK = 8
        for g in range(16):
            gp2 = g % 2
            for j in range(3):
                si = 3 * g + j

                def p7_w(gp, g=g, j=j, si=si):
                    if si >= 4:
                        kb.w(gp, "pe", ev[("p7smm", si - 4)])
                    ev[("p7w", si)] = [load_strip(gp, wt_qkv, si, slot4(si),
                                                  "wS%d" % (si % 4))]
                block.gpsimd(p7_w)

                for ci, (c0, ln) in enumerate(CH_L1):
                    def p7_smm(tensor, g=g, j=j, si=si, ci=ci, c0=c0, ln=ln):
                        if ci == 0:
                            for sm, v in ev[("p7w", si)]:
                                kb.w(tensor, sm, v)
                        idx = si * 3 + ci
                        if idx >= 2:
                            kb.w(tensor, "ve", ev[("p7sev", idx - 2)])
                        bank = (pg, ph)[idx % 2]
                        for kt in range(16):
                            last = tensor.matmul(bank[:, 0:ln],
                                                 wtile(slot4(si), kt),
                                                 xb1(kt, c0, ln),
                                                 start=(kt == 0), stop=(kt == 15))
                        kb.inc(last, "pe", 1)
                        ev[("p7smmi", idx)] = kb.cnt["pe"]
                        if ci == 2:
                            ev[("p7smm", si)] = kb.cnt["pe"]
                    block.tensor(p7_smm)

                    def p7_sev(vector, g=g, j=j, si=si, ci=ci, c0=c0, ln=ln):
                        idx = si * 3 + ci
                        kb.w(vector, "pe", ev[("p7smmi", idx)])
                        if ci == 0 and j == 0 and g >= 2:
                            kb.w(vector, "pe", ev[("p7gdone", g - 2)])
                        bank = (pg, ph)[idx % 2]
                        _li = vector.tensor_copy(qkv_buf(g % 2, j)[:, c0:c0 + ln],
                                           bank[:, 0:ln])
                        kb.inc(_li, "ve", 1)
                        ev[("p7sev", idx)] = kb.cnt["ve"]
                    block.vector(p7_sev)

            for hh in range(2):
                hgl = 2 * g + hh
                hp = hgl % 2
                for tt in range(9):
                    qt = 128 if tt < 8 else 16
                    vti = hgl * 9 + tt

                    def p7_vt(tensor, g=g, hh=hh, tt=tt, qt=qt, vti=vti):
                        if vti >= 2:
                            kb.w(tensor, "ve", ev[("p7vtv", vti - 2)])
                        if tt == 0:
                            kb.w(tensor, "ve", ev[("p7sev", (3 * g + 2) * 3 + 2)])
                        bank = (pt_a, pt_b)[vti % 2]
                        kb.inc(tensor.transpose(
                            bank[0:qt, 256:320],
                            qkv_buf(g % 2, 2)[64 * hh:64 * hh + 64,
                                              128 * tt:128 * tt + qt],
                            c_identb()[64 * hh:64 * hh + 64,
                                       64 * hh:64 * hh + 64]), "pe", 1)
                        ev[("p7vt", vti)] = kb.cnt["pe"]
                    block.tensor(p7_vt)

                    def p7_vtv(vector, hh=hh, hp=hp, tt=tt, qt=qt, vti=vti):
                        kb.w(vector, "pe", ev[("p7vt", vti)])
                        bank = (pt_a, pt_b)[vti % 2]
                        _li = vector.tensor_copy(vtok(hp, tt)[0:qt, :],
                                           bank[0:qt, 256:320])
                        kb.inc(_li, "ve", 1)
                        ev[("p7vtv", vti)] = kb.cnt["ve"]
                    block.vector(p7_vtv)

                for b in range(NBLK):
                    bi = hgl * NBLK + b
                    bp = bi % 2
                    mi = 0 if b == 0 else (2 if b == NBLK - 1 else 1)
                    mmask = AMASK[:, 288 * mi:288 * mi + 144]
                    amask = AMASK[:, 288 * mi + 144:288 * mi + 288]
                    tscr = Fa[0:128, 128 + 160 * bp:128 + 160 * bp + 144]
                    nmx = G[0:128, 960 + 4 * bp:961 + 4 * bp]
                    rsm = G[0:128, 961 + 4 * bp:962 + 4 * bp]
                    rnm = G[0:128, 962 + 4 * bp:963 + 4 * bp]
                    ptl = Hc[:, 384 + 144 * bp:384 + 144 * bp + 144]

                    def at_t1(tensor, g=g, hh=hh, b=b, bi=bi, bp=bp):
                        if bi >= 2:
                            kb.w(tensor, "ve", ev[("atV", bi - 2)])
                            kb.w(tensor, "gp", ev[("atGP2", bi - 2)])
                        kb.inc(tensor.matmul(
                            psW(bp)[0:128, 0:144],
                            qkv_buf(g % 2, 0)[64 * hh:64 * hh + 64,
                                              8 + 128 * b:8 + 128 * b + 128],
                            qkv_buf(g % 2, 1)[64 * hh:64 * hh + 64,
                                              128 * b:128 * b + 144],
                            start=True, stop=True), "pe", 1)
                        ev[("atT1", bi)] = kb.cnt["pe"]
                    block.tensor(at_t1)

                    def at_v1(vector, bi=bi, bp=bp, mmask=mmask, tscr=tscr):
                        kb.w(vector, "pe", ev[("atT1", bi)])
                        if bi >= 2:
                            kb.w(vector, "ac", ev[("atA1", bi - 2)])
                        _li = vector.tensor_mul(tscr, psW(bp)[0:128, 0:144], mmask)
                        kb.inc(_li, "ve", 1)
                        ev[("atV1", bi)] = kb.cnt["ve"]
                    block.vector(at_v1)

                    def at_gp1(gp, bi=bi, amask=amask, tscr=tscr):
                        kb.w(gp, "ve", ev[("atV1", bi)])
                        _li = gp.tensor_add(tscr, tscr, amask)
                        kb.inc(_li, "gp", 1)
                        ev[("atGP1", bi)] = kb.cnt["gp"]
                    block.gpsimd(at_gp1)

                    def at_v2(vector, bi=bi, tscr=tscr, nmx=nmx):
                        kb.w(vector, "gp", ev[("atGP1", bi)])
                        _li = vector.reduce_max(nmx, tscr, AX.X, negate=True)
                        kb.inc(_li, "ve", 1)
                        ev[("atV2", bi)] = kb.cnt["ve"]
                    block.vector(at_v2)

                    def at_a1(scalar, bi=bi, bp=bp, tscr=tscr, nmx=nmx, rsm=rsm,
                              ptl=ptl):
                        kb.w(scalar, "ve", ev[("atV2", bi)])
                        if bi >= 2:
                            kb.w(scalar, "gp", ev[("atGP2", bi - 2)])
                            kb.w(scalar, "pe", ev[("atT2", bi - 2)])
                            kb.w(scalar, "ve", ev[("atA2v", bi - 2)])
                        kb.inc(scalar.activation(ptl, tscr, AF.Exp, bias=nmx,
                                                 accum_out=rsm), "ac", 1)
                        ev[("atA1", bi)] = kb.cnt["ac"]
                    block.scalar(at_a1)

                    def at_v3(vector, bi=bi, rsm=rsm, rnm=rnm):
                        kb.w(vector, "ac", ev[("atA1", bi)])
                        _li = vector.reciprocal(rnm, rsm)
                        kb.inc(_li, "ve", 1)
                        ev[("atV3", bi)] = kb.cnt["ve"]
                    block.vector(at_v3)

                    def at_gp2(gp, bi=bi, ptl=ptl, rnm=rnm):
                        kb.w(gp, "ve", ev[("atV3", bi)])
                        _li = gp.tensor_scalar_mul(ptl, ptl, rnm)
                        kb.inc(_li, "gp", 1)
                        ev[("atGP2", bi)] = kb.cnt["gp"]
                    block.gpsimd(at_gp2)

                    def at_t2(tensor, hgl=hgl, b=b, bi=bi, bp=bp, ptl=ptl):
                        kb.w(tensor, "gp", ev[("atGP2", bi)])
                        if b == 0:
                            kb.w(tensor, "ve", ev[("p7vtv", hgl * 9 + 8)])
                        if bi >= 2:
                            kb.w(tensor, "ve", ev[("atA2v", bi - 2)])
                        tensor.transpose(psT(bp)[0:128, 0:128], ptl[:, 0:128],
                                         c_identb())
                        kb.inc(tensor.transpose(psT(bp)[0:16, 128:256],
                                                ptl[:, 128:144], c_identb()),
                               "pe", 1)
                        ev[("atT2", bi)] = kb.cnt["pe"]
                    block.tensor(at_t2)

                    def at_a2(vector, bi=bi, bp=bp):
                        kb.w(vector, "pe", ev[("atT2", bi)])
                        if bi >= 2:
                            kb.w(vector, "pe", ev[("atT3", bi - 2)])
                        vector.tensor_copy(ptile(bp, 0), psT(bp)[0:128, 0:128])
                        _li = vector.tensor_copy(ptile(bp, 1)[0:16, :],
                                           psT(bp)[0:16, 128:256])
                        kb.inc(_li, "ve", 1)
                        ev[("atA2v", bi)] = kb.cnt["ve"]
                    block.vector(at_a2)

                    def at_t3(tensor, hh=hh, hp=hp, b=b, bi=bi, bp=bp):
                        kb.w(tensor, "ve", ev[("atA2v", bi)])
                        tensor.matmul(psW(bp)[0:64, 256:384],
                                      vtok(hp, b), ptile(bp, 0),
                                      start=True, stop=False)
                        kb.inc(tensor.matmul(psW(bp)[0:64, 256:384],
                                             vtok(hp, b + 1)[0:16, :],
                                             ptile(bp, 1)[0:16, :],
                                             start=False, stop=True), "pe", 1)
                        ev[("atT3", bi)] = kb.cnt["pe"]
                    block.tensor(at_t3)

                    def at_v4(vector, g=g, hh=hh, hgl=hgl, b=b, bi=bi, bp=bp):
                        kb.w(vector, "pe", ev[("atT3", bi)])
                        dst = Dn[64 * (hgl % 2):64 * (hgl % 2) + 64,
                                 1024 * (hgl // 2) + 128 * b:
                                 1024 * (hgl // 2) + 128 * b + 128]
                        _li = vector.tensor_copy(dst, psW(bp)[0:64, 256:384])
                        kb.inc(_li, "ve", 1)
                        ev[("atV", bi)] = kb.cnt["ve"]
                    block.vector(at_v4)

            ev[("p7gdone", g)] = kb.cnt["pe"]
        fence()

        CH_PJ = [(0, 512), (512, 512)]
        for ot in range(16):
            def pj_w(gp, ot=ot):
                if ot >= 2:
                    kb.w(gp, "pe", ev[("pjmm", ot - 2)])
                ev[("pjw", ot)] = [load_strip(gp, wt_proj, ot, eslot2(ot),
                                              "wF%d" % (ot % 2))]
            block.gpsimd(pj_w)

            for ci, (c0, ln) in enumerate(CH_PJ):
                idx = ot * 2 + ci

                def pj_mm(tensor, ot=ot, ci=ci, c0=c0, ln=ln, idx=idx):
                    if ci == 0:
                        for sm, v in ev[("pjw", ot)]:
                            kb.w(tensor, sm, v)
                    if idx >= 2:
                        kb.w(tensor, "ve", ev[("pja", idx - 2)])
                    bank = (pa, pb)[idx % 2]
                    for kt in range(16):
                        last = tensor.matmul(bank[:, 0:ln], wtile(eslot2(ot), kt),
                                             Dn[:, 1024 * kt + c0:1024 * kt + c0 + ln],
                                             start=(kt == 0), stop=(kt == 15))
                    kb.inc(last, "pe", 1)
                    ev[("pjmmi", idx)] = kb.cnt["pe"]
                    if ci == 1:
                        ev[("pjmm", ot)] = kb.cnt["pe"]
                block.tensor(pj_mm)

                def pj_a(vector, ot=ot, ci=ci, c0=c0, ln=ln, idx=idx):
                    kb.w(vector, "pe", ev[("pjmmi", idx)])
                    bank = (pa, pb)[idx % 2]
                    xd_ = x_ext(ot, 8 + c0, ln)
                    _li = vector.scalar_tensor_tensor(xd_, bank[:, 0:ln],
                                                PRB[:, ot:ot + 1], xd_,
                                                ALU.add, ALU.add)
                    kb.inc(_li, "ve", 1)
                    ev[("pja", idx)] = kb.cnt["ve"]
                block.vector(pj_a)
        fence()

        if DBG == 6:
            def pdbg6(sync):
                for sname in FSEMS:
                    kb.w(sync, sname)
                kb.inc(sync.dma_start(out=dbg[:, 0:NFT * TE],
                                      in_=A[:, 0:NFT * TE]), "st0", 16)
                kb.w(sync, "st0")
            block.sync(pdbg6)

        def p8_rw(gp):
            src = wt_router.rearrange("(kt p) o -> p kt o", p=128)
            kb.inc(gp.dma_start(
                out=Hc[:, 384:384 + 128].rearrange("p (kt o) -> p kt o", o=8),
                in_=src[:, :, 0:8]), "wH", 16)
            ev["p8rw"] = kb.cnt["wH"]
        block.gpsimd(p8_rw)

        for ci, (c0, ln) in enumerate(CH_PJ):
            def p8_rmm(tensor, ci=ci, c0=c0, ln=ln):
                if ci == 0:
                    kb.w(tensor, "wH", ev["p8rw"])
                bank = (pc, pd)[ci % 2]
                for kt in range(16):
                    last = tensor.matmul(bank[0:8, 0:ln],
                                         Hc[:, 384 + 8 * kt:384 + 8 * (kt + 1)],
                                         xb1(kt, 8 + c0, ln),
                                         start=(kt == 0), stop=(kt == 15))
                kb.inc(last, "pe", 1)
                ev[("p8rmm", ci)] = kb.cnt["pe"]
            block.tensor(p8_rmm)

            def p8_rev(vector, ci=ci, c0=c0, ln=ln):
                kb.w(vector, "pe", ev[("p8rmm", ci)])
                bank = (pc, pd)[ci % 2]
                _li = vector.tensor_copy(G[0:8, c0:c0 + ln], bank[0:8, 0:ln])
                kb.inc(_li, "ve", 1)
                ev[("p8rev", ci)] = kb.cnt["ve"]
            block.vector(p8_rev)
        fence()

        for tt in range(NBLK):
            def p8_lt(tensor, tt=tt):
                if tt >= 2:
                    kb.w(tensor, "ve", ev[("p8ltv", tt - 2)])
                bank = (pc, pd)[tt % 2]
                kb.inc(tensor.transpose(bank[0:128, 134:142],
                                        G[0:8, 128 * tt:128 * (tt + 1)],
                                        c_identf()[0:8, 0:8]), "pe", 1)
                ev[("p8lt", tt)] = kb.cnt["pe"]
            block.tensor(p8_lt)

            def p8_ltv(vector, tt=tt):
                kb.w(vector, "pe", ev[("p8lt", tt)])
                bank = (pc, pd)[tt % 2]
                _li = vector.tensor_copy(Fa[0:128, 8 * tt:8 * (tt + 1)],
                                   bank[0:128, 134:142])
                kb.inc(_li, "ve", 1)
                ev[("p8ltv", tt)] = kb.cnt["ve"]
            block.vector(p8_ltv)
        fence()

        PRB_ = None
        FA_PROB = 64
        FA_CW = 128
        GR = 876

        def p8_sm1(vector):
            for tt in range(NBLK):
                _li = vector.reduce_max(G[:, GR + 16 * 0 + tt:GR + 16 * 0 + tt + 1],
                                  Fa[:, 8 * tt:8 * (tt + 1)], AX.X, negate=True)
            kb.inc(_li, "ve", 1)
            ev["p8sm1"] = kb.cnt["ve"]
        block.vector(p8_sm1)

        def p8_sm2(scalar):
            kb.w(scalar, "ve", ev["p8sm1"])
            last = None
            for tt in range(NBLK):
                last = scalar.activation(
                    Fa[:, FA_PROB + 8 * tt:FA_PROB + 8 * (tt + 1)],
                    Fa[:, 8 * tt:8 * (tt + 1)], AF.Exp,
                    bias=G[:, GR + tt:GR + tt + 1],
                    accum_out=G[:, GR + 16 + tt:GR + 16 + tt + 1])
            kb.inc(last, "ac", 1)
            ev["p8sm2"] = kb.cnt["ac"]
        block.scalar(p8_sm2)

        def p8_sm3(vector):
            kb.w(vector, "ac", ev["p8sm2"])
            for tt in range(NBLK):
                _li = vector.reduce_max(G[:, GR + 32 + tt:GR + 32 + tt + 1],
                                  Fa[:, FA_PROB + 8 * tt:FA_PROB + 8 * (tt + 1)],
                                  AX.X)
            kb.inc(_li, "ve", 1)
            ev["p8sm3"] = kb.cnt["ve"]
        block.vector(p8_sm3)

        def p8_sm4(gp):
            kb.w(gp, "ve", ev["p8sm3"])
            for tt in range(NBLK):
                _li = gp.tensor_scalar(Fa[:, FA_CW + 8 * tt:FA_CW + 8 * (tt + 1)],
                                 Fa[:, FA_PROB + 8 * tt:FA_PROB + 8 * (tt + 1)],
                                 G[:, GR + 32 + tt:GR + 32 + tt + 1], None,
                                 ALU.is_equal)
            kb.inc(_li, "gp", 1)
            ev["p8sm4"] = kb.cnt["gp"]
        block.gpsimd(p8_sm4)

        def p8_sm5(vector):
            kb.w(vector, "gp", ev["p8sm4"])
            for tt in range(NBLK):
                vector.scalar_tensor_tensor(
                    G[0:128, 128 + 8 * tt:128 + 8 * (tt + 1)],
                    Fa[:, FA_CW + 8 * tt:FA_CW + 8 * (tt + 1)], -1e30,
                    Fa[:, FA_PROB + 8 * tt:FA_PROB + 8 * (tt + 1)],
                    ALU.mult, ALU.add)
            for tt in range(NBLK):
                vector.reduce_max(G[:, GR + 48 + tt:GR + 48 + tt + 1],
                                  G[0:128, 128 + 8 * tt:128 + 8 * (tt + 1)], AX.X)
            for tt in range(NBLK):
                _li = vector.scalar_tensor_tensor(
                    G[:, GR + 64 + tt:GR + 64 + tt + 1],
                    G[:, GR + 48 + tt:GR + 48 + tt + 1], 1e-20,
                    G[:, GR + 32 + tt:GR + 32 + tt + 1], ALU.add, ALU.add)
            kb.inc(_li, "ve", 1)
            ev["p8sm5"] = kb.cnt["ve"]
        block.vector(p8_sm5)

        def p8_sm6(gp):
            kb.w(gp, "ve", ev["p8sm5"])
            for tt in range(NBLK):
                _li = gp.tensor_scalar(G[0:128, 192 + 8 * tt:192 + 8 * (tt + 1)],
                                 G[0:128, 128 + 8 * tt:128 + 8 * (tt + 1)],
                                 G[:, GR + 48 + tt:GR + 48 + tt + 1], None,
                                 ALU.is_equal)
            kb.inc(_li, "gp", 1)
            ev["p8sm6"] = kb.cnt["gp"]
        block.gpsimd(p8_sm6)

        def p8_sm7(vector):
            kb.w(vector, "gp", ev["p8sm6"])
            for tt in range(NBLK):
                vector.tensor_add(Fa[:, FA_CW + 8 * tt:FA_CW + 8 * (tt + 1)],
                                  Fa[:, FA_CW + 8 * tt:FA_CW + 8 * (tt + 1)],
                                  G[0:128, 192 + 8 * tt:192 + 8 * (tt + 1)])
            for tt in range(NBLK):
                vector.reciprocal(G[:, GR + 80 + tt:GR + 80 + tt + 1],
                                  G[:, GR + 64 + tt:GR + 64 + tt + 1])
            for tt in range(NBLK):
                _li = vector.tensor_mul(Fa[:, FA_CW + 8 * tt:FA_CW + 8 * (tt + 1)],
                                  Fa[:, FA_CW + 8 * tt:FA_CW + 8 * (tt + 1)],
                                  Fa[:, FA_PROB + 8 * tt:FA_PROB + 8 * (tt + 1)])
            kb.inc(_li, "ve", 1)
            ev["p8sm7"] = kb.cnt["ve"]
        block.vector(p8_sm7)

        def p8_sm8(gp):
            kb.w(gp, "ve", ev["p8sm7"])
            for tt in range(NBLK):
                _li = gp.tensor_scalar_mul(Fa[:, FA_CW + 8 * tt:FA_CW + 8 * (tt + 1)],
                                     Fa[:, FA_CW + 8 * tt:FA_CW + 8 * (tt + 1)],
                                     G[:, GR + 80 + tt:GR + 80 + tt + 1])
            kb.inc(_li, "gp", 1)
            ev["p8sm8"] = kb.cnt["gp"]
        block.gpsimd(p8_sm8)

        for tt in range(NBLK):
            def p8_ct(tensor, tt=tt):
                if tt == 0:
                    kb.w(tensor, "gp", ev["p8sm8"])
                if tt >= 2:
                    kb.w(tensor, "ve", ev[("p8ctv", tt - 2)])
                bank = (pc, pd)[tt % 2]
                kb.inc(tensor.transpose(bank[0:8, 144:272],
                                        Fa[:, FA_CW + 8 * tt:FA_CW + 8 * (tt + 1)],
                                        c_identf()), "pe", 1)
                ev[("p8ct", tt)] = kb.cnt["pe"]
            block.tensor(p8_ct)

            def p8_ctv(vector, tt=tt):
                kb.w(vector, "pe", ev[("p8ct", tt)])
                bank = (pc, pd)[tt % 2]
                _li = vector.tensor_copy(G[0:8, 128 * tt:128 * (tt + 1)],
                                   bank[0:8, 144:272])
                kb.inc(_li, "ve", 1)
                ev[("p8ctv", tt)] = kb.cnt["ve"]
            block.vector(p8_ctv)
        fence()

        def moe_h(ot, c0, ln):
            return C[:, 1024 * ot + c0:1024 * ot + c0 + ln]

        for e in range(10):
            routed = e < 8
            if routed:
                def m8_ohs(sync, e=e):
                    if e >= 2:
                        kb.w(sync, "pe", ev[("m8bc", e - 2, 1)])
                    sem = "s8%d" % (e % 2)
                    kb.inc(sync.dma_start(out=OH8s[:, 128 * (e % 2):128 * (e % 2) + 128],
                                          in_=oh8[:, 128 * e:128 * (e + 1)]),
                           sem, 16)
                    ev[("m8ohs", e)] = (sem, kb.cnt[sem])
                block.sync(m8_ohs)

                for ci in range(2):
                    def m8_bc(tensor, e=e, ci=ci):
                        if ci == 0:
                            kb.w(tensor, ev[("m8ohs", e)][0], ev[("m8ohs", e)][1])
                        if e >= 1:
                            kb.w(tensor, "gp",
                                 ev[("m8hgp", ((e - 1) * 11 + 10) * 2 + 1)])
                            kb.w(tensor, "ve",
                                 ev[("m8da", ((e - 1) * 16 + 15) * 2 + ci)])
                        bank = (pc, pd)[ci % 2]
                        kb.inc(tensor.matmul(bank[:, 0:512],
                                             OH8s[:, 128 * (e % 2):128 * (e % 2) + 128],
                                             G[0:8, 512 * ci:512 * (ci + 1)],
                                             start=True, stop=True), "pe", 1)
                        ev[("m8bc", e, ci)] = kb.cnt["pe"]
                    block.tensor(m8_bc)

                    def m8_bcv(vector, e=e, ci=ci):
                        kb.w(vector, "pe", ev[("m8bc", e, ci)])
                        bank = (pc, pd)[ci % 2]
                        _li = vector.tensor_copy(Fa[:, 512 * ci:512 * (ci + 1)],
                                           bank[:, 0:512])
                        kb.inc(_li, "ve", 1)
                        ev[("m8bcv", e, ci)] = kb.cnt["ve"]
                    block.vector(m8_bcv)
            else:
                def m8_ones(vector, e=e):
                    if e >= 1:
                        kb.w(vector, "gp",
                             ev[("m8hgp", ((e - 1) * 11 + 10) * 2 + 1)])
                    _li = vector.memset(Fa[:, 0:1024], 1.0)
                    kb.inc(_li, "ve", 1)
                    ev[("m8bcv", e, 1)] = kb.cnt["ve"]
                block.vector(m8_ones)

            nOT = 11
            for ot in range(nOT):
                gi = e * nOT + ot

                def m8_w(gp, e=e, ot=ot, gi=gi, routed=routed):
                    if gi >= 2:
                        kb.w(gp, "pe", ev[("m8u", (gi - 2) * 2 + 1)])
                    if ot < 2 and e >= 1:
                        kb.w(gp, "pe", ev[("m8d", ((e - 1) * 16 + 15) * 2 + 1)])
                    if routed:
                        e1 = load_strip(gp, wt_eg[e], ot, eslot(gi),
                                        "wE%d" % (gi % 2))
                        e2 = load_strip(gp, wt_eu[e], ot, eslot2(gi),
                                        "wF%d" % (gi % 2))
                    else:
                        e1 = load_strip(gp, wt_sg, 11 * (e - 8) + ot, eslot(gi),
                                        "wE%d" % (gi % 2))
                        e2 = load_strip(gp, wt_su, 11 * (e - 8) + ot, eslot2(gi),
                                        "wF%d" % (gi % 2))
                    ev[("m8w", gi)] = (e1, e2)
                block.gpsimd(m8_w)

                for ci in range(2):
                    idx = gi * 2 + ci

                    def m8_mm(tensor, e=e, ot=ot, gi=gi, ci=ci, idx=idx):
                        if ci == 0:
                            for sm, v in ev[("m8w", gi)]:
                                kb.w(tensor, sm, v)
                        if idx >= 2:
                            kb.w(tensor, "ac", ev[("m8sl", idx - 2)])
                        for kt in range(16):
                            last = tensor.matmul(
                                (pa, pb)[idx % 2][:, 0:512],
                                wtile(eslot(gi), kt),
                                xb1(kt, 8 + 512 * ci, 512),
                                start=(kt == 0), stop=(kt == 15))
                        kb.inc(last, "pe", 1)
                        ev[("m8g", idx)] = kb.cnt["pe"]
                        if idx >= 2:
                            kb.w(tensor, "gp", ev[("m8hgp", idx - 2)])
                        for kt in range(16):
                            last = tensor.matmul(
                                (pg, ph)[idx % 2][:, 0:512],
                                wtile(eslot2(gi), kt),
                                xb1(kt, 8 + 512 * ci, 512),
                                start=(kt == 0), stop=(kt == 15))
                        kb.inc(last, "pe", 1)
                        ev[("m8u", idx)] = kb.cnt["pe"]
                    block.tensor(m8_mm)

                    def m8_sl(scalar, e=e, ot=ot, ci=ci, idx=idx):
                        kb.w(scalar, "pe", ev[("m8g", idx)])
                        if ot == 0 and e >= 1:
                            kb.w(scalar, "pe",
                                 ev[("m8d", ((e - 1) * 16 + 15) * 2 + 1)])
                        if idx >= 2:
                            kb.w(scalar, "ve", ev[("m8hv", idx - 2)])
                            kb.w(scalar, "gp", ev[("m8hgp", idx - 2)])
                        kb.inc(scalar.activation(moe_h(ot, 512 * ci, 512),
                                                 (pa, pb)[idx % 2][:, 0:512],
                                                 AF.Silu), "ac", 1)
                        ev[("m8sl", idx)] = kb.cnt["ac"]
                    block.scalar(m8_sl)

                    def m8_hv(vector, ot=ot, ci=ci, idx=idx):
                        kb.w(vector, "pe", ev[("m8u", idx)])
                        kb.w(vector, "ac", ev[("m8sl", idx)])
                        hsl = moe_h(ot, 512 * ci, 512)
                        _li = vector.tensor_mul(hsl, hsl, (pg, ph)[idx % 2][:, 0:512])
                        kb.inc(_li, "ve", 1)
                        ev[("m8hv", idx)] = kb.cnt["ve"]
                    block.vector(m8_hv)

                    def m8_hgp(gp, e=e, ot=ot, ci=ci, idx=idx):
                        kb.w(gp, "ve", ev[("m8hv", idx)])
                        kb.w(gp, "ve", ev[("m8bcv", e, 1)])
                        hsl = moe_h(ot, 512 * ci, 512)
                        _li = gp.tensor_mul(hsl, hsl, Fa[:, 512 * ci:512 * (ci + 1)])
                        kb.inc(_li, "gp", 1)
                        ev[("m8hgp", idx)] = kb.cnt["gp"]
                    block.gpsimd(m8_hgp)

            for ot in range(16):
                di = e * 16 + ot

                def m8_dw(gp, e=e, ot=ot, di=di, routed=routed):
                    if di % 16 >= 2:
                        kb.w(gp, "pe", ev[("m8d", (di - 2) * 2 + 1)])
                    else:
                        kb.w(gp, "pe", ev[("m8u", (e * 11 + 10) * 2 + 1)])
                    wd_ap = wt_ed[e] if routed else wt_sd[e - 8]
                    ev[("m8dw", di)] = [load_strip(gp, wd_ap, ot, eslot(22 + di),
                                                   "wE%d" % ((22 + di) % 2),
                                                   kmax=11)]
                block.gpsimd(m8_dw)

                for ci in range(2):
                    idx = di * 2 + ci

                    def m8_dmm(tensor, e=e, ot=ot, di=di, ci=ci, idx=idx):
                        if ci == 0:
                            for sm, v in ev[("m8dw", di)]:
                                kb.w(tensor, sm, v)
                        if di % 16 * 2 + ci >= 2:
                            kb.w(tensor, "ve", ev[("m8da", idx - 2)])
                        elif e >= 1:
                            kb.w(tensor, "ve", ev[("m8da", (e - 1) * 32 + 30 + ci)])
                        bank = (pc, pd)[idx % 2]
                        for kt in range(11):
                            last = tensor.matmul(bank[:, 0:512],
                                                 wtile(eslot(22 + di), kt),
                                                 moe_h(kt, 512 * ci, 512),
                                                 start=(kt == 0), stop=(kt == 10))
                        kb.inc(last, "pe", 1)
                        ev[("m8d", idx)] = kb.cnt["pe"]
                    block.tensor(m8_dmm)

                    def m8_da(vector, ot=ot, ci=ci, idx=idx):
                        kb.w(vector, "pe", ev[("m8d", idx)])
                        bank = (pc, pd)[idx % 2]
                        xd_ = x_ext(ot, 8 + 512 * ci, 512)
                        _li = vector.tensor_add(xd_, xd_, bank[:, 0:512])
                        kb.inc(_li, "ve", 1)
                        ev[("m8da", idx)] = kb.cnt["ve"]
                    block.vector(m8_da)
        fence()

        if DBG:
            def pdbg(sync):
                for s in ("pe", "ve", "ac", "gp"):
                    kb.w(sync, s)
                kb.inc(sync.dma_start(out=dbg[:, 0:NFT * TE],
                                      in_=A[:, 0:NFT * TE]), "st0", 16)
                kb.inc(sync.dma_start(out=dbg[:, NFT * TE:NFT * TE + NFT * TW // 2],
                                      in_=B[:, :].bitcast(F32)), "st0", 16)
                kb.inc(sync.dma_start(out=dbg[:, 27008:27008 + 2 * TW],
                                      in_=R1[:, :]), "st0", 16)
                kb.inc(sync.dma_start(out=dbg[:, 29700:29700 + 1024],
                                      in_=Fa[:, :]), "st0", 16)
                kb.w(sync, "st0")
            block.sync(pdbg)

        fence()
        for tt in range(8):
            for fg in range(4):
                def pout_t(tensor, tt=tt, fg=fg):
                    bank = (pc, pd)[(tt * 4 + fg) % 2]
                    if tt * 4 + fg >= 2:
                        kb.w(tensor, "ve", ev[("povg", tt * 4 + fg - 2)])
                    last = None
                    for f2 in range(4):
                        ft = 4 * fg + f2
                        last = tensor.transpose(
                            bank[0:128, 128 * f2:128 * (f2 + 1)],
                            x_ext(ft, HALO + 128 * tt, 128), c_identf())
                    kb.inc(last, "pe", 1)
                    ev[("pot", tt * 4 + fg)] = kb.cnt["pe"]
                block.tensor(pout_t)

                def pout_v(vector, tt=tt, fg=fg):
                    if fg == 0 and tt >= 2:
                        kb.w(vector, ev[("pos", tt - 2)][0], ev[("pos", tt - 2)][1])
                    kb.w(vector, "pe", ev[("pot", tt * 4 + fg)])
                    bank = (pc, pd)[(tt * 4 + fg) % 2]
                    for f2 in range(4):
                        ft = 4 * fg + f2
                        _li = vector.tensor_copy(
                            Dn32[:, 2048 * (tt % 2) + 128 * ft:
                                 2048 * (tt % 2) + 128 * (ft + 1)],
                            bank[0:128, 128 * f2:128 * (f2 + 1)])
                    kb.inc(_li, "ve", 1)
                    ev[("povg", tt * 4 + fg)] = kb.cnt["ve"]
                    if fg == 3:
                        ev[("pov", tt)] = kb.cnt["ve"]
                block.vector(pout_v)

            def pout_s(sync, tt=tt):
                kb.w(sync, "ve", ev[("pov", tt)])
                sem = "st%d" % (tt % 2)
                kb.inc(sync.dma_start(out=out[128 * tt:128 * (tt + 1), :],
                                      in_=Dn32[:, 2048 * (tt % 2):2048 * (tt % 2 + 1)]),
                       sem, 16)
                ev[("pos", tt)] = (sem, kb.cnt[sem])
                if tt == 7:
                    kb.w(sync, "st0")
                    kb.w(sync, "st1")
            block.sync(pout_s)

    kb.close()
    return kb


def kernel(**inputs):
    kb = build_program()
    shared = host_prep(inputs)
    in_maps = []
    for core in range(NCORES):
        m = dict(shared)
        m.update(host_percore(inputs, core))
        in_maps.append(m)
    res = run_bass_kernel_spmd(kb.nc, in_maps, core_ids=list(range(NCORES)))
    outs = [res.results[c]["out"] for c in range(NCORES)]
    full = np.concatenate(outs, axis=0).reshape(1, T_SEQ, D)
    if DBG:
        kernel._dbg = [res.results[c].get("dbg") for c in range(NCORES)]
    return full


# version 12 (final)
# speedup vs baseline: 1.0502x; 1.0502x over previous
import numpy as np
import ml_dtypes
import concourse.bass as bass
import concourse.mybir as mybir
from concourse.bass_utils import run_bass_kernel_spmd

F32 = mybir.dt.float32
BF16 = mybir.dt.bfloat16
AF = mybir.ActivationFunctionType
ALU = mybir.AluOpType
AX = mybir.AxisListType

NCORES = 8
D = 2048
NFT = 16
T_SEQ = 8192
OWN = 1024
HALO = 8
TE = OWN + 2 * HALO
W0 = 256
TW = W0 + TE
CHUNKS = [(i * 128, 128) for i in range(10)] + [(1280, 16)]
WARM_CHUNKS = 2
CH_W = [(0, 512), (512, 512), (1024, 272)]
CH_E = [(256, 512), (768, 512), (1280, 16)]
CH_OWN = [(264, 512), (776, 512)]

INTER = 5632
NQ = 4
QI = INTER // NQ
EPS = 1e-5

DBG = 0


def _pf32(a):
    return np.ascontiguousarray(a, dtype=np.float32)


def _sw(wt):
    K, O = wt.shape
    a = wt.reshape(K // 128, 128, O // 128, 128)
    return _pf32(a.transpose(2, 1, 0, 3).reshape(O // 128, 128, K * 128 // 128))


def host_prep(inp):
    out = {}
    W_in = np.asarray(inp["L0_attn_W_in"])
    WT = W_in.T
    out["wt_in_a"] = _pf32(WT[:, 8192:8224])
    cols = []
    for g in range(16):
        for base in (6144, 4096, 0):
            for h in (2 * g, 2 * g + 1):
                cols.append(np.arange(base + 64 * h, base + 64 * h + 64))
    out["wt_in_cbx"] = _sw(_pf32(WT[:, np.concatenate(cols)]))
    out["wt_in_z"] = _pf32(WT[:, 2048:4096])

    out["wt_out"] = _sw(np.asarray(inp["L0_attn_W_out"]).T)
    out["wt_gate"] = _sw(np.asarray(inp["L0_ffn_gate_w"]).T)
    out["wt_up"] = _sw(np.asarray(inp["L0_ffn_up_w"]).T)
    dwn = np.asarray(inp["L0_ffn_down_w"]).T
    out["wt_down"] = _pf32(np.stack([_sw(dwn[1408 * q:1408 * (q + 1), :])
                                     for q in range(4)]))
    out["s_gate"] = _pf32(np.asarray(inp["L0_ffn_gate_s"]).reshape(44, 128).T)
    out["s_up"] = _pf32(np.asarray(inp["L0_ffn_up_s"]).reshape(44, 128).T)
    out["s_down"] = _pf32(np.asarray(inp["L0_ffn_down_s"]).reshape(16, 128).T)
    out["gamma"] = _pf32(np.asarray(inp["L0_attn_gamma"]).reshape(1, 2048))
    out["beta"] = _pf32(np.asarray(inp["L0_attn_beta"]).reshape(1, 2048))
    out["dmat"] = _pf32(np.asarray(inp["L0_attn_D"]))

    qkv = np.asarray(inp["L1_attn_qkv_w"]).T
    cols = []
    for g in range(16):
        for base in (0, 2048, 4096):
            for h in (2 * g, 2 * g + 1):
                cols.append(np.arange(base + 64 * h, base + 64 * h + 64))
    out["wt_qkv"] = _sw(_pf32(qkv[:, np.concatenate(cols)]))
    out["wt_proj"] = _sw(np.asarray(inp["L1_attn_proj_w"]).T)
    out["proj_b"] = _pf32(np.asarray(inp["L1_attn_proj_b"]).reshape(16, 128).T)
    out["wt_router"] = _pf32(np.asarray(inp["L1_ffn_gate_weight"]).T)
    out["wt_eg"] = _pf32(np.stack([_sw(np.asarray(inp["L1_ffn_w_gate"])[e].T)
                                   for e in range(8)]))
    out["wt_eu"] = _pf32(np.stack([_sw(np.asarray(inp["L1_ffn_w_up"])[e].T)
                                   for e in range(8)]))
    out["wt_ed"] = _pf32(np.stack([_sw(np.asarray(inp["L1_ffn_w_down"])[e].T)
                                   for e in range(8)]))
    out["wt_sg"] = _sw(np.asarray(inp["L1_ffn_sw_gate"]).T)
    out["wt_su"] = _sw(np.asarray(inp["L1_ffn_sw_up"]).T)
    sdn = np.asarray(inp["L1_ffn_sw_down"]).T
    out["wt_sd"] = _pf32(np.stack([_sw(sdn[1408 * h:1408 * (h + 1), :])
                                   for h in range(2)]))

    out["ident_f"] = _pf32(np.eye(128))
    out["ident_b"] = np.eye(128).astype(ml_dtypes.bfloat16)
    jj, ii = np.meshgrid(np.arange(128), np.arange(128), indexing="ij")
    out["neglt"] = _pf32(-(jj <= ii).astype(np.float32))
    out["lt_b"] = (jj <= ii).astype(ml_dtypes.bfloat16)
    sel = np.zeros((128, 256), np.float32)
    sel[127, 0:128] = 1.0
    sel[15, 128:256] = 1.0
    out["sel_last"] = sel
    oh32 = np.zeros((32, 4096), np.float32)
    for h in range(32):
        oh32[h, 128 * h:128 * h + 128] = 1.0
    out["oh32"] = oh32
    oh8 = np.zeros((8, 1024), np.float32)
    for e in range(8):
        oh8[e, 128 * e:128 * e + 128] = 1.0
    out["oh8"] = oh8
    oh2 = np.zeros((2, 256), np.float32)
    oh2[0, 0:128] = 1.0
    oh2[1, 128:256] = 1.0
    out["oh2"] = oh2
    return out


def host_percore(inp, core):
    x0 = np.asarray(inp["L0_attn_x"]).reshape(T_SEQ, D)
    start = 1024 * core - HALO - W0
    pos = start + np.arange(TW)
    xl = _pf32(x0[np.clip(pos, 0, T_SEQ - 1)])
    xl[(pos < 0) | (pos >= T_SEQ)] = 0.0

    scale = 64 ** -0.5
    r = np.arange(128)[:, None]
    c = np.arange(144)[None, :]
    dd = np.abs(r + 8 - c).astype(np.float32)
    band = (c >= r) & (c < r + 16)
    mmul = np.where(band, scale * (8.0 - dd / 2.0) / 8.0, 0.0).astype(ml_dtypes.bfloat16)
    madd = np.where(band, 0.0, -1e30).astype(ml_dtypes.bfloat16)
    mm_first, ma_first = mmul.copy(), madd.copy()
    mm_last, ma_last = mmul.copy(), madd.copy()
    if core == 0:
        mm_first[:, :8] = 0.0
        ma_first[:, :8] = -1e30
    if core == NCORES - 1:
        mm_last[:, 136:] = 0.0
        ma_last[:, 136:] = -1e30
    return {
        "x_loc": xl,
        "mm_first": mm_first, "ma_first": ma_first,
        "mm_mid": mmul, "ma_mid": madd,
        "mm_last": mm_last, "ma_last": ma_last,
    }


class KB:
    def __init__(self):
        self.nc = bass.Bass("TRN2", target_bir_lowering=False)
        self.cnt = {}
        self.sems = {}
        self.floor = {}
        self._ctx = []

    def sem(self, name):
        h = self.nc.semaphore(name)
        sem = h.__enter__()
        self._ctx.append(h)
        self.sems[name] = sem
        self.cnt[name] = 0
        return sem

    def sb(self, name, shape, dtype):
        h = self.nc.sbuf_tensor(name, shape, dtype)
        t = h.__enter__()
        self._ctx.append(h)
        return t

    def ps(self, name, shape, dtype):
        h = self.nc.psum_tensor(name, shape, dtype)
        t = h.__enter__()
        self._ctx.append(h)
        return t

    def inc(self, instr, name, k=1):
        instr.then_inc(self.sems[name], k)
        self.cnt[name] += k
        return self.cnt[name]

    def w(self, eng, name, val=None):
        v = self.cnt[name] if val is None else val
        if v <= 0:
            return
        key = (type(eng).__name__, name)
        if self.floor.get(key, -1) >= v:
            return
        self.floor[key] = v
        eng.wait_ge(self.sems[name], v)

    def close(self):
        for h in reversed(self._ctx):
            h.__exit__(None, None, None)


def build_program():
    kb = KB()
    nc = kb.nc

    def ein(name, shape, dt=F32):
        return nc.dram_tensor(name, shape, dt, kind="ExternalInput").ap()

    x_loc = ein("x_loc", [TW, D])
    wt_in_a = ein("wt_in_a", [D, 32])
    wt_in_cbx = ein("wt_in_cbx", [48, 128, 2048])
    wt_in_z = ein("wt_in_z", [D, D])
    wt_out = ein("wt_out", [16, 128, 2048])
    wt_gate = ein("wt_gate", [44, 128, 2048])
    wt_up = ein("wt_up", [44, 128, 2048])
    wt_down = ein("wt_down", [4, 16, 128, 1408])
    s_gate = ein("s_gate", [128, 44])
    s_up = ein("s_up", [128, 44])
    s_down = ein("s_down", [128, 16])
    gamma = ein("gamma", [1, D])
    beta = ein("beta", [1, D])
    dmat = ein("dmat", [32, 64])
    wt_qkv = ein("wt_qkv", [48, 128, 2048])
    wt_proj = ein("wt_proj", [16, 128, 2048])
    proj_b = ein("proj_b", [128, 16])
    wt_router = ein("wt_router", [D, 8])
    wt_eg = ein("wt_eg", [8, 11, 128, 2048])
    wt_eu = ein("wt_eu", [8, 11, 128, 2048])
    wt_ed = ein("wt_ed", [8, 16, 128, 1408])
    wt_sg = ein("wt_sg", [22, 128, 2048])
    wt_su = ein("wt_su", [22, 128, 2048])
    wt_sd = ein("wt_sd", [2, 16, 128, 1408])
    ident_f = ein("ident_f", [128, 128])
    ident_b = ein("ident_b", [128, 128], BF16)
    neglt = ein("neglt", [128, 128])
    lt_b = ein("lt_b", [128, 128], BF16)
    sel_last = ein("sel_last", [128, 256])
    oh32 = ein("oh32", [32, 4096])
    oh8 = ein("oh8", [8, 1024])
    oh2 = ein("oh2", [2, 256])
    am = {nm: ein(nm, [128, 144], BF16) for nm in
          ("mm_first", "ma_first", "mm_mid", "ma_mid", "mm_last", "ma_last")}

    out = nc.dram_tensor("out", [OWN, D], F32, kind="ExternalOutput").ap()
    dbg = None
    if DBG:
        dbg = nc.dram_tensor("dbg", [128, 31000], F32, kind="ExternalOutput").ap()

    A = kb.sb("A_x", [128, NFT * TE], F32)
    B = kb.sb("B_xn", [128, NFT * TW], BF16)
    C = kb.sb("C_scr", [128, 11712], BF16)
    Dn = kb.sb("D_y", [128, 18432], BF16)
    E = kb.sb("E_w", [128, 4096], BF16)
    E2 = kb.sb("E2_w", [128, 4096], BF16)
    Fa = kb.sb("F_bc", [128, 1024], F32)
    G = kb.sb("G_ms", [128, 1024], F32)
    Hc = kb.sb("H_cn", [128, 896], BF16)
    CNST = kb.sb("CNST", [128, 512], F32)
    OH32s = kb.sb("OH32s", [32, 256], F32)
    OH2 = kb.sb("OH2", [2, 256], F32)
    OH8s = kb.sb("OH8s", [8, 256], F32)
    AMASK = kb.sb("AMASK", [128, 864], BF16)
    DMT = kb.sb("DMT", [32, 64], F32)
    SCL = kb.sb("SCL", [128, 104], F32)
    PRB = kb.sb("PRB", [128, 16], F32)
    R1 = kb.sb("R1", [128, 2 * TW], F32)

    pa = kb.ps("pa", [128, 512], F32)
    pb = kb.ps("pb", [128, 512], F32)
    pc = kb.ps("pc", [128, 512], F32)
    pd = kb.ps("pd", [128, 512], F32)
    pg = kb.ps("pg", [128, 512], F32)
    ph = kb.ps("ph", [128, 512], F32)
    pt_a = kb.ps("pt_a", [128, 1024], BF16)
    pt_b = kb.ps("pt_b", [128, 1024], BF16)

    for s in ("sx0", "sx1", "sxc", "so0", "so1", "s80", "s81", "sxg",
              "wE0", "wE1", "wF0", "wF1", "wS0", "wS1", "wS2", "wS3",
              "wZ", "wH", "pe", "ve", "ac", "gp", "st0", "st1"):
        kb.sem(s)

    SUMS = R1[0:1, :]
    MEAN = R1[32:33, :]

    ev = {}

    def c_identf(): return CNST[:, 0:128]
    def c_neglt(): return CNST[:, 128:256]
    def c_sel(): return CNST[:, 256:384]
    def c_ones1f(p=0): return CNST[p:p + 1, 384:512]
    def c_onesbf(): return Hc[:, 0:1]
    def c_identb(): return Hc[:, 128:256]
    def c_ltb(): return Hc[:, 256:384]

    SRF = Fa[:, 512:576]
    SRB = C[:, 8032:8096]
    Dn32 = Dn[:, :].bitcast(F32)
    XW = C[:, :].bitcast(F32)
    GSB = Fa[:, :].bitcast(BF16)

    def x_ext(ft, c0, ln):
        return A[:, TE * ft + c0: TE * ft + c0 + ln]

    def x_warm(ft, c0, ln):
        return XW[:, W0 * ft + c0: W0 * ft + c0 + ln]

    def xb(ft, c0, ln):
        return B[:, TW * ft + c0: TW * ft + c0 + ln]

    def xslot(par):
        return Dn32[:, 2048 * par:2048 * (par + 1)]

    with nc.Block() as block:
        FSEMS = ("pe", "ve", "ac", "gp", "st0", "st1", "sx0", "sx1", "sxc",
                 "so0", "so1", "s80", "s81", "sxg", "wE0", "wE1", "wF0",
                 "wF1", "wS0", "wS1", "wS2", "wS3", "wZ", "wH")

        def fence(engines=("tensor", "vector", "scalar", "gpsimd", "sync")):
            def vset(vector):
                vector.tensor_copy(G[0:1, 1020:1021], G[0:1, 1021:1022])
                kb.inc(vector.tensor_copy(G[0:1, 1020:1021],
                                          G[0:1, 1021:1022]), "ve", 1)
            block.vector(vset)

            def gset(gp):
                gp.tensor_copy(G[0:1, 1021:1022], G[0:1, 1022:1023])
                kb.inc(gp.tensor_copy(G[0:1, 1021:1022],
                                      G[0:1, 1022:1023]), "gp", 1)
            block.gpsimd(gset)

            def mk(eng):
                for s in FSEMS:
                    kb.w(eng, s)
            for en in engines:
                getattr(block, en)(mk)

        def p0(sync):
            for dst, src in ((CNST[:, 0:128], ident_f),
                             (CNST[:, 128:256], neglt),
                             (CNST[:, 256:384], sel_last[:, 0:128]),
                             (G[:, 748:876], sel_last[:, 128:256]),
                             (DMT[:, :], dmat), (OH2[:, :], oh2),
                             (SCL[:, 0:44], s_gate), (SCL[:, 44:88], s_up),
                             (SCL[:, 88:104], s_down), (PRB[:, :], proj_b),
                             (Hc[:, 128:256], ident_b), (Hc[:, 256:384], lt_b)):
                kb.inc(sync.dma_start(out=dst, in_=src), "sxc", 16)
            for i, nm in enumerate(("mm_first", "ma_first", "mm_mid", "ma_mid",
                                    "mm_last", "ma_last")):
                kb.inc(sync.dma_start(out=AMASK[:, 144 * i:144 * (i + 1)],
                                      in_=am[nm]), "sxc", 16)
        block.sync(p0)

        def p0v(vector):
            vector.memset(CNST[:, 384:512], 1.0)
            vector.memset(G[:, 1023:1024], EPS)
            _li = vector.memset(Hc[:, 0:1], 1.0)
            kb.inc(_li, "ve", 1)
        block.vector(p0v)

        for tt in range(11):
            rows = 128 if tt < 10 else 16
            par = tt % 2

            def p0x_s(sync, tt=tt, rows=rows, par=par):
                if tt >= 2:
                    kb.w(sync, "pe", ev[("p0t", tt - 2)])
                sem = "sx%d" % par
                kb.inc(sync.dma_start(out=xslot(par)[0:rows, :],
                                      in_=x_loc[128 * tt:128 * tt + rows, :]),
                       sem, 16)
                ev[("p0s", tt)] = (sem, kb.cnt[sem])
            block.sync(p0x_s)

            for fg in range(4):
                def p0x_t(tensor, tt=tt, rows=rows, par=par, fg=fg):
                    if tt == 0 and fg == 0:
                        kb.w(tensor, "sxc")
                    if fg == 0:
                        kb.w(tensor, ev[("p0s", tt)][0], ev[("p0s", tt)][1])
                    bank = (pa, pb)[(tt * 4 + fg) % 2]
                    if tt * 4 + fg >= 2:
                        kb.w(tensor, "ve", ev[("p0v", tt * 4 + fg - 2)])
                    last = None
                    for f2 in range(4):
                        ft = 4 * fg + f2
                        last = tensor.transpose(
                            bank[0:128, 128 * f2:128 * f2 + rows],
                            xslot(par)[0:rows, 128 * ft:128 * (ft + 1)],
                            c_identf()[0:rows, 0:rows])
                    kb.inc(last, "pe", 1)
                    ev[("p0t_g", tt * 4 + fg)] = kb.cnt["pe"]
                    if fg == 3:
                        ev[("p0t", tt)] = kb.cnt["pe"]
                block.tensor(p0x_t)

                def p0x_v(vector, tt=tt, rows=rows, fg=fg):
                    kb.w(vector, "pe", ev[("p0t_g", tt * 4 + fg)])
                    bank = (pa, pb)[(tt * 4 + fg) % 2]
                    for f2 in range(4):
                        ft = 4 * fg + f2
                        if tt < 2:
                            dstv = x_warm(ft, 128 * tt, rows)
                        else:
                            dstv = x_ext(ft, 128 * tt - W0, rows)
                        _li = vector.tensor_copy(dstv, bank[0:128, 128 * f2:128 * f2 + rows])
                    kb.inc(_li, "ve", 1)
                    ev[("p0v", tt * 4 + fg)] = kb.cnt["ve"]
                block.vector(p0x_v)

        fence()

        def p1_cast(vector):
            for ft in range(NFT):
                vector.tensor_copy(xb(ft, 0, W0), x_warm(ft, 0, W0))
                _li = vector.tensor_copy(xb(ft, W0, TE), x_ext(ft, 0, TE))
            kb.inc(_li, "ve", 1)
        block.vector(p1_cast)
        fence()

        for ci, (c0, ln) in enumerate(CH_W):
            for ft in range(NFT):
                def p1_sq(scalar, ci=ci, c0=c0, ln=ln, ft=ft):
                    if (ci, ft) > (0, 1):
                        pi, pf = (ci, ft - 2) if ft >= 2 else (ci - 1, NFT + ft - 2)
                        kb.w(scalar, "pe", ev[("p1sqmm", pi, pf)])
                    kb.inc(scalar.activation(
                        GSB[:, 512 * (ft % 2):512 * (ft % 2) + ln],
                        xb(ft, c0, ln), AF.Square), "ac", 1)
                    ev[("p1sq", ci, ft)] = kb.cnt["ac"]
                block.scalar(p1_sq)

                def p1_mm(tensor, ci=ci, c0=c0, ln=ln, ft=ft):
                    if ft == 0 and ci >= 1:
                        kb.w(tensor, "ve", ev[("p1ev", ci - 1)])
                    tensor.matmul(pg[0:1, 0:ln], c_onesbf(), xb(ft, c0, ln),
                                  start=(ft == 0), stop=(ft == NFT - 1))
                    kb.w(tensor, "ac", ev[("p1sq", ci, ft)])
                    kb.inc(tensor.matmul(ph[0:1, 0:ln], c_onesbf(),
                                         GSB[:, 512 * (ft % 2):512 * (ft % 2) + ln],
                                         start=(ft == 0), stop=(ft == NFT - 1)),
                           "pe", 1)
                    ev[("p1sqmm", ci, ft)] = kb.cnt["pe"]
                    if ft == NFT - 1:
                        ev[("p1mm", ci)] = kb.cnt["pe"]
                block.tensor(p1_mm)

            def p1_ev(vector, ci=ci, c0=c0, ln=ln):
                kb.w(vector, "pe", ev[("p1mm", ci)])
                vector.tensor_copy(SUMS[0:1, c0:c0 + ln], pg[0:1, 0:ln])
                _li = vector.tensor_copy(SUMS[0:1, TW + c0:TW + c0 + ln], ph[0:1, 0:ln])
                kb.inc(_li, "ve", 1)
                ev[("p1ev", ci)] = kb.cnt["ve"]
            block.vector(p1_ev)
        fence()

        def p1_rows(vector):
            vector.tensor_scalar_mul(MEAN[0:1, 0:TW], SUMS[0:1, 0:TW], 1.0 / D)
            vector.tensor_scalar_mul(SUMS[0:1, TW:], SUMS[0:1, TW:], 1.0 / D)
            vector.tensor_mul(SUMS[0:1, 0:TW], MEAN[0:1, 0:TW], MEAN[0:1, 0:TW])
            _li = vector.tensor_sub(SUMS[0:1, TW:], SUMS[0:1, TW:], SUMS[0:1, 0:TW])
            kb.inc(_li, "ve", 1)
        block.vector(p1_rows)
        fence(("scalar",))

        def p1_sqrt(scalar):
            kb.inc(scalar.activation(SUMS[0:1, TW:], SUMS[0:1, TW:], AF.Sqrt,
                                     bias=G[0:1, 1023:1024]), "ac", 1)
        block.scalar(p1_sqrt)
        fence(("vector",))

        def p1_rstd(vector):
            _li = vector.reciprocal(MEAN[0:1, TW:], SUMS[0:1, TW:])
            kb.inc(_li, "ve", 1)
        block.vector(p1_rstd)
        fence()

        def Mb(ln): return Fa[:, 0:ln]
        def Rb(ln): return Fa[:, 512:512 + ln]

        CH_A = [(0, 256), (256, 512), (768, 512), (1024, 272)]
        for ci, (c0, ln) in enumerate(CH_A):
            def p1_bc(tensor, ci=ci, c0=c0, ln=ln):
                if ci >= 1:
                    kb.w(tensor, "ve", ev[("p1ap", ci - 1)])
                kb.inc(tensor.matmul(pa[0:128, 0:ln], c_ones1f(32),
                                     MEAN[0:1, c0:c0 + ln],
                                     start=True, stop=True), "pe", 1)
                kb.inc(tensor.matmul(pb[0:128, 0:ln], c_ones1f(32),
                                     MEAN[0:1, TW + c0:TW + c0 + ln],
                                     start=True, stop=True), "pe", 1)
                ev[("p1bc", ci)] = kb.cnt["pe"]
            block.tensor(p1_bc)

            def p1_ap(vector, ci=ci, c0=c0, ln=ln):
                kb.w(vector, "pe", ev[("p1bc", ci)])
                vector.tensor_copy(Mb(ln), pa[0:128, 0:ln])
                vector.tensor_copy(Rb(ln), pb[0:128, 0:ln])
                for ft in range(NFT):
                    xsrc = x_warm(ft, c0, ln) if c0 == 0 else x_ext(ft, c0 - W0, ln)
                    vector.tensor_sub(G[:, 0:ln], xsrc, Mb(ln))
                    _li = vector.tensor_mul(xb(ft, c0, ln), G[:, 0:ln], Rb(ln))
                kb.inc(_li, "ve", 1)
                ev[("p1ap", ci)] = kb.cnt["ve"]
            block.vector(p1_ap)
        fence()

        def load_strip(gp, swz_ap, ot, slot_ap, sem, kmax=16):
            kb.inc(gp.dma_start(out=slot_ap[:, 0:kmax * 128],
                                in_=swz_ap[ot]), sem, 16)
            return (sem, kb.cnt[sem])

        def eslot(i):
            return E[:, 2048 * (i % 2):2048 * (i % 2) + 2048]

        def eslot2(i):
            return E2[:, 2048 * (i % 2):2048 * (i % 2) + 2048]

        def wtile(slot, kt, owid=128):
            return slot[:, owid * kt:owid * (kt + 1)]

        def hq(ot, c0, ln):
            return C[:, TE * ot + (c0 - W0): TE * ot + (c0 - W0) + ln]

        if DBG != 3:
            for q in range(NQ):
                for ot in range(11):
                    gi = q * 11 + ot

                    def p2_w(gp, q=q, ot=ot, gi=gi):
                        if gi >= 2:
                            kb.w(gp, "pe", ev[("p2u", 3 * (gi - 2) + 2)])
                        if ot < 2 and q >= 1:
                            kb.w(gp, "pe", ev[("p2d", 3 * (q * 16 - 1) + 2)])
                        e1 = load_strip(gp, wt_gate, gi, eslot(gi),
                                        "wE%d" % (gi % 2))
                        e2 = load_strip(gp, wt_up, gi, eslot2(gi),
                                        "wF%d" % (gi % 2))
                        ev[("p2w", gi)] = (e1, e2)
                    block.gpsimd(p2_w)

                    for ci, (c0, ln) in enumerate(CH_E):
                        idx = gi * 3 + ci

                        def p2_mm(tensor, q=q, ot=ot, gi=gi, ci=ci, c0=c0, ln=ln,
                                  idx=idx):
                            if ci == 0:
                                for sm, v in ev[("p2w", gi)]:
                                    kb.w(tensor, sm, v)
                            if idx >= 2:
                                kb.w(tensor, "ac", ev[("p2sl", idx - 2)])
                            for kt in range(16):
                                last = tensor.matmul(
                                    (pa, pb)[idx % 2][:, 0:ln],
                                    wtile(eslot(gi), kt), xb(kt, c0, ln),
                                    start=(kt == 0), stop=(kt == 15))
                            kb.inc(last, "pe", 1)
                            ev[("p2g", idx)] = kb.cnt["pe"]
                            if idx >= 2:
                                kb.w(tensor, "ve", ev[("p2h", idx - 2)])
                            for kt in range(16):
                                last = tensor.matmul(
                                    (pc, pd)[idx % 2][:, 0:ln],
                                    wtile(eslot2(gi), kt), xb(kt, c0, ln),
                                    start=(kt == 0), stop=(kt == 15))
                            kb.inc(last, "pe", 1)
                            ev[("p2u", idx)] = kb.cnt["pe"]
                        block.tensor(p2_mm)

                        def p2_silu(scalar, q=q, ot=ot, ci=ci, c0=c0, ln=ln, idx=idx):
                            kb.w(scalar, "pe", ev[("p2g", idx)])
                            if idx >= 2:
                                kb.w(scalar, "ve", ev[("p2h", idx - 2)])
                            kb.inc(scalar.activation(
                                GSB[:, 512 * (idx % 2):512 * (idx % 2) + ln],
                                (pa, pb)[idx % 2][:, 0:ln], AF.Silu,
                                scale=SCL[:, q * 11 + ot:q * 11 + ot + 1]), "ac", 1)
                            ev[("p2sl", idx)] = kb.cnt["ac"]
                        block.scalar(p2_silu)

                        def p2_h(vector, q=q, ot=ot, ci=ci, c0=c0, ln=ln, idx=idx):
                            kb.w(vector, "pe", ev[("p2u", idx)])
                            kb.w(vector, "ac", ev[("p2sl", idx)])
                            _li = vector.scalar_tensor_tensor(
                                hq(ot, c0, ln), (pc, pd)[idx % 2][:, 0:ln],
                                SCL[:, 44 + q * 11 + ot:44 + q * 11 + ot + 1],
                                GSB[:, 512 * (idx % 2):512 * (idx % 2) + ln],
                                ALU.mult, ALU.mult)
                            kb.inc(_li, "ve", 1)
                            ev[("p2h", idx)] = kb.cnt["ve"]
                        block.vector(p2_h)
                fence()

                for ot in range(16):
                    di = q * 16 + ot

                    def p2_dw(gp, q=q, ot=ot, di=di):
                        if di % 16 >= 2:
                            kb.w(gp, "pe", ev[("p2d", 3 * (di - 2) + 2)])
                        else:
                            kb.w(gp, "pe", ev[("p2u", 3 * (q * 11 + 10) + 2)])
                        ev[("p2dw", di)] = [load_strip(gp, wt_down[q], ot,
                                                       eslot(di),
                                                       "wE%d" % (di % 2),
                                                       kmax=11)]
                    block.gpsimd(p2_dw)

                    for ci, (c0, ln) in enumerate(CH_E):
                        idx = di * 3 + ci

                        def p2_dmm(tensor, q=q, ot=ot, di=di, ci=ci, c0=c0, ln=ln,
                                   idx=idx):
                            if ci == 0:
                                for sm, v in ev[("p2dw", di)]:
                                    kb.w(tensor, sm, v)
                            if di % 16 * 3 + ci >= 2:
                                kb.w(tensor, "ve", ev[("p2da", idx - 2)])
                            for kt in range(11):
                                last = tensor.matmul(
                                    (pa, pb)[idx % 2][:, 0:ln],
                                    wtile(eslot(di), kt), hq(kt, c0, ln),
                                    start=(kt == 0), stop=(kt == 10))
                            kb.inc(last, "pe", 1)
                            ev[("p2d", idx)] = kb.cnt["pe"]
                        block.tensor(p2_dmm)

                        def p2_dadd(vector, ot=ot, ci=ci, c0=c0, ln=ln, idx=idx):
                            kb.w(vector, "pe", ev[("p2d", idx)])
                            xd = x_ext(ot, c0 - W0, ln)
                            _li = vector.scalar_tensor_tensor(
                                xd, (pa, pb)[idx % 2][:, 0:ln],
                                SCL[:, 88 + ot:88 + ot + 1], xd,
                                ALU.mult, ALU.add)
                            kb.inc(_li, "ve", 1)
                            ev[("p2da", idx)] = kb.cnt["ve"]
                        block.vector(p2_dadd)
                fence()


        def awt(kt):
            return Hc[:, 384 + 32 * kt:384 + 32 * (kt + 1)]

        def p3_w(gp):
            src = wt_in_a.rearrange("(kt p) o -> p kt o", p=128)
            kb.inc(gp.dma_start(
                out=Hc[:, 384:896].rearrange("p (kt o) -> p kt o", o=32),
                in_=src[:, :, 0:32]), "wH", 16)
            ev["p3w"] = ("wH", kb.cnt["wH"])
        block.gpsimd(p3_w)

        for c, (tc0, qc) in enumerate(CHUNKS):
            def p3_mm(tensor, c=c, tc0=tc0, qc=qc):
                if c == 0:
                    kb.w(tensor, ev["p3w"][0], ev["p3w"][1])
                if c >= 2:
                    kb.w(tensor, "ac", ev[("p3ea", c - 2)])
                bank = (pg, ph)[c % 2]
                for kt in range(16):
                    last = tensor.matmul(bank[0:qc, 0:32], xb(kt, tc0, qc),
                                         awt(kt), start=(kt == 0), stop=(kt == 15))
                kb.inc(last, "pe", 1)
                ev[("p3mm", c)] = kb.cnt["pe"]
            block.tensor(p3_mm)

            def p3_ea(scalar, c=c, qc=qc):
                kb.w(scalar, "pe", ev[("p3mm", c)])
                bank = (pg, ph)[c % 2]
                kb.inc(scalar.activation(G[0:qc, 32 * c:32 * c + 32],
                                         bank[0:qc, 0:32], AF.Exp), "ac", 1)
                ev[("p3ea", c)] = kb.cnt["ac"]
            block.scalar(p3_ea)
        fence()

        GC_CAG = (352, 480)
        GC_NCG = (608, 610)
        GC_EAT = (612, 614)
        GC_DTB = (616, 618)
        GC_DBC = (620, 684)
        GC_SEL15 = 748
        FA_TCL = (0, 128)
        FA_XD = (256, 320)
        FA_Y2 = (384, 448)
        CBX0 = 0
        CBX1 = 3888
        C_XT = (7776, 7840)
        C_BH = (7904, 7968)

        def cbx(gp2, j):
            base = (CBX0, CBX1)[gp2] + 1296 * j
            return C[:, base:base + 1296]

        def slot4(si):
            sl = si % 4
            return (E, E, E2, E2)[sl][:, 2048 * (sl % 2):2048 * (sl % 2) + 2048]

        def etile(sp):
            return Hc[:, 384 + 128 * sp:384 + 128 * (sp + 1)]

        def wtile_m(sp):
            return Hc[:, 640 + 128 * sp:640 + 128 * (sp + 1)]

        def psW(sp):
            return (pa, pb)[sp]

        def psT(sp):
            return (pt_a, pt_b)[sp]

        def pcd(cp):
            return (pc, pd)[cp]

        NWARM = WARM_CHUNKS
        for g in range(16):
            gp2 = g % 2
            for j in range(3):
                si = 3 * g + j

                def p4_w(gp, g=g, j=j, si=si):
                    if si >= 4:
                        kb.w(gp, "pe", ev[("p4smm", si - 4)])
                    ev[("p4w", si)] = [load_strip(gp, wt_in_cbx, si, slot4(si),
                                                  "wS%d" % (si % 4))]
                block.gpsimd(p4_w)

                for ci, (c0, ln) in enumerate(CH_W):
                    def p4_smm(tensor, g=g, j=j, si=si, ci=ci, c0=c0, ln=ln):
                        if ci == 0:
                            for sm, v in ev[("p4w", si)]:
                                kb.w(tensor, sm, v)
                        idx = si * 3 + ci
                        if idx >= 2:
                            kb.w(tensor, "ve", ev[("p4sev", idx - 2)])
                        bank = (pg, ph)[idx % 2]
                        for kt in range(16):
                            last = tensor.matmul(bank[:, 0:ln],
                                                 wtile(slot4(si), kt), xb(kt, c0, ln),
                                                 start=(kt == 0), stop=(kt == 15))
                        kb.inc(last, "pe", 1)
                        ev[("p4smmi", idx)] = kb.cnt["pe"]
                        if ci == 2:
                            ev[("p4smm", si)] = kb.cnt["pe"]
                    block.tensor(p4_smm)

                    def p4_sev(vector, g=g, j=j, si=si, ci=ci, c0=c0, ln=ln):
                        idx = si * 3 + ci
                        kb.w(vector, "pe", ev[("p4smmi", idx)])
                        if ci == 0 and g >= 2:
                            kb.w(vector, "pe", ev[("p4gdone", g - 2)])
                        bank = (pg, ph)[idx % 2]
                        _li = vector.tensor_copy(cbx(g % 2, j)[:, c0:c0 + ln],
                                           bank[:, 0:ln])
                        kb.inc(_li, "ve", 1)
                        ev[("p4sev", idx)] = kb.cnt["ve"]
                    block.vector(p4_sev)

            def p4_sinit(vector, g=g):
                if g >= 1:
                    kb.w(vector, "pe", ev[("p4gdone", g - 1)])
                    kb.w(vector, "gp", ev[("stepG", g - 1, 2 * len(CHUNKS) - 1)])
                vector.memset(SRF[:, :], 0.0)
                _li = vector.memset(SRB[:, :], 0.0)
                kb.inc(_li, "ve", 1)
                ev[("p4si", g)] = kb.cnt["ve"]
            block.vector(p4_sinit)

            for hh in range(2):
                hgl = 2 * g + hh
                hp = hgl % 2

                def p4_ohs(sync, hgl=hgl, hp=hp):
                    if hgl >= 2:
                        kb.w(sync, "pe", ev[("dbcmm", hgl - 2)])
                    sem = "so%d" % hp
                    kb.inc(sync.dma_start(out=OH32s[:, 128 * hp:128 * (hp + 1)],
                                          in_=oh32[:, 128 * hgl:128 * (hgl + 1)]),
                           sem, 16)
                    ev[("ohs", hgl)] = (sem, kb.cnt[sem])
                block.sync(p4_ohs)

                def p4_dbc(tensor, g=g, hh=hh, hgl=hgl, hp=hp):
                    kb.w(tensor, ev[("ohs", hgl)][0], ev[("ohs", hgl)][1])
                    if hgl >= 2:
                        kb.w(tensor, "ve", ev[("dbcev", hgl - 2)])
                    bank = psW(hh)
                    kb.inc(tensor.matmul(bank[:, 448:512],
                                         OH32s[:, 128 * hp:128 * (hp + 1)],
                                         DMT[:, :], start=True, stop=True), "pe", 1)
                    ev[("dbcmm", hgl)] = kb.cnt["pe"]
                block.tensor(p4_dbc)

                def p4_dbcv(vector, hh=hh, hgl=hgl, hp=hp):
                    kb.w(vector, "pe", ev[("dbcmm", hgl)])
                    bank = psW(hh)
                    _li = vector.tensor_copy(G[:, GC_DBC[0] + 64 * hp:GC_DBC[0] + 64 * (hp + 1)],
                                       bank[:, 448:512])
                    kb.inc(_li, "ve", 1)
                    ev[("dbcev", hgl)] = kb.cnt["ve"]
                block.vector(p4_dbcv)

            def emit_prep(c):
                tc0, qc = CHUNKS[c]
                cp = c % 2
                sel_ap = (c_sel()[0:qc, 0:128] if qc == 128
                          else G[0:qc, GC_SEL15:GC_SEL15 + 128])
                def prep_t1(tensor, g=g, c=c, tc0=tc0, qc=qc, cp=cp):
                    if c >= 2:
                        kb.w(tensor, "ve", ev[("prepv2", g, c - 2)])
                        kb.w(tensor, "ve", ev[("stepV", g, 2 * (c - 2) + 1)])
                        kb.w(tensor, "gp", ev[("stepG", g, 2 * (c - 2) + 1)])
                    kb.inc(tensor.matmul(pcd(cp)[0:2, 0:qc],
                                         G[0:qc, 32 * c + 2 * g:32 * c + 2 * g + 2],
                                         c_neglt()[0:qc, 0:qc],
                                         start=True, stop=True), "pe", 1)
                    ev[("prept1", g, c)] = kb.cnt["pe"]
                block.tensor(prep_t1)

                def prep_v1(vector, g=g, c=c, qc=qc, cp=cp):
                    kb.w(vector, "pe", ev[("prept1", g, c)])
                    _li = vector.tensor_copy(G[0:2, GC_CAG[0] + 128 * cp:
                                         GC_CAG[0] + 128 * cp + qc],
                                       pcd(cp)[0:2, 0:qc])
                    kb.inc(_li, "ve", 1)
                    ev[("prepv1", g, c)] = kb.cnt["ve"]
                block.vector(prep_v1)

                def prep_t2(tensor, g=g, c=c, qc=qc, cp=cp):
                    kb.w(tensor, "ve", ev[("prepv1", g, c)])
                    kb.inc(tensor.transpose(
                        pcd(cp)[0:qc, 128:130],
                        G[0:2, GC_CAG[0] + 128 * cp:GC_CAG[0] + 128 * cp + qc],
                        c_identf()[0:2, 0:2]), "pe", 1)
                    ev[("prept2", g, c)] = kb.cnt["pe"]
                block.tensor(prep_t2)

                def prep_a1(scalar, g=g, c=c, qc=qc, cp=cp):
                    kb.w(scalar, "pe", ev[("prept2", g, c)])
                    if c >= 2:
                        kb.w(scalar, "ve", ev[("stepV", g, 2 * (c - 2) + 1)])
                    kb.inc(scalar.activation(
                        G[0:qc, GC_EAT[0] + 2 * cp:GC_EAT[0] + 2 * cp + 2],
                        pcd(cp)[0:qc, 128:130], AF.Exp), "ac", 1)
                    ev[("prepa1", g, c)] = kb.cnt["ac"]
                block.scalar(prep_a1)

                def prep_vn(vector, g=g, c=c, qc=qc, cp=cp):
                    kb.w(vector, "pe", ev[("prept2", g, c)])
                    if c >= 2:
                        kb.w(vector, "ac",
                             ev[("stA1", g, 2 * (c - 2) + 1)])
                    _li = vector.tensor_scalar_mul(
                        G[0:qc, GC_NCG[0] + 2 * cp:GC_NCG[0] + 2 * cp + 2],
                        pcd(cp)[0:qc, 128:130], -1.0)
                    kb.inc(_li, "ve", 1)
                    ev[("prepvn", g, c)] = kb.cnt["ve"]
                block.vector(prep_vn)

                def prep_t3(tensor, g=g, c=c, qc=qc, cp=cp, sel_ap=sel_ap):
                    kb.w(tensor, "ac", ev[("prepa1", g, c)])
                    kb.inc(tensor.matmul(
                        pcd(cp)[0:128, 132:134], sel_ap,
                        G[0:qc, GC_EAT[0] + 2 * cp:GC_EAT[0] + 2 * cp + 2],
                        start=True, stop=True), "pe", 1)
                    ev[("prept3", g, c)] = kb.cnt["pe"]
                block.tensor(prep_t3)

                def prep_v2(vector, g=g, c=c, cp=cp):
                    kb.w(vector, "pe", ev[("prept3", g, c)])
                    _li = vector.tensor_copy(G[:, GC_DTB[0] + 2 * cp:GC_DTB[0] + 2 * cp + 2],
                                       pcd(cp)[0:128, 132:134])
                    kb.inc(_li, "ve", 1)
                    ev[("prepv2", g, c)] = kb.cnt["ve"]
                block.vector(prep_v2)

            def emit_steps(c):
                tc0, qc = CHUNKS[c]
                cp = c % 2
                for hh in range(2):
                    s = 2 * c + hh
                    sp = s % 2
                    hgl = 2 * g + hh
                    hp = hgl % 2
                    warm = c < NWARM
                    ncg = G[0:qc, GC_NCG[0] + 2 * cp + hh:GC_NCG[0] + 2 * cp + hh + 1]
                    eat = G[0:qc, GC_EAT[0] + 2 * cp + hh:GC_EAT[0] + 2 * cp + hh + 1]
                    dtb = G[64 * hh:64 * hh + 64,
                            GC_DTB[0] + 2 * cp + hh:GC_DTB[0] + 2 * cp + hh + 1]
                    dbc = G[0:qc, GC_DBC[0] + 64 * hp:GC_DBC[0] + 64 * hp + 64]
                    xtok = C[0:qc, C_XT[0] + 64 * sp:C_XT[0] + 64 * sp + 64]
                    bhat = C[0:qc, C_BH[0] + 64 * sp:C_BH[0] + 64 * sp + 64]
                    tcl = Fa[0:qc, FA_TCL[0] + 128 * sp:FA_TCL[0] + 128 * sp + qc]
                    xd = Fa[0:qc, FA_XD[0] + 64 * sp:FA_XD[0] + 64 * sp + 64]
                    y2 = Fa[0:qc, FA_Y2[0] + 64 * sp:FA_Y2[0] + 64 * sp + 64]
                    srf = SRF[64 * hh:64 * hh + 64, 0:64]
                    srb = SRB[64 * hh:64 * hh + 64, 0:64]
                    cgs = cbx(gp2, 0)[64 * hh:64 * hh + 64, tc0:tc0 + qc]
                    bgs = cbx(gp2, 1)[64 * hh:64 * hh + 64, tc0:tc0 + qc]
                    xgs = cbx(gp2, 2)[64 * hh:64 * hh + 64, tc0:tc0 + qc]
                    idb64 = c_identb()[64 * hh:64 * hh + 64, 64 * hh:64 * hh + 64]

                    def st_tA(tensor, g=g, c=c, s=s, sp=sp, hh=hh, qc=qc, cp=cp):
                        if s >= 2:
                            kb.w(tensor, "ve", ev[("stepV", g, s - 2)])
                            kb.w(tensor, "gp", ev[("stepG", g, s - 2)])
                        elif g >= 1:
                            kb.w(tensor, "ve", ev[("stepV", g - 1, 2 * len(CHUNKS) - 2 + sp)])
                            kb.w(tensor, "gp", ev[("stepG", g - 1, 2 * len(CHUNKS) - 2 + sp)])
                        kb.w(tensor, "ve", ev[("prepv1", g, c)])
                        kb.inc(tensor.matmul(
                            psW(sp)[0:qc, 0:qc],
                            OH2[0:2, 128 * hh:128 * hh + qc],
                            G[0:2, GC_CAG[0] + 128 * cp:GC_CAG[0] + 128 * cp + qc],
                            start=True, stop=True), "pe", 1)
                        ev[("stT1", g, s)] = kb.cnt["pe"]
                    block.tensor(st_tA)

                    def st_vA(vector, g=g, c=c, s=s, sp=sp, qc=qc, ncg=ncg, tcl=tcl):
                        kb.w(vector, "pe", ev[("stT1", g, s)])
                        if s >= 2:
                            kb.w(vector, "ac", ev[("stA1", g, s - 2)])
                        _li = vector.tensor_scalar(tcl, psW(sp)[0:qc, 0:qc],
                                             ncg, 0.0, ALU.add, ALU.min)
                        kb.inc(_li, "ve", 1)
                        ev[("stV1", g, s)] = kb.cnt["ve"]
                    block.vector(st_vA)

                    def st_aA(scalar, g=g, s=s, sp=sp, qc=qc, tcl=tcl):
                        kb.w(scalar, "ve", ev[("stV1", g, s)])
                        if s >= 2:
                            kb.w(scalar, "ve", ev[("stepV", g, s - 2)])
                            if ("stGP", g, s - 2) in ev:
                                kb.w(scalar, "gp", ev[("stGP", g, s - 2)])
                        scalar.activation(etile(sp)[0:qc, 0:qc], tcl, AF.Exp)
                        kb.inc(scalar.activation(G[0:qc, 908 + sp:909 + sp],
                                                 tcl[0:qc, qc - 1:qc], AF.Exp),
                               "ac", 1)
                        ev[("stA1", g, s)] = kb.cnt["ac"]
                    block.scalar(st_aA)

                    if not warm:
                        def st_gA(gp, g=g, s=s, sp=sp, qc=qc):
                            kb.w(gp, "ac", ev[("stA1", g, s)])
                            if s >= 2:
                                kb.w(gp, "ve", ev[("stepV", g, s - 2)])
                            _li = gp.tensor_mul(wtile_m(sp)[0:qc, 0:qc],
                                          etile(sp)[0:qc, 0:qc],
                                          c_ltb()[0:qc, 0:qc])
                            kb.inc(_li, "gp", 1)
                            ev[("stGP", g, s)] = kb.cnt["gp"]
                        block.gpsimd(st_gA)

                    def st_tB(tensor, g=g, s=s, sp=sp, qc=qc, hh=hh, warm=warm,
                              cgs=cgs, bgs=bgs, xgs=xgs, idb64=idb64):
                        if not warm:
                            tensor.matmul(psW(sp)[0:qc, 128:256][:, 0:qc],
                                          bgs, cgs, start=True, stop=True)
                        tensor.transpose(psT(sp)[0:qc, 0:64], xgs, idb64)
                        last = tensor.transpose(psT(sp)[0:qc, 64:128], bgs, idb64)
                        kb.inc(last, "pe", 1)
                        ev[("stT234", g, s)] = kb.cnt["pe"]
                    block.tensor(st_tB)

                    def st_vB(vector, g=g, s=s, sp=sp, qc=qc, xtok=xtok,
                              bhat=bhat, warm=warm):
                        kb.w(vector, "pe", ev[("stT234", g, s)])
                        kb.w(vector, "ac", ev[("stA1", g, s)])
                        if s >= 2:
                            kb.w(vector, "pe", ev[("stT567", g, s - 2)])
                            kb.w(vector, "ve", ev[("stepV", g, s - 2)])
                        vector.tensor_copy(xtok, psT(sp)[0:qc, 0:64])
                        _li = vector.tensor_scalar_mul(
                            bhat, psT(sp)[0:qc, 64:128],
                            G[0:qc, 908 + sp:909 + sp])
                        if not warm:
                            kb.w(vector, "gp", ev[("stGP", g, s)])
                            _li = vector.tensor_mul(wtile_m(sp)[0:qc, 0:qc],
                                              wtile_m(sp)[0:qc, 0:qc],
                                              psW(sp)[0:qc, 128:256][:, 0:qc])
                        kb.inc(_li, "ve", 1)
                        ev[("stV2", g, s)] = kb.cnt["ve"]
                    block.vector(st_vB)

                    def st_tC(tensor, g=g, s=s, sp=sp, qc=qc, hh=hh, warm=warm,
                              cgs=cgs, xtok=xtok, bhat=bhat):
                        kb.w(tensor, "ve", ev[("stV2", g, s)])
                        if s >= 2:
                            kb.w(tensor, "gp", ev[("stA3g", g, s - 2)])
                        kb.w(tensor, "ve", ev[("p4si", g)])
                        if not warm:
                            tensor.matmul(psW(sp)[0:qc, 256:320],
                                          wtile_m(sp)[0:qc, 0:qc], xtok,
                                          start=True, stop=True)
                            tensor.matmul(psW(sp)[0:qc, 320:384], cgs,
                                          SRB[64 * hh:64 * hh + 64, 0:64],
                                          start=True, stop=True)
                        last = tensor.matmul(
                            psW(sp)[64 * hh:64 * hh + 64, 384:448], bhat, xtok,
                            start=True, stop=True, tile_position=(0, 64 * hh))
                        kb.inc(last, "pe", 1)
                        ev[("stT567", g, s)] = kb.cnt["pe"]
                    block.tensor(st_tC)

                    def st_vC(vector, g=g, c=c, s=s, sp=sp, qc=qc, hh=hh,
                              hgl=hgl, warm=warm, xd=xd, y2=y2, xtok=xtok,
                              eat=eat, dtb=dtb, dbc=dbc, srf=srf):
                        kb.w(vector, "pe", ev[("stT567", g, s)])
                        kb.w(vector, "ac", ev[("prepa1", g, c)])
                        kb.w(vector, "ve", ev[("prepv2", g, c)])
                        if not warm:
                            kb.w(vector, "ve", ev[("dbcev", hgl)])
                            vector.tensor_mul(xd, xtok, dbc)
                            vector.scalar_tensor_tensor(
                                y2, psW(sp)[0:qc, 320:384][:, 0:64], eat, xd,
                                ALU.mult, ALU.add)
                            yslice = Dn[0:qc, 2048 * (c - NWARM) + 64 * hgl:
                                        2048 * (c - NWARM) + 64 * hgl + 64]
                            vector.tensor_add(yslice, y2,
                                              psW(sp)[0:qc, 256:320][:, 0:64])
                        _li = vector.scalar_tensor_tensor(
                            srf, srf, dtb,
                            psW(sp)[64 * hh:64 * hh + 64, 384:448], ALU.mult,
                            ALU.add)
                        kb.inc(_li, "ve", 1)
                        ev[("stepV", g, s)] = kb.cnt["ve"]
                    block.vector(st_vC)

                    def st_gC(gp, g=g, s=s, hh=hh, srf=srf, srb=srb):
                        kb.w(gp, "ve", ev[("stepV", g, s)])
                        _li = gp.tensor_copy(srb, srf)
                        kb.inc(_li, "gp", 1)
                        ev[("stA3g", g, s)] = kb.cnt["gp"]
                        ev[("stepA", g, s)] = kb.cnt["ac"]
                        ev[("stepG", g, s)] = kb.cnt["gp"]
                    block.gpsimd(st_gC)

            emit_prep(0)
            for c in range(len(CHUNKS)):
                if c + 1 < len(CHUNKS):
                    emit_prep(c + 1)
                emit_steps(c)
            ev[("p4gdone", g)] = kb.cnt["pe"]
        fence()

        if DBG == 4:
            def pdbg4(sync):
                for sname in FSEMS:
                    kb.w(sync, sname)
                kb.inc(sync.dma_start(out=dbg[:, 0:9216],
                                      in_=Dn[:, :].bitcast(F32)), "st0", 16)
                kb.w(sync, "st0")
            block.sync(pdbg4)

        NTT = 9

        for zg in range(4):
            def p5_zw(gp, zg=zg):
                if zg >= 1:
                    kb.w(gp, "pe", ev[("p5zmm", zg - 1)])
                src = wt_in_z.rearrange("(kt p) o -> p kt o", p=128)
                for kt in range(16):
                    buf = (E, E2)[kt // 8]
                    k2 = kt % 8
                    dst = buf[:, 512 * k2:512 * (k2 + 1)]
                    kb.inc(gp.dma_start(
                        out=dst.rearrange("p (a o) -> p a o", a=1),
                        in_=src[:, kt:kt + 1, 512 * zg:512 * (zg + 1)]),
                        "wZ", 16)
                ev[("p5zw", zg)] = kb.cnt["wZ"]
            block.gpsimd(p5_zw)

            for tt in range(NTT):
                qt = 128 if tt < 8 else 16
                idx = zg * NTT + tt

                def p5_zmm(tensor, zg=zg, tt=tt, qt=qt, idx=idx):
                    if tt == 0:
                        kb.w(tensor, "wZ", ev[("p5zw", zg)])
                    if idx >= 2:
                        kb.w(tensor, "ve", ev[("p5gat", idx - 2)])
                    bank = (pa, pb)[idx % 2]
                    for kt in range(16):
                        buf = (E, E2)[kt // 8]
                        k2 = kt % 8
                        last = tensor.matmul(
                            bank[0:qt, 0:512],
                            xb(kt, W0 + 128 * tt, qt),
                            buf[:, 512 * k2:512 * (k2 + 1)],
                            start=(kt == 0), stop=(kt == 15))
                    kb.inc(last, "pe", 1)
                    ev[("p5zmmi", idx)] = kb.cnt["pe"]
                    if tt == NTT - 1:
                        ev[("p5zmm", zg)] = kb.cnt["pe"]
                block.tensor(p5_zmm)

                def p5_sil(scalar, zg=zg, tt=tt, qt=qt, idx=idx):
                    kb.w(scalar, "pe", ev[("p5zmmi", idx)])
                    if idx >= 2:
                        kb.w(scalar, "ve", ev[("p5gat", idx - 2)])
                    bank = (pa, pb)[idx % 2]
                    kb.inc(scalar.activation(
                        GSB[0:qt, 512 * (idx % 2):512 * (idx % 2) + 512],
                        bank[0:qt, 0:512], AF.Silu), "ac", 1)
                    ev[("p5sil", idx)] = kb.cnt["ac"]
                block.scalar(p5_sil)

                def p5_gat(vector, zg=zg, tt=tt, qt=qt, idx=idx):
                    kb.w(vector, "ac", ev[("p5sil", idx)])
                    ysl = Dn[0:qt, 2048 * tt + 512 * zg:2048 * tt + 512 * zg + 512]
                    _li = vector.tensor_mul(ysl, ysl,
                                      GSB[0:qt, 512 * (idx % 2):512 * (idx % 2) + 512])
                    kb.inc(_li, "ve", 1)
                    ev[("p5gat", idx)] = kb.cnt["ve"]
                block.vector(p5_gat)
        fence()

        for cg in range(4):
            def p5_gbl(sync, cg=cg):
                if cg >= 1:
                    kb.w(sync, "ve", ev[("p5gbe", cg - 1)])
                kb.inc(sync.dma_start(out=Fa[0:1, 0:512],
                                      in_=gamma[0:1, 512 * cg:512 * (cg + 1)]),
                       "sxg", 16)
                kb.inc(sync.dma_start(out=Fa[0:1, 512:1024],
                                      in_=beta[0:1, 512 * cg:512 * (cg + 1)]),
                       "sxg", 16)
                ev[("p5gbl", cg)] = kb.cnt["sxg"]
            block.sync(p5_gbl)

            def p5_gbm(tensor, cg=cg):
                kb.w(tensor, "sxg", ev[("p5gbl", cg)])
                if cg >= 1:
                    kb.w(tensor, "ve", ev[("p5gbe", cg - 1)])
                tensor.matmul(pa[:, 0:512], c_ones1f(0), Fa[0:1, 0:512],
                              start=True, stop=True)
                kb.inc(tensor.matmul(pb[:, 0:512], c_ones1f(0), Fa[0:1, 512:1024],
                                     start=True, stop=True), "pe", 1)
                ev[("p5gbm", cg)] = kb.cnt["pe"]
            block.tensor(p5_gbm)

            def p5_gbe(vector, cg=cg):
                kb.w(vector, "pe", ev[("p5gbm", cg)])
                vector.tensor_copy(E[:, 512 * cg:512 * (cg + 1)], pa[:, 0:512])
                _li = vector.tensor_copy(E[:, 2048 + 512 * cg:2048 + 512 * (cg + 1)],
                                   pb[:, 0:512])
                kb.inc(_li, "ve", 1)
                ev[("p5gbe", cg)] = kb.cnt["ve"]
            block.vector(p5_gbe)
        fence()

        GLN = 876
        for tt in range(NTT):
            qt = 128 if tt < 8 else 16
            tp = tt % 2
            gc0 = GLN + 16 * tp
            ysl = Dn[0:qt, 2048 * tt:2048 * (tt + 1)]
            nsc = G[0:qt, gc0 + 0:gc0 + 1]
            sqc = G[0:qt, gc0 + 1:gc0 + 2]
            m2c = G[0:qt, gc0 + 2:gc0 + 3]
            qcc = G[0:qt, gc0 + 3:gc0 + 4]
            sdc = G[0:qt, gc0 + 4:gc0 + 5]
            rsc = G[0:qt, gc0 + 5:gc0 + 6]
            nmc = G[0:qt, gc0 + 6:gc0 + 7]

            def ln2_v1(vector, tt=tt, qt=qt, ysl=ysl, nsc=nsc):
                if tt >= 2:
                    kb.w(vector, "ve", ev[("ln2v6", tt - 2)])
                    kb.w(vector, "gp", ev[("ln2gp", tt - 2)])
                    kb.w(vector, "ac", ev[("ln2a2", tt - 2)])
                _li = vector.reduce_sum(nsc, ysl, AX.X, negate=True)
                kb.inc(_li, "ve", 1)
                ev[("ln2v1", tt)] = kb.cnt["ve"]
            block.vector(ln2_v1)

            def ln2_a1(scalar, tt=tt, qt=qt, ysl=ysl, sqc=sqc):
                if tt >= 2:
                    kb.w(scalar, "gp", ev[("ln2gp", tt - 2)])
                kb.inc(scalar.activation(C[0:qt, 0:2048], ysl, AF.Square,
                                          accum_out=sqc), "ac", 1)
                ev[("ln2a1", tt)] = kb.cnt["ac"]
            block.scalar(ln2_a1)

            def ln2_gp1(gp, tt=tt, nsc=nsc, m2c=m2c):
                kb.w(gp, "ve", ev[("ln2v1", tt)])
                _li = gp.tensor_mul(m2c, nsc, nsc)
                kb.inc(_li, "gp", 1)
                ev[("ln2gp1", tt)] = kb.cnt["gp"]
            block.gpsimd(ln2_gp1)

            def ln2_v2(vector, tt=tt, m2c=m2c, sqc=sqc, qcc=qcc):
                kb.w(vector, "gp", ev[("ln2gp1", tt)])
                kb.w(vector, "ac", ev[("ln2a1", tt)])
                _li = vector.scalar_tensor_tensor(qcc, m2c, -1.0 / D, sqc,
                                            ALU.mult, ALU.add)
                kb.inc(_li, "ve", 1)
                ev[("ln2v2", tt)] = kb.cnt["ve"]
            block.vector(ln2_v2)

            def ln2_a2(scalar, tt=tt, qcc=qcc, sdc=sdc):
                kb.w(scalar, "ve", ev[("ln2v2", tt)])
                kb.inc(scalar.activation(sdc, qcc, AF.Sqrt, scale=1.0 / D,
                                         bias=G[0:qt, 1023:1024]), "ac", 1)
                ev[("ln2a2", tt)] = kb.cnt["ac"]
            block.scalar(ln2_a2)

            def ln2_v3(vector, tt=tt, sdc=sdc, rsc=rsc, nsc=nsc, nmc=nmc):
                kb.w(vector, "ac", ev[("ln2a2", tt)])
                vector.tensor_scalar_mul(nmc, nsc, 1.0 / D)
                _li = vector.reciprocal(rsc, sdc)
                kb.inc(_li, "ve", 1)
                ev[("ln2v3", tt)] = kb.cnt["ve"]
            block.vector(ln2_v3)

            def ln2_gp2(gp, tt=tt, qt=qt, ysl=ysl, nmc=nmc, rsc=rsc):
                kb.w(gp, "ve", ev[("ln2v3", tt)])
                _li = gp.tensor_scalar(ysl, ysl, nmc, rsc, ALU.add, ALU.mult)
                kb.inc(_li, "gp", 1)
                ev[("ln2gp", tt)] = kb.cnt["gp"]
            block.gpsimd(ln2_gp2)

            def ln2_v6(vector, tt=tt, qt=qt, ysl=ysl):
                kb.w(vector, "gp", ev[("ln2gp", tt)])
                vector.tensor_mul(ysl, ysl, E[0:qt, 0:2048])
                _li = vector.tensor_add(ysl, ysl, E[0:qt, 2048:4096])
                kb.inc(_li, "ve", 1)
                ev[("ln2v6", tt)] = kb.cnt["ve"]
            block.vector(ln2_v6)
        fence()

        for tt in range(NTT):
            qt = 128 if tt < 8 else 16
            for fg in range(2):
                idx = tt * 2 + fg

                def yt_t(tensor, tt=tt, qt=qt, fg=fg, idx=idx):
                    if idx >= 2:
                        kb.w(tensor, "ve", ev[("ytv", idx - 2)])
                    bank = (pt_a, pt_b)[idx % 2]
                    last = None
                    for f2 in range(8):
                        ft = 8 * fg + f2
                        last = tensor.transpose(
                            bank[0:128, 128 * f2:128 * f2 + qt],
                            Dn[0:qt, 2048 * tt + 128 * ft:2048 * tt + 128 * (ft + 1)],
                            c_identb()[0:qt, 0:qt])
                    kb.inc(last, "pe", 1)
                    ev[("ytt", idx)] = kb.cnt["pe"]
                block.tensor(yt_t)

                def yt_v(vector, tt=tt, qt=qt, fg=fg, idx=idx):
                    kb.w(vector, "pe", ev[("ytt", idx)])
                    bank = (pt_a, pt_b)[idx % 2]
                    for f2 in range(8):
                        ft = 8 * fg + f2
                        _li = vector.tensor_copy(
                            B[:, TE * ft + 128 * tt:TE * ft + 128 * tt + qt],
                            bank[0:128, 128 * f2:128 * f2 + qt])
                    kb.inc(_li, "ve", 1)
                    ev[("ytv", idx)] = kb.cnt["ve"]
                block.vector(yt_v)
        fence()

        if DBG == 5:
            def pdbg5(sync):
                for sname in FSEMS:
                    kb.w(sync, sname)
                kb.inc(sync.dma_start(out=dbg[:, 0:8320],
                                      in_=B[:, 0:NFT * TE].bitcast(F32)), "st0", 16)
                kb.w(sync, "st0")
            block.sync(pdbg5)

        def ynT(ft, c0, ln):
            return B[:, TE * ft + c0:TE * ft + c0 + ln]

        CH_P5 = [(0, 512), (512, 512), (1024, 16)]
        for ot in range(16):
            def wo_w(gp, ot=ot):
                if ot >= 2:
                    kb.w(gp, "pe", ev[("womm", ot - 2)])
                ev[("wow", ot)] = [load_strip(gp, wt_out, ot, eslot2(ot),
                                              "wF%d" % (ot % 2))]
            block.gpsimd(wo_w)

            for ci, (c0, ln) in enumerate(CH_P5):
                idx = ot * 3 + ci

                def wo_mm(tensor, ot=ot, ci=ci, c0=c0, ln=ln, idx=idx):
                    if ci == 0:
                        for sm, v in ev[("wow", ot)]:
                            kb.w(tensor, sm, v)
                    if idx >= 2:
                        kb.w(tensor, "ve", ev[("woa", idx - 2)])
                    bank = (pa, pb)[idx % 2]
                    for kt in range(16):
                        last = tensor.matmul(bank[:, 0:ln], wtile(eslot2(ot), kt),
                                             ynT(kt, c0, ln),
                                             start=(kt == 0), stop=(kt == 15))
                    kb.inc(last, "pe", 1)
                    ev[("wommi", idx)] = kb.cnt["pe"]
                    if ci == 2:
                        ev[("womm", ot)] = kb.cnt["pe"]
                block.tensor(wo_mm)

                def wo_a(vector, ot=ot, ci=ci, c0=c0, ln=ln, idx=idx):
                    kb.w(vector, "pe", ev[("wommi", idx)])
                    bank = (pa, pb)[idx % 2]
                    xd_ = x_ext(ot, c0, ln)
                    _li = vector.tensor_add(xd_, xd_, bank[:, 0:ln])
                    kb.inc(_li, "ve", 1)
                    ev[("woa", idx)] = kb.cnt["ve"]
                block.vector(wo_a)
        fence()


        def xb1(ft, c0, ln):
            return B[:, TE * ft + c0:TE * ft + c0 + ln]

        def p6_cast(vector):
            for ft in range(NFT):
                _li = vector.tensor_copy(xb1(ft, 0, TE), x_ext(ft, 0, TE))
            kb.inc(_li, "ve", 1)
        block.vector(p6_cast)
        fence()

        CH_L1 = [(0, 512), (512, 512), (1024, 16)]
        for ci, (c0, ln) in enumerate(CH_L1):
            for ft in range(NFT):
                def p6_sq(scalar, ci=ci, c0=c0, ln=ln, ft=ft):
                    if (ci, ft) > (0, 1):
                        pi, pf = (ci, ft - 2) if ft >= 2 else (ci - 1, NFT + ft - 2)
                        kb.w(scalar, "pe", ev[("p6sqmm", pi, pf)])
                    kb.inc(scalar.activation(
                        GSB[:, 512 * (ft % 2):512 * (ft % 2) + ln],
                        xb1(ft, c0, ln), AF.Square), "ac", 1)
                    ev[("p6sq", ci, ft)] = kb.cnt["ac"]
                block.scalar(p6_sq)

                def p6_mm(tensor, ci=ci, c0=c0, ln=ln, ft=ft):
                    if ft == 0 and ci >= 1:
                        kb.w(tensor, "ve", ev[("p6ev", ci - 1)])
                    tensor.matmul(pg[0:1, 0:ln], c_onesbf(), xb1(ft, c0, ln),
                                  start=(ft == 0), stop=(ft == NFT - 1))
                    kb.w(tensor, "ac", ev[("p6sq", ci, ft)])
                    kb.inc(tensor.matmul(ph[0:1, 0:ln], c_onesbf(),
                                         GSB[:, 512 * (ft % 2):512 * (ft % 2) + ln],
                                         start=(ft == 0), stop=(ft == NFT - 1)),
                           "pe", 1)
                    ev[("p6sqmm", ci, ft)] = kb.cnt["pe"]
                    if ft == NFT - 1:
                        ev[("p6mm", ci)] = kb.cnt["pe"]
                block.tensor(p6_mm)

            def p6_ev(vector, ci=ci, c0=c0, ln=ln):
                kb.w(vector, "pe", ev[("p6mm", ci)])
                vector.tensor_copy(SUMS[0:1, c0:c0 + ln], pg[0:1, 0:ln])
                _li = vector.tensor_copy(SUMS[0:1, TW + c0:TW + c0 + ln], ph[0:1, 0:ln])
                kb.inc(_li, "ve", 1)
                ev[("p6ev", ci)] = kb.cnt["ve"]
            block.vector(p6_ev)
        fence()

        def p6_rows(vector):
            vector.tensor_scalar_mul(MEAN[0:1, 0:TE], SUMS[0:1, 0:TE], 1.0 / D)
            vector.tensor_scalar_mul(SUMS[0:1, TW:TW + TE], SUMS[0:1, TW:TW + TE],
                                     1.0 / D)
            vector.tensor_mul(SUMS[0:1, 0:TE], MEAN[0:1, 0:TE], MEAN[0:1, 0:TE])
            _li = vector.tensor_sub(SUMS[0:1, TW:TW + TE], SUMS[0:1, TW:TW + TE],
                              SUMS[0:1, 0:TE])
            kb.inc(_li, "ve", 1)
        block.vector(p6_rows)
        fence(("scalar",))

        def p6_sqrt(scalar):
            kb.inc(scalar.activation(SUMS[0:1, TW:TW + TE], SUMS[0:1, TW:TW + TE],
                                     AF.Sqrt, bias=G[0:1, 1023:1024]), "ac", 1)
        block.scalar(p6_sqrt)
        fence(("vector",))

        def p6_rstd(vector):
            _li = vector.reciprocal(MEAN[0:1, TW:TW + TE], SUMS[0:1, TW:TW + TE])
            kb.inc(_li, "ve", 1)
        block.vector(p6_rstd)
        fence()

        CH_A1 = [(0, 512), (512, 512), (768, 272)]
        for ci, (c0, ln) in enumerate(CH_A1):
            def p6_bc(tensor, ci=ci, c0=c0, ln=ln):
                if ci >= 1:
                    kb.w(tensor, "ve", ev[("p6ap", ci - 1)])
                tensor.matmul(pa[0:128, 0:ln], c_ones1f(32),
                              MEAN[0:1, c0:c0 + ln], start=True, stop=True)
                kb.inc(tensor.matmul(pb[0:128, 0:ln], c_ones1f(32),
                                     MEAN[0:1, TW + c0:TW + c0 + ln],
                                     start=True, stop=True), "pe", 1)
                ev[("p6bc", ci)] = kb.cnt["pe"]
            block.tensor(p6_bc)

            def p6_ap(vector, ci=ci, c0=c0, ln=ln):
                kb.w(vector, "pe", ev[("p6bc", ci)])
                vector.tensor_copy(Mb(ln), pa[0:128, 0:ln])
                vector.tensor_copy(Rb(ln), pb[0:128, 0:ln])
                for ft in range(NFT):
                    vector.tensor_sub(G[:, 0:ln], x_ext(ft, c0, ln), Mb(ln))
                    _li = vector.tensor_mul(xb1(ft, c0, ln), G[:, 0:ln], Rb(ln))
                kb.inc(_li, "ve", 1)
                ev[("p6ap", ci)] = kb.cnt["ve"]
            block.vector(p6_ap)
        fence()

        def qkv_buf(gp2, j):
            return C[:, 3120 * gp2 + 1040 * j:3120 * gp2 + 1040 * (j + 1)]

        def vtok(hp, tt):
            return C[:, 6240 + 576 * hp + 64 * tt:6240 + 576 * hp + 64 * (tt + 1)]

        def ptile(bp, part):
            return C[:, 7392 + 256 * bp + 128 * part:7392 + 256 * bp + 128 * (part + 1)]

        NBLK = 8
        for g in range(16):
            gp2 = g % 2
            for j in range(3):
                si = 3 * g + j

                def p7_w(gp, g=g, j=j, si=si):
                    if si >= 4:
                        kb.w(gp, "pe", ev[("p7smm", si - 4)])
                    ev[("p7w", si)] = [load_strip(gp, wt_qkv, si, slot4(si),
                                                  "wS%d" % (si % 4))]
                block.gpsimd(p7_w)

                for ci, (c0, ln) in enumerate(CH_L1):
                    def p7_smm(tensor, g=g, j=j, si=si, ci=ci, c0=c0, ln=ln):
                        if ci == 0:
                            for sm, v in ev[("p7w", si)]:
                                kb.w(tensor, sm, v)
                        idx = si * 3 + ci
                        if idx >= 2:
                            kb.w(tensor, "ve", ev[("p7sev", idx - 2)])
                        bank = (pg, ph)[idx % 2]
                        for kt in range(16):
                            last = tensor.matmul(bank[:, 0:ln],
                                                 wtile(slot4(si), kt),
                                                 xb1(kt, c0, ln),
                                                 start=(kt == 0), stop=(kt == 15))
                        kb.inc(last, "pe", 1)
                        ev[("p7smmi", idx)] = kb.cnt["pe"]
                        if ci == 2:
                            ev[("p7smm", si)] = kb.cnt["pe"]
                    block.tensor(p7_smm)

                    def p7_sev(vector, g=g, j=j, si=si, ci=ci, c0=c0, ln=ln):
                        idx = si * 3 + ci
                        kb.w(vector, "pe", ev[("p7smmi", idx)])
                        if ci == 0 and j == 0 and g >= 2:
                            kb.w(vector, "pe", ev[("p7gdone", g - 2)])
                        bank = (pg, ph)[idx % 2]
                        _li = vector.tensor_copy(qkv_buf(g % 2, j)[:, c0:c0 + ln],
                                           bank[:, 0:ln])
                        kb.inc(_li, "ve", 1)
                        ev[("p7sev", idx)] = kb.cnt["ve"]
                    block.vector(p7_sev)

            for hh in range(2):
                hgl = 2 * g + hh
                hp = hgl % 2
                for tt in range(9):
                    qt = 128 if tt < 8 else 16
                    vti = hgl * 9 + tt

                    def p7_vt(tensor, g=g, hh=hh, tt=tt, qt=qt, vti=vti):
                        if vti >= 2:
                            kb.w(tensor, "ve", ev[("p7vtv", vti - 2)])
                        if tt == 0:
                            kb.w(tensor, "ve", ev[("p7sev", (3 * g + 2) * 3 + 2)])
                        bank = (pt_a, pt_b)[vti % 2]
                        kb.inc(tensor.transpose(
                            bank[0:qt, 256:320],
                            qkv_buf(g % 2, 2)[64 * hh:64 * hh + 64,
                                              128 * tt:128 * tt + qt],
                            c_identb()[64 * hh:64 * hh + 64,
                                       64 * hh:64 * hh + 64]), "pe", 1)
                        ev[("p7vt", vti)] = kb.cnt["pe"]
                    block.tensor(p7_vt)

                    def p7_vtv(vector, hh=hh, hp=hp, tt=tt, qt=qt, vti=vti):
                        kb.w(vector, "pe", ev[("p7vt", vti)])
                        bank = (pt_a, pt_b)[vti % 2]
                        _li = vector.tensor_copy(vtok(hp, tt)[0:qt, :],
                                           bank[0:qt, 256:320])
                        kb.inc(_li, "ve", 1)
                        ev[("p7vtv", vti)] = kb.cnt["ve"]
                    block.vector(p7_vtv)

                for b in range(NBLK):
                    bi = hgl * NBLK + b
                    bp = bi % 2
                    mi = 0 if b == 0 else (2 if b == NBLK - 1 else 1)
                    mmask = AMASK[:, 288 * mi:288 * mi + 144]
                    amask = AMASK[:, 288 * mi + 144:288 * mi + 288]
                    tscr = Fa[0:128, 128 + 160 * bp:128 + 160 * bp + 144]
                    nmx = G[0:128, 960 + 4 * bp:961 + 4 * bp]
                    rsm = G[0:128, 961 + 4 * bp:962 + 4 * bp]
                    rnm = G[0:128, 962 + 4 * bp:963 + 4 * bp]
                    ptl = Hc[:, 384 + 144 * bp:384 + 144 * bp + 144]

                    def at_t1(tensor, g=g, hh=hh, b=b, bi=bi, bp=bp):
                        if bi >= 2:
                            kb.w(tensor, "ve", ev[("atV", bi - 2)])
                            kb.w(tensor, "gp", ev[("atGP2", bi - 2)])
                        kb.inc(tensor.matmul(
                            psW(bp)[0:128, 0:144],
                            qkv_buf(g % 2, 0)[64 * hh:64 * hh + 64,
                                              8 + 128 * b:8 + 128 * b + 128],
                            qkv_buf(g % 2, 1)[64 * hh:64 * hh + 64,
                                              128 * b:128 * b + 144],
                            start=True, stop=True), "pe", 1)
                        ev[("atT1", bi)] = kb.cnt["pe"]
                    block.tensor(at_t1)

                    def at_v1(vector, bi=bi, bp=bp, mmask=mmask, amask=amask,
                              tscr=tscr, nmx=nmx):
                        kb.w(vector, "pe", ev[("atT1", bi)])
                        if bi >= 2:
                            kb.w(vector, "ac", ev[("atA1", bi - 2)])
                        vector.tensor_mul(tscr, psW(bp)[0:128, 0:144], mmask)
                        vector.tensor_add(tscr, tscr, amask)
                        _li = vector.reduce_max(nmx, tscr, AX.X, negate=True)
                        kb.inc(_li, "ve", 1)
                        ev[("atV2", bi)] = kb.cnt["ve"]
                    block.vector(at_v1)

                    def at_a1(scalar, bi=bi, bp=bp, tscr=tscr, nmx=nmx, rsm=rsm,
                              ptl=ptl):
                        kb.w(scalar, "ve", ev[("atV2", bi)])
                        if bi >= 2:
                            kb.w(scalar, "gp", ev[("atGP2", bi - 2)])
                            kb.w(scalar, "pe", ev[("atT2", bi - 2)])
                            kb.w(scalar, "ve", ev[("atA2v", bi - 2)])
                        kb.inc(scalar.activation(ptl, tscr, AF.Exp, bias=nmx,
                                                 accum_out=rsm), "ac", 1)
                        ev[("atA1", bi)] = kb.cnt["ac"]
                    block.scalar(at_a1)

                    def at_v3(vector, bi=bi, rsm=rsm, rnm=rnm):
                        kb.w(vector, "ac", ev[("atA1", bi)])
                        _li = vector.reciprocal(rnm, rsm)
                        kb.inc(_li, "ve", 1)
                        ev[("atV3", bi)] = kb.cnt["ve"]
                    block.vector(at_v3)

                    def at_gp2(gp, bi=bi, ptl=ptl, rnm=rnm):
                        kb.w(gp, "ve", ev[("atV3", bi)])
                        _li = gp.tensor_scalar_mul(ptl, ptl, rnm)
                        kb.inc(_li, "gp", 1)
                        ev[("atGP2", bi)] = kb.cnt["gp"]
                    block.gpsimd(at_gp2)

                    def at_t2(tensor, hgl=hgl, b=b, bi=bi, bp=bp, ptl=ptl):
                        kb.w(tensor, "gp", ev[("atGP2", bi)])
                        if b == 0:
                            kb.w(tensor, "ve", ev[("p7vtv", hgl * 9 + 8)])
                        if bi >= 2:
                            kb.w(tensor, "ve", ev[("atA2v", bi - 2)])
                        tensor.transpose(psT(bp)[0:128, 0:128], ptl[:, 0:128],
                                         c_identb())
                        kb.inc(tensor.transpose(psT(bp)[0:16, 128:256],
                                                ptl[:, 128:144], c_identb()),
                               "pe", 1)
                        ev[("atT2", bi)] = kb.cnt["pe"]
                    block.tensor(at_t2)

                    def at_a2(vector, bi=bi, bp=bp):
                        kb.w(vector, "pe", ev[("atT2", bi)])
                        if bi >= 2:
                            kb.w(vector, "pe", ev[("atT3", bi - 2)])
                        vector.tensor_copy(ptile(bp, 0), psT(bp)[0:128, 0:128])
                        _li = vector.tensor_copy(ptile(bp, 1)[0:16, :],
                                           psT(bp)[0:16, 128:256])
                        kb.inc(_li, "ve", 1)
                        ev[("atA2v", bi)] = kb.cnt["ve"]
                    block.vector(at_a2)

                    def at_t3(tensor, hh=hh, hp=hp, b=b, bi=bi, bp=bp):
                        kb.w(tensor, "ve", ev[("atA2v", bi)])
                        tensor.matmul(psW(bp)[0:64, 256:384],
                                      vtok(hp, b), ptile(bp, 0),
                                      start=True, stop=False)
                        kb.inc(tensor.matmul(psW(bp)[0:64, 256:384],
                                             vtok(hp, b + 1)[0:16, :],
                                             ptile(bp, 1)[0:16, :],
                                             start=False, stop=True), "pe", 1)
                        ev[("atT3", bi)] = kb.cnt["pe"]
                    block.tensor(at_t3)

                    def at_v4(vector, g=g, hh=hh, hgl=hgl, b=b, bi=bi, bp=bp):
                        kb.w(vector, "pe", ev[("atT3", bi)])
                        dst = Dn[64 * (hgl % 2):64 * (hgl % 2) + 64,
                                 1024 * (hgl // 2) + 128 * b:
                                 1024 * (hgl // 2) + 128 * b + 128]
                        _li = vector.tensor_copy(dst, psW(bp)[0:64, 256:384])
                        kb.inc(_li, "ve", 1)
                        ev[("atV", bi)] = kb.cnt["ve"]
                    block.vector(at_v4)

            ev[("p7gdone", g)] = kb.cnt["pe"]
        fence()

        CH_PJ = [(0, 512), (512, 512)]
        for ot in range(16):
            def pj_w(gp, ot=ot):
                if ot >= 2:
                    kb.w(gp, "pe", ev[("pjmm", ot - 2)])
                ev[("pjw", ot)] = [load_strip(gp, wt_proj, ot, eslot2(ot),
                                              "wF%d" % (ot % 2))]
            block.gpsimd(pj_w)

            for ci, (c0, ln) in enumerate(CH_PJ):
                idx = ot * 2 + ci

                def pj_mm(tensor, ot=ot, ci=ci, c0=c0, ln=ln, idx=idx):
                    if ci == 0:
                        for sm, v in ev[("pjw", ot)]:
                            kb.w(tensor, sm, v)
                    if idx >= 2:
                        kb.w(tensor, "ve", ev[("pja", idx - 2)])
                    bank = (pa, pb)[idx % 2]
                    for kt in range(16):
                        last = tensor.matmul(bank[:, 0:ln], wtile(eslot2(ot), kt),
                                             Dn[:, 1024 * kt + c0:1024 * kt + c0 + ln],
                                             start=(kt == 0), stop=(kt == 15))
                    kb.inc(last, "pe", 1)
                    ev[("pjmmi", idx)] = kb.cnt["pe"]
                    if ci == 1:
                        ev[("pjmm", ot)] = kb.cnt["pe"]
                block.tensor(pj_mm)

                def pj_a(vector, ot=ot, ci=ci, c0=c0, ln=ln, idx=idx):
                    kb.w(vector, "pe", ev[("pjmmi", idx)])
                    bank = (pa, pb)[idx % 2]
                    xd_ = x_ext(ot, 8 + c0, ln)
                    _li = vector.scalar_tensor_tensor(xd_, bank[:, 0:ln],
                                                PRB[:, ot:ot + 1], xd_,
                                                ALU.add, ALU.add)
                    kb.inc(_li, "ve", 1)
                    ev[("pja", idx)] = kb.cnt["ve"]
                block.vector(pj_a)
        fence()

        if DBG == 6:
            def pdbg6(sync):
                for sname in FSEMS:
                    kb.w(sync, sname)
                kb.inc(sync.dma_start(out=dbg[:, 0:NFT * TE],
                                      in_=A[:, 0:NFT * TE]), "st0", 16)
                kb.w(sync, "st0")
            block.sync(pdbg6)

        def p8_rw(gp):
            src = wt_router.rearrange("(kt p) o -> p kt o", p=128)
            kb.inc(gp.dma_start(
                out=Hc[:, 384:384 + 128].rearrange("p (kt o) -> p kt o", o=8),
                in_=src[:, :, 0:8]), "wH", 16)
            ev["p8rw"] = kb.cnt["wH"]
        block.gpsimd(p8_rw)

        for ci, (c0, ln) in enumerate(CH_PJ):
            def p8_rmm(tensor, ci=ci, c0=c0, ln=ln):
                if ci == 0:
                    kb.w(tensor, "wH", ev["p8rw"])
                bank = (pc, pd)[ci % 2]
                for kt in range(16):
                    last = tensor.matmul(bank[0:8, 0:ln],
                                         Hc[:, 384 + 8 * kt:384 + 8 * (kt + 1)],
                                         xb1(kt, 8 + c0, ln),
                                         start=(kt == 0), stop=(kt == 15))
                kb.inc(last, "pe", 1)
                ev[("p8rmm", ci)] = kb.cnt["pe"]
            block.tensor(p8_rmm)

            def p8_rev(vector, ci=ci, c0=c0, ln=ln):
                kb.w(vector, "pe", ev[("p8rmm", ci)])
                bank = (pc, pd)[ci % 2]
                _li = vector.tensor_copy(G[0:8, c0:c0 + ln], bank[0:8, 0:ln])
                kb.inc(_li, "ve", 1)
                ev[("p8rev", ci)] = kb.cnt["ve"]
            block.vector(p8_rev)
        fence()

        for tt in range(NBLK):
            def p8_lt(tensor, tt=tt):
                if tt >= 2:
                    kb.w(tensor, "ve", ev[("p8ltv", tt - 2)])
                bank = (pc, pd)[tt % 2]
                kb.inc(tensor.transpose(bank[0:128, 134:142],
                                        G[0:8, 128 * tt:128 * (tt + 1)],
                                        c_identf()[0:8, 0:8]), "pe", 1)
                ev[("p8lt", tt)] = kb.cnt["pe"]
            block.tensor(p8_lt)

            def p8_ltv(vector, tt=tt):
                kb.w(vector, "pe", ev[("p8lt", tt)])
                bank = (pc, pd)[tt % 2]
                _li = vector.tensor_copy(Fa[0:128, 8 * tt:8 * (tt + 1)],
                                   bank[0:128, 134:142])
                kb.inc(_li, "ve", 1)
                ev[("p8ltv", tt)] = kb.cnt["ve"]
            block.vector(p8_ltv)
        fence()

        PRB_ = None
        FA_PROB = 64
        FA_CW = 128
        GR = 876

        def p8_sm1(vector):
            for tt in range(NBLK):
                _li = vector.reduce_max(G[:, GR + 16 * 0 + tt:GR + 16 * 0 + tt + 1],
                                  Fa[:, 8 * tt:8 * (tt + 1)], AX.X, negate=True)
            kb.inc(_li, "ve", 1)
            ev["p8sm1"] = kb.cnt["ve"]
        block.vector(p8_sm1)

        def p8_sm2(scalar):
            kb.w(scalar, "ve", ev["p8sm1"])
            last = None
            for tt in range(NBLK):
                last = scalar.activation(
                    Fa[:, FA_PROB + 8 * tt:FA_PROB + 8 * (tt + 1)],
                    Fa[:, 8 * tt:8 * (tt + 1)], AF.Exp,
                    bias=G[:, GR + tt:GR + tt + 1],
                    accum_out=G[:, GR + 16 + tt:GR + 16 + tt + 1])
            kb.inc(last, "ac", 1)
            ev["p8sm2"] = kb.cnt["ac"]
        block.scalar(p8_sm2)

        def p8_sm3(vector):
            kb.w(vector, "ac", ev["p8sm2"])
            for tt in range(NBLK):
                _li = vector.reduce_max(G[:, GR + 32 + tt:GR + 32 + tt + 1],
                                  Fa[:, FA_PROB + 8 * tt:FA_PROB + 8 * (tt + 1)],
                                  AX.X)
            kb.inc(_li, "ve", 1)
            ev["p8sm3"] = kb.cnt["ve"]
        block.vector(p8_sm3)

        def p8_sm4(gp):
            kb.w(gp, "ve", ev["p8sm3"])
            for tt in range(NBLK):
                _li = gp.tensor_scalar(Fa[:, FA_CW + 8 * tt:FA_CW + 8 * (tt + 1)],
                                 Fa[:, FA_PROB + 8 * tt:FA_PROB + 8 * (tt + 1)],
                                 G[:, GR + 32 + tt:GR + 32 + tt + 1], None,
                                 ALU.is_equal)
            kb.inc(_li, "gp", 1)
            ev["p8sm4"] = kb.cnt["gp"]
        block.gpsimd(p8_sm4)

        def p8_sm5(vector):
            kb.w(vector, "gp", ev["p8sm4"])
            for tt in range(NBLK):
                vector.scalar_tensor_tensor(
                    G[0:128, 128 + 8 * tt:128 + 8 * (tt + 1)],
                    Fa[:, FA_CW + 8 * tt:FA_CW + 8 * (tt + 1)], -1e30,
                    Fa[:, FA_PROB + 8 * tt:FA_PROB + 8 * (tt + 1)],
                    ALU.mult, ALU.add)
            for tt in range(NBLK):
                vector.reduce_max(G[:, GR + 48 + tt:GR + 48 + tt + 1],
                                  G[0:128, 128 + 8 * tt:128 + 8 * (tt + 1)], AX.X)
            for tt in range(NBLK):
                _li = vector.scalar_tensor_tensor(
                    G[:, GR + 64 + tt:GR + 64 + tt + 1],
                    G[:, GR + 48 + tt:GR + 48 + tt + 1], 1e-20,
                    G[:, GR + 32 + tt:GR + 32 + tt + 1], ALU.add, ALU.add)
            kb.inc(_li, "ve", 1)
            ev["p8sm5"] = kb.cnt["ve"]
        block.vector(p8_sm5)

        def p8_sm6(gp):
            kb.w(gp, "ve", ev["p8sm5"])
            for tt in range(NBLK):
                _li = gp.tensor_scalar(G[0:128, 192 + 8 * tt:192 + 8 * (tt + 1)],
                                 G[0:128, 128 + 8 * tt:128 + 8 * (tt + 1)],
                                 G[:, GR + 48 + tt:GR + 48 + tt + 1], None,
                                 ALU.is_equal)
            kb.inc(_li, "gp", 1)
            ev["p8sm6"] = kb.cnt["gp"]
        block.gpsimd(p8_sm6)

        def p8_sm7(vector):
            kb.w(vector, "gp", ev["p8sm6"])
            for tt in range(NBLK):
                vector.tensor_add(Fa[:, FA_CW + 8 * tt:FA_CW + 8 * (tt + 1)],
                                  Fa[:, FA_CW + 8 * tt:FA_CW + 8 * (tt + 1)],
                                  G[0:128, 192 + 8 * tt:192 + 8 * (tt + 1)])
            for tt in range(NBLK):
                vector.reciprocal(G[:, GR + 80 + tt:GR + 80 + tt + 1],
                                  G[:, GR + 64 + tt:GR + 64 + tt + 1])
            for tt in range(NBLK):
                _li = vector.tensor_mul(Fa[:, FA_CW + 8 * tt:FA_CW + 8 * (tt + 1)],
                                  Fa[:, FA_CW + 8 * tt:FA_CW + 8 * (tt + 1)],
                                  Fa[:, FA_PROB + 8 * tt:FA_PROB + 8 * (tt + 1)])
            kb.inc(_li, "ve", 1)
            ev["p8sm7"] = kb.cnt["ve"]
        block.vector(p8_sm7)

        def p8_sm8(gp):
            kb.w(gp, "ve", ev["p8sm7"])
            for tt in range(NBLK):
                _li = gp.tensor_scalar_mul(Fa[:, FA_CW + 8 * tt:FA_CW + 8 * (tt + 1)],
                                     Fa[:, FA_CW + 8 * tt:FA_CW + 8 * (tt + 1)],
                                     G[:, GR + 80 + tt:GR + 80 + tt + 1])
            kb.inc(_li, "gp", 1)
            ev["p8sm8"] = kb.cnt["gp"]
        block.gpsimd(p8_sm8)

        for tt in range(NBLK):
            def p8_ct(tensor, tt=tt):
                if tt == 0:
                    kb.w(tensor, "gp", ev["p8sm8"])
                if tt >= 2:
                    kb.w(tensor, "ve", ev[("p8ctv", tt - 2)])
                bank = (pc, pd)[tt % 2]
                kb.inc(tensor.transpose(bank[0:8, 144:272],
                                        Fa[:, FA_CW + 8 * tt:FA_CW + 8 * (tt + 1)],
                                        c_identf()), "pe", 1)
                ev[("p8ct", tt)] = kb.cnt["pe"]
            block.tensor(p8_ct)

            def p8_ctv(vector, tt=tt):
                kb.w(vector, "pe", ev[("p8ct", tt)])
                bank = (pc, pd)[tt % 2]
                _li = vector.tensor_copy(G[0:8, 128 * tt:128 * (tt + 1)],
                                   bank[0:8, 144:272])
                kb.inc(_li, "ve", 1)
                ev[("p8ctv", tt)] = kb.cnt["ve"]
            block.vector(p8_ctv)
        fence()

        def moe_h(ot, c0, ln):
            return C[:, 1024 * ot + c0:1024 * ot + c0 + ln]

        for e in range(10):
            routed = e < 8
            if routed:
                def m8_ohs(sync, e=e):
                    if e >= 2:
                        kb.w(sync, "pe", ev[("m8bc", e - 2, 1)])
                    sem = "s8%d" % (e % 2)
                    kb.inc(sync.dma_start(out=OH8s[:, 128 * (e % 2):128 * (e % 2) + 128],
                                          in_=oh8[:, 128 * e:128 * (e + 1)]),
                           sem, 16)
                    ev[("m8ohs", e)] = (sem, kb.cnt[sem])
                block.sync(m8_ohs)

                for ci in range(2):
                    def m8_bc(tensor, e=e, ci=ci):
                        if ci == 0:
                            kb.w(tensor, ev[("m8ohs", e)][0], ev[("m8ohs", e)][1])
                        if e >= 1:
                            kb.w(tensor, "gp",
                                 ev[("m8hgp", ((e - 1) * 11 + 10) * 2 + 1)])
                            kb.w(tensor, "ve",
                                 ev[("m8da", ((e - 1) * 16 + 15) * 2 + ci)])
                        bank = (pc, pd)[ci % 2]
                        kb.inc(tensor.matmul(bank[:, 0:512],
                                             OH8s[:, 128 * (e % 2):128 * (e % 2) + 128],
                                             G[0:8, 512 * ci:512 * (ci + 1)],
                                             start=True, stop=True), "pe", 1)
                        ev[("m8bc", e, ci)] = kb.cnt["pe"]
                    block.tensor(m8_bc)

                    def m8_bcv(vector, e=e, ci=ci):
                        kb.w(vector, "pe", ev[("m8bc", e, ci)])
                        bank = (pc, pd)[ci % 2]
                        _li = vector.tensor_copy(Fa[:, 512 * ci:512 * (ci + 1)],
                                           bank[:, 0:512])
                        kb.inc(_li, "ve", 1)
                        ev[("m8bcv", e, ci)] = kb.cnt["ve"]
                    block.vector(m8_bcv)
            else:
                def m8_ones(vector, e=e):
                    if e >= 1:
                        kb.w(vector, "gp",
                             ev[("m8hgp", ((e - 1) * 11 + 10) * 2 + 1)])
                    _li = vector.memset(Fa[:, 0:1024], 1.0)
                    kb.inc(_li, "ve", 1)
                    ev[("m8bcv", e, 1)] = kb.cnt["ve"]
                block.vector(m8_ones)

            nOT = 11
            for ot in range(nOT):
                gi = e * nOT + ot

                def m8_w(gp, e=e, ot=ot, gi=gi, routed=routed):
                    if gi >= 2:
                        kb.w(gp, "pe", ev[("m8u", (gi - 2) * 2 + 1)])
                    if ot < 2 and e >= 1:
                        kb.w(gp, "pe", ev[("m8d", ((e - 1) * 16 + 15) * 2 + 1)])
                    if routed:
                        e1 = load_strip(gp, wt_eg[e], ot, eslot(gi),
                                        "wE%d" % (gi % 2))
                        e2 = load_strip(gp, wt_eu[e], ot, eslot2(gi),
                                        "wF%d" % (gi % 2))
                    else:
                        e1 = load_strip(gp, wt_sg, 11 * (e - 8) + ot, eslot(gi),
                                        "wE%d" % (gi % 2))
                        e2 = load_strip(gp, wt_su, 11 * (e - 8) + ot, eslot2(gi),
                                        "wF%d" % (gi % 2))
                    ev[("m8w", gi)] = (e1, e2)
                block.gpsimd(m8_w)

                for ci in range(2):
                    idx = gi * 2 + ci

                    def m8_mm(tensor, e=e, ot=ot, gi=gi, ci=ci, idx=idx):
                        if ci == 0:
                            for sm, v in ev[("m8w", gi)]:
                                kb.w(tensor, sm, v)
                        if idx >= 2:
                            kb.w(tensor, "ac", ev[("m8sl", idx - 2)])
                        for kt in range(16):
                            last = tensor.matmul(
                                (pa, pb)[idx % 2][:, 0:512],
                                wtile(eslot(gi), kt),
                                xb1(kt, 8 + 512 * ci, 512),
                                start=(kt == 0), stop=(kt == 15))
                        kb.inc(last, "pe", 1)
                        ev[("m8g", idx)] = kb.cnt["pe"]
                        if idx >= 2:
                            kb.w(tensor, "gp", ev[("m8hgp", idx - 2)])
                        for kt in range(16):
                            last = tensor.matmul(
                                (pg, ph)[idx % 2][:, 0:512],
                                wtile(eslot2(gi), kt),
                                xb1(kt, 8 + 512 * ci, 512),
                                start=(kt == 0), stop=(kt == 15))
                        kb.inc(last, "pe", 1)
                        ev[("m8u", idx)] = kb.cnt["pe"]
                    block.tensor(m8_mm)

                    def m8_sl(scalar, e=e, ot=ot, ci=ci, idx=idx):
                        kb.w(scalar, "pe", ev[("m8g", idx)])
                        if ot == 0 and e >= 1:
                            kb.w(scalar, "pe",
                                 ev[("m8d", ((e - 1) * 16 + 15) * 2 + 1)])
                        if idx >= 2:
                            kb.w(scalar, "ve", ev[("m8hv", idx - 2)])
                            kb.w(scalar, "gp", ev[("m8hgp", idx - 2)])
                        kb.inc(scalar.activation(moe_h(ot, 512 * ci, 512),
                                                 (pa, pb)[idx % 2][:, 0:512],
                                                 AF.Silu), "ac", 1)
                        ev[("m8sl", idx)] = kb.cnt["ac"]
                    block.scalar(m8_sl)

                    def m8_hv(vector, ot=ot, ci=ci, idx=idx):
                        kb.w(vector, "pe", ev[("m8u", idx)])
                        kb.w(vector, "ac", ev[("m8sl", idx)])
                        hsl = moe_h(ot, 512 * ci, 512)
                        _li = vector.tensor_mul(hsl, hsl, (pg, ph)[idx % 2][:, 0:512])
                        kb.inc(_li, "ve", 1)
                        ev[("m8hv", idx)] = kb.cnt["ve"]
                    block.vector(m8_hv)

                    def m8_hgp(gp, e=e, ot=ot, ci=ci, idx=idx):
                        kb.w(gp, "ve", ev[("m8hv", idx)])
                        kb.w(gp, "ve", ev[("m8bcv", e, 1)])
                        hsl = moe_h(ot, 512 * ci, 512)
                        _li = gp.tensor_mul(hsl, hsl, Fa[:, 512 * ci:512 * (ci + 1)])
                        kb.inc(_li, "gp", 1)
                        ev[("m8hgp", idx)] = kb.cnt["gp"]
                    block.gpsimd(m8_hgp)

            for ot in range(16):
                di = e * 16 + ot

                def m8_dw(gp, e=e, ot=ot, di=di, routed=routed):
                    if di % 16 >= 2:
                        kb.w(gp, "pe", ev[("m8d", (di - 2) * 2 + 1)])
                    else:
                        kb.w(gp, "pe", ev[("m8u", (e * 11 + 10) * 2 + 1)])
                    wd_ap = wt_ed[e] if routed else wt_sd[e - 8]
                    ev[("m8dw", di)] = [load_strip(gp, wd_ap, ot, eslot(22 + di),
                                                   "wE%d" % ((22 + di) % 2),
                                                   kmax=11)]
                block.gpsimd(m8_dw)

                for ci in range(2):
                    idx = di * 2 + ci

                    def m8_dmm(tensor, e=e, ot=ot, di=di, ci=ci, idx=idx):
                        if ci == 0:
                            for sm, v in ev[("m8dw", di)]:
                                kb.w(tensor, sm, v)
                        if di % 16 * 2 + ci >= 2:
                            kb.w(tensor, "ve", ev[("m8da", idx - 2)])
                        elif e >= 1:
                            kb.w(tensor, "ve", ev[("m8da", (e - 1) * 32 + 30 + ci)])
                        bank = (pc, pd)[idx % 2]
                        for kt in range(11):
                            last = tensor.matmul(bank[:, 0:512],
                                                 wtile(eslot(22 + di), kt),
                                                 moe_h(kt, 512 * ci, 512),
                                                 start=(kt == 0), stop=(kt == 10))
                        kb.inc(last, "pe", 1)
                        ev[("m8d", idx)] = kb.cnt["pe"]
                    block.tensor(m8_dmm)

                    def m8_da(vector, ot=ot, ci=ci, idx=idx):
                        kb.w(vector, "pe", ev[("m8d", idx)])
                        bank = (pc, pd)[idx % 2]
                        xd_ = x_ext(ot, 8 + 512 * ci, 512)
                        _li = vector.tensor_add(xd_, xd_, bank[:, 0:512])
                        kb.inc(_li, "ve", 1)
                        ev[("m8da", idx)] = kb.cnt["ve"]
                    block.vector(m8_da)
        fence()

        if DBG:
            def pdbg(sync):
                for s in ("pe", "ve", "ac", "gp"):
                    kb.w(sync, s)
                kb.inc(sync.dma_start(out=dbg[:, 0:NFT * TE],
                                      in_=A[:, 0:NFT * TE]), "st0", 16)
                kb.inc(sync.dma_start(out=dbg[:, NFT * TE:NFT * TE + NFT * TW // 2],
                                      in_=B[:, :].bitcast(F32)), "st0", 16)
                kb.inc(sync.dma_start(out=dbg[:, 27008:27008 + 2 * TW],
                                      in_=R1[:, :]), "st0", 16)
                kb.inc(sync.dma_start(out=dbg[:, 29700:29700 + 1024],
                                      in_=Fa[:, :]), "st0", 16)
                kb.w(sync, "st0")
            block.sync(pdbg)

        fence()
        for tt in range(8):
            for fg in range(4):
                def pout_t(tensor, tt=tt, fg=fg):
                    bank = (pc, pd)[(tt * 4 + fg) % 2]
                    if tt * 4 + fg >= 2:
                        kb.w(tensor, "ve", ev[("povg", tt * 4 + fg - 2)])
                    last = None
                    for f2 in range(4):
                        ft = 4 * fg + f2
                        last = tensor.transpose(
                            bank[0:128, 128 * f2:128 * (f2 + 1)],
                            x_ext(ft, HALO + 128 * tt, 128), c_identf())
                    kb.inc(last, "pe", 1)
                    ev[("pot", tt * 4 + fg)] = kb.cnt["pe"]
                block.tensor(pout_t)

                def pout_v(vector, tt=tt, fg=fg):
                    if fg == 0 and tt >= 2:
                        kb.w(vector, ev[("pos", tt - 2)][0], ev[("pos", tt - 2)][1])
                    kb.w(vector, "pe", ev[("pot", tt * 4 + fg)])
                    bank = (pc, pd)[(tt * 4 + fg) % 2]
                    for f2 in range(4):
                        ft = 4 * fg + f2
                        _li = vector.tensor_copy(
                            Dn32[:, 2048 * (tt % 2) + 128 * ft:
                                 2048 * (tt % 2) + 128 * (ft + 1)],
                            bank[0:128, 128 * f2:128 * (f2 + 1)])
                    kb.inc(_li, "ve", 1)
                    ev[("povg", tt * 4 + fg)] = kb.cnt["ve"]
                    if fg == 3:
                        ev[("pov", tt)] = kb.cnt["ve"]
                block.vector(pout_v)

            def pout_s(sync, tt=tt):
                kb.w(sync, "ve", ev[("pov", tt)])
                sem = "st%d" % (tt % 2)
                kb.inc(sync.dma_start(out=out[128 * tt:128 * (tt + 1), :],
                                      in_=Dn32[:, 2048 * (tt % 2):2048 * (tt % 2 + 1)]),
                       sem, 16)
                ev[("pos", tt)] = (sem, kb.cnt[sem])
                if tt == 7:
                    kb.w(sync, "st0")
                    kb.w(sync, "st1")
            block.sync(pout_s)

    kb.close()
    return kb


def kernel(**inputs):
    kb = build_program()
    shared = host_prep(inputs)
    in_maps = []
    for core in range(NCORES):
        m = dict(shared)
        m.update(host_percore(inputs, core))
        in_maps.append(m)
    res = run_bass_kernel_spmd(kb.nc, in_maps, core_ids=list(range(NCORES)))
    outs = [res.results[c]["out"] for c in range(NCORES)]
    full = np.concatenate(outs, axis=0).reshape(1, T_SEQ, D)
    if DBG:
        kernel._dbg = [res.results[c].get("dbg") for c in range(NCORES)]
    return full
